# version 04
# baseline (speedup 1.0000x reference)
import numpy as np
import ml_dtypes
from contextlib import ExitStack

import concourse.bass as bass
import concourse.mybir as mybir
from concourse.bass_utils import run_bass_kernel_spmd

F32 = mybir.dt.float32
BF16 = mybir.dt.bfloat16
I32 = mybir.dt.int32
AF = mybir.ActivationFunctionType
ALU = mybir.AluOpType
AX = mybir.AxisListType

NB, NTOK, E = 8, 4096, 1024
H, D = 16, 64
C = 128
NCH = NTOK // C
KT = E // 128
NEXP, DH = 8, 2048
KT2 = DH // 128
EPS = 1e-5
RBASE = 10000.0
NGRP = 2
CPG = NCH // NGRP
TPG = NTOK // NGRP
BIG = 1.0e30

ENGS = ("sync", "gpsimd", "vector", "scalar", "tensor")


class Sched:

    def __init__(self, nc, ctx, tag):
        self.nc = nc
        self.ctx = ctx
        self.tag = tag
        self.sem = {}
        self.cnt = {}
        for e in ENGS:
            self.sem[e] = ctx.enter_context(nc.semaphore(f"{tag}_{e}"))
            self.cnt[e] = 0
        self.recs = {e: [] for e in ENGS}
        self.pend_bar = {e: False for e in ENGS}
        self.bar_snapshot = None

    def _dma_sem(self, name):
        if name not in self.sem:
            self.sem[name] = self.ctx.enter_context(
                self.nc.semaphore(f"{self.tag}_{name}"))
            self.cnt[name] = 0
        return name

    SELF_CHAIN = ("vector", "scalar", "gpsimd")

    def emit(self, eng, fn, after=(), dma=None):
        waits = []
        if eng in self.SELF_CHAIN and self.cnt[eng] > 0:
            waits.append((eng, self.cnt[eng]))
        if self.pend_bar[eng] and self.bar_snapshot:
            for s, v in self.bar_snapshot.items():
                if s != eng and v > 0:
                    waits.append((s, v))
            self.pend_bar[eng] = False
        for s, v in after:
            if s == eng:
                continue
            waits.append((s, v))
        if dma is not None:
            semname = self._dma_sem(dma)
            inc = 16
        else:
            semname = eng
            inc = 1
        self.cnt[semname] += inc
        tok = (semname, self.cnt[semname])
        self.recs[eng].append((waits, fn, semname, inc))
        return tok

    def barrier(self):
        self.bar_snapshot = {e: self.cnt[e] for e in ENGS}
        for e in ENGS:
            self.pend_bar[e] = True

    def flush(self, block, final_tokens=()):
        recs = self.recs
        sems = self.sem

        def make_body(ename, extra_final):
            def body(eng):
                waited = {}
                for waits, fn, semname, inc in recs[ename]:
                    for s, v in waits:
                        if waited.get(s, 0) >= v:
                            continue
                        eng.wait_ge(sems[s], v)
                        waited[s] = v
                    inst = fn(eng)
                    inst.then_inc(sems[semname], inc)
                for s, v in extra_final:
                    if waited.get(s, 0) >= v:
                        continue
                    eng.wait_ge(sems[s], v)
                    waited[s] = v
            return body

        block.sync(make_body("sync", list(final_tokens)))
        block.gpsimd(make_body("gpsimd", []))
        block.vector(make_body("vector", []))
        block.scalar(make_body("scalar", []))
        block.tensor(make_body("tensor", []))


def _host_tables():
    d2 = np.arange(D // 2, dtype=np.float64)
    inv = RBASE ** (-(d2 * 2.0) / D)
    pos = np.arange(NTOK, dtype=np.float64)
    ang = pos[:, None] * inv[None, :]
    cos = np.cos(ang).astype(np.float32)
    sin = np.sin(ang).astype(np.float32)
    cos_rep = np.ascontiguousarray(np.tile(cos, (1, H)))
    sin_rep = np.ascontiguousarray(np.tile(sin, (1, H)))
    triu = np.triu(np.ones((C, C), dtype=np.float32))
    ident = np.eye(C, dtype=np.float32)
    return cos_rep, sin_rep, triu, ident


def build(debug=False, n_chunks=NCH, phases="ABC", stage=999):
    nc = bass.Bass("TRN2", target_bir_lowering=False)
    dt = nc.dram_tensor
    x = dt("x", [NTOK, E], F32, kind="ExternalInput")
    Wq = dt("Wq", [E, E], F32, kind="ExternalInput")
    Wk = dt("Wk", [E, E], F32, kind="ExternalInput")
    Wv = dt("Wv", [E, E], F32, kind="ExternalInput")
    Wo = dt("Wo", [E, E], F32, kind="ExternalInput")
    Wg1 = dt("Wg1", [E, D], F32, kind="ExternalInput")
    Wg2 = dt("Wg2", [D, E], F32, kind="ExternalInput")
    gamma_fm = dt("gamma_fm", [128, KT], F32, kind="ExternalInput")
    beta_fm = dt("beta_fm", [128, KT], F32, kind="ExternalInput")
    rw1 = dt("rw1", [E, E], F32, kind="ExternalInput")
    rb1_fm = dt("rb1_fm", [128, KT], F32, kind="ExternalInput")
    rw2 = dt("rw2", [E, NEXP], F32, kind="ExternalInput")
    rb2_b = dt("rb2_b", [128, NEXP], F32, kind="ExternalInput")
    ew1 = dt("ew1", [NEXP, E, DH], F32, kind="ExternalInput")
    eb1_fm = dt("eb1_fm", [128, NEXP, KT2], F32, kind="ExternalInput")
    ew2 = dt("ew2", [NEXP, DH, E], F32, kind="ExternalInput")
    eb2 = dt("eb2", [NEXP, E], F32, kind="ExternalInput")
    cosr = dt("cosr", [NTOK, H * 32], F32, kind="ExternalInput")
    sinr = dt("sinr", [NTOK, H * 32], F32, kind="ExternalInput")
    triu_in = dt("triu", [C, C], F32, kind="ExternalInput")
    ident_in = dt("ident", [C, C], F32, kind="ExternalInput")
    out = dt("out", [NTOK, E], F32, kind="ExternalOutput")
    skind = "ExternalOutput" if debug else "Internal"
    x1s = dt("x1s", [NTOK, E], F32, kind=skind)
    ts_ = dt("ts", [NTOK, E], BF16, kind=skind)
    dbg_w = dt("dbg_w", [NCH, 128, NEXP], F32, kind=skind) if debug else None
    dbg_y = dt("dbg_y", [NTOK, E], F32, kind=skind) if debug else None
    dbg_eh = dt("dbg_eh", [128, KT2, 512], BF16, kind=skind) if debug else None
    dbg_yac = dt("dbg_yac", [128, CPG, E], BF16, kind=skind) if debug else None

    with ExitStack() as octx:
        oe = octx.enter_context
        tT = oe(nc.sbuf_tensor("tT", [128, KT, NTOK], BF16))
        wsb = oe(nc.sbuf_tensor("wsb", [128, NCH, NEXP], F32))
        wTsb = oe(nc.sbuf_tensor("wTsb", [NEXP, NTOK], BF16))
        ones_row = oe(nc.sbuf_tensor("ones_row", [1, 128], BF16))
        bWo_row = oe(nc.sbuf_tensor("bWo_row", [1, E], BF16))
        identbf = oe(nc.sbuf_tensor("identbf", [C, C], BF16))
        trium = oe(nc.sbuf_tensor("trium", [C, C], F32))

        with ExitStack() as actx:
            ae = actx.enter_context
            Wq_bf = ae(nc.sbuf_tensor("Wq_bf", [128, KT, E], BF16))
            Wk_bf = ae(nc.sbuf_tensor("Wk_bf", [128, KT, E], BF16))
            Wv_bf = ae(nc.sbuf_tensor("Wv_bf", [128, KT, E], BF16))
            Wo_bf = ae(nc.sbuf_tensor("Wo_bf", [128, KT, E], BF16))
            Wg1_bf = ae(nc.sbuf_tensor("Wg1_bf", [128, KT, D], BF16))
            Wg2_bf = ae(nc.sbuf_tensor("Wg2_bf", [D, E], BF16))
            gamma_sb = ae(nc.sbuf_tensor("gamma_sb", [128, KT], F32))
            beta_sb = ae(nc.sbuf_tensor("beta_sb", [128, KT], F32))
            beta_bf = ae(nc.sbuf_tensor("beta_bf", [128, KT], BF16))
            cosb = ae(nc.sbuf_tensor("cosb", [128, 512], F32))
            sinb = ae(nc.sbuf_tensor("sinb", [128, 512], F32))
            xb = ae(nc.sbuf_tensor("xb", [128, 2, E], F32))
            ln1b = ae(nc.sbuf_tensor("ln1b", [128, E], BF16))
            ln1T = ae(nc.sbuf_tensor("ln1T", [128, KT, C], BF16))
            qs = ae(nc.sbuf_tensor("qs", [128, E], F32))
            ks = ae(nc.sbuf_tensor("ks", [128, E], F32))
            rt2 = ae(nc.sbuf_tensor("rt2", [128, 512], F32))
            kt2 = ae(nc.sbuf_tensor("kt2", [128, 512], F32))
            qro = ae(nc.sbuf_tensor("qro", [128, E], BF16))
            kro = ae(nc.sbuf_tensor("kro", [128, E], BF16))
            vb = ae(nc.sbuf_tensor("vb", [128, E], BF16))
            qT = ae(nc.sbuf_tensor("qT", [128, KT, C], BF16))
            kTt = ae(nc.sbuf_tensor("kTt", [128, KT, C], BF16))
            g1b = ae(nc.sbuf_tensor("g1b", [D, C], BF16))
            th = ae(nc.sbuf_tensor("th", [128, E], BF16))
            gateb = ae(nc.sbuf_tensor("gateb", [128, E], BF16))
            eT = ae(nc.sbuf_tensor("eT", [128, 2, C], BF16))
            Sf = ae(nc.sbuf_tensor("Sf", [128, 512], F32))
            Sbf = ae(nc.sbuf_tensor("Sbf", [128, 512], BF16))
            sb_o = ae(nc.sbuf_tensor("sb_o", [128, E], BF16))
            x1b = ae(nc.sbuf_tensor("x1b", [128, E], F32))
            dump = ae(nc.sbuf_tensor("dump", [128, E], BF16))
            og = ae(nc.sbuf_tensor("og", [128, 2, E], F32))
            scol = ae(nc.sbuf_tensor("scol", [128, 32], F32))
            scol2 = ae(nc.sbuf_tensor("scol2", [128, 32], F32))
            pbig0 = ae(nc.psum_tensor("pbig0", [128, 1024], F32))
            pbig1 = ae(nc.psum_tensor("pbig1", [128, 1024], F32))
            ptr = ae(nc.psum_tensor("ptr", [128, KT, C], BF16))
            pS = ae(nc.psum_tensor("pS", [128, 512], F32))
            pm6 = ae(nc.psum_tensor("pm6", [128, 512], F32))
            pm7 = ae(nc.psum_tensor("pm7", [128, 512], F32))

            s = Sched(nc, actx, "a")
            T, V, S, GP, SY = "tensor", "vector", "scalar", "gpsimd", "sync"

            t_wq = s.emit(GP, lambda e: e.dma_start(
                Wq_bf[:, :, :], Wq[:, :].rearrange("(k p) n -> p k n", p=128)), dma="dw")
            t_wk = s.emit(GP, lambda e: e.dma_start(
                Wk_bf[:, :, :], Wk[:, :].rearrange("(k p) n -> p k n", p=128)), dma="dw")
            t_wv = s.emit(GP, lambda e: e.dma_start(
                Wv_bf[:, :, :], Wv[:, :].rearrange("(k p) n -> p k n", p=128)), dma="dw")
            t_wo = s.emit(GP, lambda e: e.dma_start(
                Wo_bf[:, :, :], Wo[:, :].rearrange("(k p) n -> p k n", p=128)), dma="dw")
            t_wg1 = s.emit(GP, lambda e: e.dma_start(
                Wg1_bf[:, :, :], Wg1[:, :].rearrange("(k p) n -> p k n", p=128)), dma="dw")
            t_wg2 = s.emit(GP, lambda e: e.dma_start(Wg2_bf[:, :], Wg2[:, :]), dma="dw")
            t_idb = s.emit(GP, lambda e: e.dma_start(identbf[:, :], ident_in[:, :]), dma="dw")
            tok_wall = ("dw", s.cnt["dw"])
            t_wq = t_wk = t_wv = t_wo = t_wg1 = t_wg2 = t_idb = tok_wall
            t_gam = s.emit(SY, lambda e: e.dma_start(gamma_sb[:, :], gamma_fm[:, :]), dma="dmisc")
            t_bet = s.emit(SY, lambda e: e.dma_start(beta_sb[:, :], beta_fm[:, :]), dma="dmisc")
            t_tri = s.emit(SY, lambda e: e.dma_start(trium[:, :], triu_in[:, :]), dma="dmisc")
            tok_misc = ("dmisc", s.cnt["dmisc"])
            t_gam = t_bet = t_tri = tok_misc
            t_bbf = s.emit(V, lambda e: e.tensor_copy(beta_bf[:, :], beta_sb[:, :]),
                           after=[t_bet])
            t_ones = s.emit(V, lambda e: e.memset(ones_row[:, :], 1.0))
            t_sf0 = s.emit(V, lambda e: e.memset(Sf[:, :], 0.0))
            t_sbf0 = s.emit(V, lambda e: e.memset(Sbf[:, :], 0.0), after=[t_sf0])
            t_bwo_mm = None
            t_bwo_ev = None
            for half in range(2):
                for k in range(KT):
                    def f(e, half=half, k=k):
                        return e.matmul(pm7[0:1, 0:512],
                                        beta_bf[:, k:k + 1],
                                        Wo_bf[:, k, half * 512:(half + 1) * 512],
                                        start=(k == 0), stop=(k == KT - 1))
                    aft = []
                    if k == 0:
                        aft = [t_wo, t_bbf]
                        if t_bwo_ev is not None:
                            aft.append(t_bwo_ev)
                    t_bwo_mm = s.emit(T, f, after=aft)
                t_bwo_ev = s.emit(S, lambda e, half=half: e.copy(
                    bWo_row[0:1, half * 512:(half + 1) * 512], pm7[0:1, 0:512]),
                    after=[t_bwo_mm])
            t_wog = []
            for k in range(KT):
                t_wog.append(s.emit(V, lambda e, k=k: e.tensor_scalar(
                    Wo_bf[:, k, :], Wo_bf[:, k, :], gamma_sb[:, k:k + 1], None,
                    ALU.mult), after=[t_bwo_ev, t_gam, t_wo]))

            tok_x = [None] * (NCH + 2)
            tok_x[0] = s.emit(SY, lambda e: e.dma_start(xb[:, 0, :], x[0:C, :]), dma="dx0")
            if n_chunks > 1:
                tok_x[1] = s.emit(SY, lambda e: e.dma_start(
                    xb[:, 1, :], x[C:2 * C, :]), dma="dx1")
            t_cos = s.emit(SY, lambda e: e.dma_start(cosb[:, :], cosr[0:C, :]), dma="dtab")
            t_sin = s.emit(SY, lambda e: e.dma_start(sinb[:, :], sinr[0:C, :]), dma="dtab")
            t_cos = t_sin = ("dtab", s.cnt["dtab"])

            prev = {
                "x1_store": None,
                "t_store": None,
                "ptr_free": t_idb,
                "pbig0_free": t_bwo_ev,
                "pbig1_free": t_bwo_ev,
                "Sbf_ready": t_sbf0,
                "Sf_ready": t_sf0,
                "pS_free": t_sf0,
                "rope_done_v": t_ones,
                "rope_done_g": t_idb,
                "x_free": [t_bbf],
                "tT_slot": None,
            }

            MAGIC = float(0x5F3759DF)

            def emit_rsqrt(colbuf, c_v, c_out, after):
                ci = colbuf[:, :].bitcast(I32)
                t = s.emit(V, lambda e: e.tensor_scalar(
                    ci[:, 20:21], ci[:, c_v:c_v + 1], 1, None,
                    ALU.logical_shift_right), after=after)
                s.emit(V, lambda e: e.tensor_copy(colbuf[:, 21:22], ci[:, 20:21]))
                s.emit(V, lambda e: e.tensor_scalar(
                    colbuf[:, 22:23], colbuf[:, 21:22], -1.0, MAGIC,
                    ALU.mult, ALU.add))
                s.emit(V, lambda e: e.tensor_copy(ci[:, 23:24], colbuf[:, 22:23]))
                s.emit(V, lambda e: e.tensor_copy(
                    colbuf[:, c_out:c_out + 1], ci[:, 23:24].bitcast(F32)[:, 0:1]))
                for _ in range(3):
                    s.emit(V, lambda e: e.tensor_tensor(
                        colbuf[:, 20:21], colbuf[:, c_out:c_out + 1],
                        colbuf[:, c_out:c_out + 1], ALU.mult))
                    s.emit(V, lambda e: e.tensor_tensor(
                        colbuf[:, 21:22], colbuf[:, c_v:c_v + 1],
                        colbuf[:, 20:21], ALU.mult))
                    s.emit(V, lambda e: e.tensor_scalar(
                        colbuf[:, 22:23], colbuf[:, 21:22], -0.5, 1.5,
                        ALU.mult, ALU.add))
                    t = s.emit(V, lambda e: e.tensor_tensor(
                        colbuf[:, c_out:c_out + 1], colbuf[:, c_out:c_out + 1],
                        colbuf[:, 22:23], ALU.mult))
                return t

            def emit_ln_stats(colbuf, src_ap, after_tok, psum_src=False,
                              sq_eng="S"):
                c_sx, c_sq, c_m, c_a, c_msq, c_b, c_rs = 0, 1, 2, 3, 4, 5, 6
                t_sx = s.emit(V, lambda e: e.reduce_sum(
                    colbuf[:, c_sx:c_sx + 1], src_ap, axis=AX.X), after=after_tok)
                if sq_eng == "S":
                    t_sq = s.emit(S, lambda e: e.activation(
                        dump[:, :], src_ap, AF.Square,
                        accum_out=colbuf[:, c_sq:c_sq + 1]),
                        after=after_tok + ([t_sx] if psum_src else []))
                else:
                    t_sq = s.emit(V, lambda e: e.scalar_tensor_tensor(
                        dump[:, :], src_ap, 1.0, src_ap, ALU.mult, ALU.mult,
                        accum_out=colbuf[:, c_sq:c_sq + 1]), after=after_tok)
                t_m = s.emit(V, lambda e: e.tensor_scalar(
                    colbuf[:, c_m:c_m + 1], colbuf[:, c_sx:c_sx + 1], 1.0 / E,
                    None, ALU.mult), after=[t_sx])
                t_a = s.emit(V, lambda e: e.tensor_scalar(
                    colbuf[:, c_a:c_a + 1], colbuf[:, c_sq:c_sq + 1], 1.0 / E,
                    EPS, ALU.mult, ALU.add), after=[t_sq])
                t_msq = s.emit(V, lambda e: e.tensor_tensor(
                    colbuf[:, c_msq:c_msq + 1], colbuf[:, c_m:c_m + 1],
                    colbuf[:, c_m:c_m + 1], ALU.mult), after=[t_m])
                t_b = s.emit(V, lambda e: e.tensor_tensor(
                    colbuf[:, c_b:c_b + 1], colbuf[:, c_a:c_a + 1],
                    colbuf[:, c_msq:c_msq + 1], ALU.subtract),
                    after=[t_a, t_msq])
                t_rs = emit_rsqrt(colbuf, c_b, c_rs, [t_b])
                return ([t_rs], colbuf[:, c_rs:c_rs + 1],
                        colbuf[:, c_m:c_m + 1])

            def emit_nmrs(colbuf, toks):
                c_m, c_rs, c_mrs, c_nmrs = 2, 6, 7, 8
                s.emit(V, lambda e: e.tensor_tensor(
                    colbuf[:, c_mrs:c_mrs + 1], colbuf[:, c_m:c_m + 1],
                    colbuf[:, c_rs:c_rs + 1], ALU.mult), after=toks)
                t = s.emit(V, lambda e: e.tensor_scalar(
                    colbuf[:, c_nmrs:c_nmrs + 1], colbuf[:, c_mrs:c_mrs + 1],
                    -1.0, None, ALU.mult))
                return ([t], colbuf[:, c_rs:c_rs + 1],
                        colbuf[:, c_nmrs:c_nmrs + 1])

            cos_v = cosb[:, :].rearrange("p (h d) -> p h d", h=H)
            sin_v = sinb[:, :].rearrange("p (h d) -> p h d", h=H)

            def emit_rope(eng, src, scr2, dst, after_toks):
                sv = src[:, :].rearrange("p (h d) -> p h d", h=H)
                dv = dst[:, :].rearrange("p (h d) -> p h d", h=H)
                s2 = scr2[:, :].rearrange("p (h d) -> p h d", h=H)
                x1v, x2v = sv[:, :, 0:32], sv[:, :, 32:64]
                d1, d2 = dv[:, :, 0:32], dv[:, :, 32:64]
                s.emit(eng, lambda e: e.tensor_tensor(
                    s2[:, :, :], x2v, sin_v, ALU.mult), after=after_toks)
                s.emit(eng, lambda e: e.tensor_tensor(
                    d1, x1v, cos_v, ALU.mult))
                s.emit(eng, lambda e: e.tensor_tensor(
                    d1, d1, s2[:, :, :], ALU.subtract))
                s.emit(eng, lambda e: e.tensor_tensor(
                    s2[:, :, :], x2v, cos_v, ALU.mult))
                s.emit(eng, lambda e: e.tensor_tensor(
                    d2, x1v, sin_v, ALU.mult))
                t6 = s.emit(eng, lambda e: e.tensor_tensor(
                    d2, d2, s2[:, :, :], ALU.add))
                return t6

            prev.update({
                "scol_rd": None, "scol2_rd": None, "ln1b_rd": None,
                "ln1T_rd": None, "qs_rd": None, "ks_rd": None,
                "qro_rd": None, "kro_rd": [], "vb_rd": [],
                "qT_rd": [], "kT_rd": [], "th_rd": None,
                "sb_o_rd": None, "outT_rd": None, "og_rd2": [None, None],
                "t_mask": [None, None],
            })

            def emit_front(c):
                f = {}
                sl = c % 2
                if c > 0:
                    s.emit(SY, lambda e, c=c: e.dma_start(
                        cosb[:, :], cosr[c * C:(c + 1) * C, :]),
                        after=[prev["rope_done_v"], prev["rope_done_g"]],
                        dma="dtab")
                    s.emit(SY, lambda e, c=c: e.dma_start(
                        sinb[:, :], sinr[c * C:(c + 1) * C, :]),
                        after=[prev["rope_done_v"], prev["rope_done_g"]],
                        dma="dtab")
                    f["tab"] = ("dtab", s.cnt["dtab"])
                else:
                    f["tab"] = t_cos
                xc = xb[:, sl, :]
                f["xc"] = xc
                f["c"] = c
                ln_toks, rs_ap, m_ap = emit_ln_stats(
                    scol, xc, [tok_x[c]] +
                    ([prev["scol_rd"]] if prev["scol_rd"] else []))
                ln_toks, rs_ap, nmrs_ap = emit_nmrs(scol, ln_toks)
                t_ln1 = s.emit(S, lambda e, rs_ap=rs_ap, nmrs_ap=nmrs_ap, xc=xc:
                               e.activation(ln1b[:, :], xc, AF.Identity,
                                            bias=nmrs_ap, scale=rs_ap),
                               after=ln_toks +
                               ([prev["ln1b_rd"]] if prev["ln1b_rd"] else []))
                prev["scol_rd"] = t_ln1
                t_tr = None
                for j in range(KT):
                    t_tr = s.emit(T, lambda e, j=j: e.transpose(
                        ptr[:, j, :], ln1b[:, j * 128:(j + 1) * 128],
                        identbf[:, :]),
                        after=[t_ln1, prev["ptr_free"], t_idb] if j == 0 else [])
                prev["ln1b_rd"] = t_tr
                t_ln1T = s.emit(S, lambda e: e.copy(
                    ln1T[:, :, :], ptr[:, :, :]),
                    after=[t_tr] +
                    ([prev["ln1T_rd"]] if prev["ln1T_rd"] else []))
                prev["ptr_free"] = t_ln1T
                t_qmm = None
                for nh in range(2):
                    for k in range(KT):
                        def fq(e, nh=nh, k=k):
                            return e.matmul(pbig0[:, nh * 512:(nh + 1) * 512],
                                            ln1T[:, k, :],
                                            Wq_bf[:, k, nh * 512:(nh + 1) * 512],
                                            start=(k == 0), stop=(k == KT - 1))
                        t_qmm = s.emit(T, fq,
                                       after=[t_ln1T, prev["pbig0_free"], t_wq]
                                       if (nh == 0 and k == 0) else [])
                t_kmm = None
                for nh in range(2):
                    for k in range(KT):
                        def fk(e, nh=nh, k=k):
                            return e.matmul(pbig1[:, nh * 512:(nh + 1) * 512],
                                            ln1T[:, k, :],
                                            Wk_bf[:, k, nh * 512:(nh + 1) * 512],
                                            start=(k == 0), stop=(k == KT - 1))
                        t_kmm = s.emit(T, fk,
                                       after=[t_ln1T, prev["pbig1_free"], t_wk]
                                       if (nh == 0 and k == 0) else [])
                t_sq_ = s.emit(S, lambda e: e.activation(qs[:, :], pbig0[:, :],
                                                         AF.Silu),
                               after=[t_qmm] +
                               ([prev["qs_rd"]] if prev["qs_rd"] else []))
                t_sk_ = s.emit(S, lambda e: e.activation(ks[:, :], pbig1[:, :],
                                                         AF.Silu),
                               after=[t_kmm] +
                               ([prev["ks_rd"]] if prev["ks_rd"] else []))
                t_vmm = None
                for nh in range(2):
                    for k in range(KT):
                        def fv(e, nh=nh, k=k):
                            return e.matmul(pbig0[:, nh * 512:(nh + 1) * 512],
                                            ln1T[:, k, :],
                                            Wv_bf[:, k, nh * 512:(nh + 1) * 512],
                                            start=(k == 0), stop=(k == KT - 1))
                        t_vmm = s.emit(T, fv, after=[t_sq_, t_wv]
                                       if (nh == 0 and k == 0) else [])
                t_g1 = None
                for k in range(KT):
                    def fg(e, k=k):
                        return e.matmul(pm6[0:64, 0:128], Wg1_bf[:, k, :],
                                        ln1T[:, k, :],
                                        start=(k == 0), stop=(k == KT - 1))
                    aft = []
                    if k == 0:
                        aft = [t_ln1T, t_wg1]
                        if prev["t_mask"][0] is not None:
                            aft.append(prev["t_mask"][0])
                    t_g1 = s.emit(T, fg, after=aft)
                prev["ln1T_rd"] = t_g1
                t_g1e = s.emit(S, lambda e: e.copy(g1b[:, :], pm6[0:64, 0:128]),
                               after=[t_g1])
                t_g2 = None
                for nh in range(2):
                    def fg2(e, nh=nh):
                        return e.matmul(pbig1[:, nh * 512:(nh + 1) * 512],
                                        g1b[:, :],
                                        Wg2_bf[:, nh * 512:(nh + 1) * 512],
                                        start=True, stop=True)
                    t_g2 = s.emit(T, fg2, after=[t_sk_, t_g1e, t_wg2]
                                  if nh == 0 else [])
                t_th = s.emit(S, lambda e: e.activation(th[:, :], pbig1[:, :],
                                                        AF.Tanh, scale=0.5),
                              after=[t_g2] +
                              ([prev["th_rd"]] if prev["th_rd"] else []))
                t_vb = s.emit(V, lambda e: e.tensor_copy(vb[:, :], pbig0[:, :]),
                              after=[t_vmm] + prev["vb_rd"])
                f["t_vb"] = t_vb
                t_ropeq = emit_rope(GP, qs, rt2, qro,
                                    [f["tab"], t_sq_] +
                                    ([prev["qro_rd"]] if prev["qro_rd"] else []))
                t_ropek = emit_rope(GP, ks, kt2, kro,
                                    [f["tab"], t_sk_] + prev["kro_rd"])
                prev["rope_done_v"] = t_ropeq
                prev["rope_done_g"] = t_ropek
                prev["qs_rd"] = t_ropeq
                prev["ks_rd"] = t_ropek
                t_gate = s.emit(V, lambda e: e.tensor_scalar(
                    gateb[:, :], th[:, :], 0.5, 0.5, ALU.mult, ALU.add),
                    after=[t_th])
                prev["th_rd"] = t_gate
                t_trq = None
                for j in range(KT):
                    t_trq = s.emit(T, lambda e, j=j: e.transpose(
                        ptr[:, j, :], qro[:, j * 128:(j + 1) * 128], identbf[:, :]),
                        after=[t_ropeq, prev["ptr_free"]] if j == 0 else [])
                prev["qro_rd"] = t_trq
                t_qT = s.emit(S, lambda e: e.copy(qT[:, :, :], ptr[:, :, :]),
                              after=[t_trq] + prev["qT_rd"])
                prev["ptr_free"] = t_qT
                t_trk = None
                for j in range(KT):
                    t_trk = s.emit(T, lambda e, j=j: e.transpose(
                        ptr[:, j, :], kro[:, j * 128:(j + 1) * 128], identbf[:, :]),
                        after=[t_ropek, prev["ptr_free"]] if j == 0 else [])
                t_kT = s.emit(S, lambda e: e.copy(kTt[:, :, :], ptr[:, :, :]),
                              after=[t_trk] + prev["kT_rd"])
                prev["ptr_free"] = t_kT
                t_mask = prev["t_mask"]
                t_intra_last = None
                t_inter_last = None
                t_state_last = None
                for h in range(H):
                    po = 64 * (h % 2)
                    tile = h // 2
                    slot = h % 2
                    pe_slot = (pm6, pm7)[slot]
                    t_en = s.emit(T, lambda e, po=po, tile=tile, pe_slot=pe_slot:
                                  e.matmul(pe_slot[:, 128:256],
                                           kTt[po:po + 64, tile, :],
                                           qT[po:po + 64, tile, :],
                                           start=True, stop=True),
                                  after=([t_qT, t_kT, t_th] if h == 0 else []) +
                                        ([t_mask[slot]] if t_mask[slot] else []))
                    t_mask[slot] = s.emit(V, lambda e, slot=slot, pe_slot=pe_slot:
                                          e.tensor_tensor(
                                              eT[:, slot, :],
                                              pe_slot[:, 128:256],
                                              trium[:, :], ALU.mult),
                                          after=[t_en, t_tri])
                    t_intra_last = s.emit(T, lambda e, h=h, slot=slot: e.matmul(
                        pbig1[:, h * 64:(h + 1) * 64], eT[:, slot, :],
                        vb[:, h * 64:(h + 1) * 64], start=True, stop=False),
                        after=[t_mask[slot], t_vb] + ([t_th] if h == 0 else []))
                    t_inter_last = s.emit(T, lambda e, h=h, po=po, tile=tile:
                                          e.matmul(
                        pbig1[:, h * 64:(h + 1) * 64],
                        qT[po:po + 64, tile, :],
                        Sbf[po:po + 64, (h // 2) * 64:(h // 2) * 64 + 64],
                        start=False, stop=True),
                        after=[prev["Sbf_ready"]] if h == 0 else [])
                    t_state_last = s.emit(T, lambda e, h=h, po=po: e.matmul(
                        pS[po:po + 64, (h // 2) * 64:(h // 2) * 64 + 64],
                        kro[:, h * 64:(h + 1) * 64],
                        vb[:, h * 64:(h + 1) * 64], start=True, stop=True),
                        after=[t_ropek, t_vb, prev["pS_free"]] if h == 0 else [])
                prev["kro_rd"] = [t_trk, t_state_last]
                prev["vb_rd"] = [t_intra_last, t_state_last]
                prev["kT_rd"] = [t_inter_last]
                t_sfu = s.emit(V, lambda e: e.tensor_tensor(
                    Sf[:, :], Sf[:, :], pS[:, :], ALU.add),
                    after=[t_state_last, prev["Sf_ready"]])
                prev["pS_free"] = t_sfu
                prev["Sf_ready"] = t_sfu
                t_sbf = s.emit(V, lambda e: e.tensor_copy(Sbf[:, :], Sf[:, :]),
                               after=[t_sfu, t_inter_last])
                prev["Sbf_ready"] = t_sbf
                t_og = s.emit(V, lambda e, sl=sl: e.tensor_tensor(
                    og[:, sl, :], pbig1[:, :], gateb[:, :], ALU.mult),
                    after=[t_intra_last, t_inter_last, t_gate] +
                          ([prev["og_rd2"][sl]] if prev["og_rd2"][sl] else []))
                prev["pbig1_free"] = t_og
                f["t_og"] = t_og
                f["t_inter_last"] = t_inter_last
                f["t_vb"] = t_vb
                f["t_th"] = t_th
                return f

            def emit_back(c, f):
                xc = f["xc"]
                sl = c % 2
                ogs = og[:, sl, :]
                ln_toks, rs_ap, m_ap = emit_ln_stats(
                    scol2, ogs, [f["t_og"]] +
                    ([prev["scol2_rd"]] if prev["scol2_rd"] else []),
                    sq_eng="V")
                t_oln = s.emit(V, lambda e, rs_ap=rs_ap, m_ap=m_ap, ogs=ogs:
                               e.tensor_scalar(sb_o[:, :], ogs, m_ap, rs_ap,
                                               ALU.subtract, ALU.mult),
                               after=ln_toks +
                               ([prev["sb_o_rd"]] if prev["sb_o_rd"] else []) +
                               ([prev["t_store"]] if prev["t_store"] else []))
                prev["og_rd2"][sl] = t_oln
                t_tro = None
                for j in range(KT):
                    t_tro = s.emit(T, lambda e, j=j: e.transpose(
                        ptr[:, j, :], sb_o[:, j * 128:(j + 1) * 128], identbf[:, :]),
                        after=[t_oln, prev["ptr_free"]] if j == 0 else [])
                t_oT = s.emit(S, lambda e: e.copy(qT[:, :, :], ptr[:, :, :]),
                              after=[t_tro])
                prev["ptr_free"] = t_oT
                t_ao = None
                for nh in range(2):
                    for k in range(KT):
                        def fa(e, nh=nh, k=k):
                            return e.matmul(pbig0[:, nh * 512:(nh + 1) * 512],
                                            qT[:, k, :],
                                            Wo_bf[:, k, nh * 512:(nh + 1) * 512],
                                            start=(k == 0), stop=False)
                        t_ao = s.emit(T, fa,
                                      after=[t_oT, f["t_vb"]] + t_wog
                                      if (nh == 0 and k == 0) else [])
                    def fb(e, nh=nh):
                        return e.matmul(pbig0[:, nh * 512:(nh + 1) * 512],
                                        ones_row[:, :],
                                        bWo_row[0:1, nh * 512:(nh + 1) * 512],
                                        start=False, stop=True)
                    t_ao = s.emit(T, fb, after=[t_ones] if nh == 0 else [])
                prev["qT_rd"] = [t_ao]
                ln_toks, rs_ap, m_ap = emit_ln_stats(scol2, pbig0[:, :],
                                                     [t_ao], psum_src=True)
                t_aoln = s.emit(V, lambda e, rs_ap=rs_ap, m_ap=m_ap:
                                e.tensor_scalar(sb_o[:, :], pbig0[:, :],
                                                m_ap, rs_ap,
                                                ALU.subtract, ALU.mult),
                                after=ln_toks + [t_oT])
                prev["pbig0_free"] = t_aoln
                t_x1 = s.emit(V, lambda e, xc=xc: e.tensor_tensor(
                    x1b[:, :], sb_o[:, :], xc, ALU.add),
                    after=[t_aoln] +
                    ([prev["x1_store"]] if prev["x1_store"] else []))
                t_x1d = s.emit(SY, lambda e, c=c: e.dma_start(
                    x1s[c * C:(c + 1) * C, :], x1b[:, :]), after=[t_x1],
                    dma="dx1s")
                prev["x1_store"] = t_x1d
                if c + 2 < n_chunks:
                    tok_x[c + 2] = s.emit(SY, lambda e, c=c: e.dma_start(
                        xb[:, c % 2, :], x[(c + 2) * C:(c + 3) * C, :]),
                        after=[t_x1], dma="dx0" if (c + 2) % 2 == 0 else "dx1")
                ln_toks, rs_ap, m_ap = emit_ln_stats(scol2, x1b[:, :],
                                                     [t_x1], sq_eng="V")
                t_tb = s.emit(V, lambda e, rs_ap=rs_ap, m_ap=m_ap:
                              e.tensor_scalar(sb_o[:, :], x1b[:, :],
                                              m_ap, rs_ap,
                                              ALU.subtract, ALU.mult),
                              after=ln_toks + [t_x1])
                prev["scol2_rd"] = t_tb
                t_tbd = s.emit(SY, lambda e, c=c: e.dma_start(
                    ts_[c * C:(c + 1) * C, :], sb_o[:, :]), after=[t_tb],
                    dma="dts")
                prev["t_store"] = t_tbd
                t_trt = None
                for j in range(KT):
                    t_trt = s.emit(T, lambda e, j=j: e.transpose(
                        ptr[:, j, :], sb_o[:, j * 128:(j + 1) * 128], identbf[:, :]),
                        after=[t_tb, prev["ptr_free"]] if j == 0 else [])
                prev["sb_o_rd"] = t_trt
                t_tT = s.emit(S, lambda e, c=c: e.copy(
                    tT[:, :, c * C:(c + 1) * C], ptr[:, :, :]), after=[t_trt])
                prev["ptr_free"] = t_tT
                prev["outT_rd"] = t_ao
                prev["tT_slot"] = t_tT

            fstate = {}
            for i in range(n_chunks + 1):
                if i < n_chunks:
                    fstate[i] = emit_front(i)
                if i >= 1:
                    emit_back(i - 1, fstate.pop(i - 1))

            a_final = []
            if prev["tT_slot"] is not None:
                a_final.append(prev["tT_slot"])
            for nm in ("dx1s", "dts"):
                if s.cnt.get(nm, 0) > 0:
                    a_final.append((nm, s.cnt[nm]))
            with nc.Block() as blockA:
                s.flush(blockA, final_tokens=a_final)

        if "B" not in phases:
            return nc
        with ExitStack() as bctx:
            be = bctx.enter_context
            rw1_bf = be(nc.sbuf_tensor("rw1_bf", [128, KT, E], BF16))
            rw2_bf = be(nc.sbuf_tensor("rw2_bf", [128, KT, NEXP], BF16))
            rb1_sb = be(nc.sbuf_tensor("rb1_sb", [128, KT], F32))
            rb2_sb = be(nc.sbuf_tensor("rb2_sb", [128, NEXP], F32))
            r1b = be(nc.sbuf_tensor("r1b", [128, KT, 512], BF16))
            l_sb = be(nc.sbuf_tensor("l_sb", [128, NCH, NEXP], F32))
            tpw = be(nc.sbuf_tensor("tpw", [128, 6 * NEXP], F32))
            wbf = be(nc.sbuf_tensor("wbf", [128, NEXP], BF16))
            pr0 = be(nc.psum_tensor("pr0", [128, 512], F32))
            pr1 = be(nc.psum_tensor("pr1", [128, 512], F32))
            pl = be(nc.psum_tensor("pl", [128, NEXP], F32))
            pwt = be(nc.psum_tensor("pwt", [NEXP, 128], BF16))

            s2 = Sched(nc, bctx, "b")
            T, V, S, GP, SY = "tensor", "vector", "scalar", "gpsimd", "sync"
            t_rw1 = s2.emit(GP, lambda e: e.dma_start(
                rw1_bf[:, :, :], rw1[:, :].rearrange("(k p) n -> p k n", p=128)),
                dma="dw")
            t_rw2 = s2.emit(GP, lambda e: e.dma_start(
                rw2_bf[:, :, :], rw2[:, :].rearrange("(k p) n -> p k n", p=128)),
                dma="dw")
            t_rw1 = t_rw2 = ("dw", s2.cnt["dw"])
            t_rb1 = s2.emit(SY, lambda e: e.dma_start(rb1_sb[:, :], rb1_fm[:, :]),
                            dma="dmisc")
            t_rb2 = s2.emit(SY, lambda e: e.dma_start(rb2_sb[:, :], rb2_b[:, :]),
                            dma="dmisc")
            t_rb1 = t_rb2 = ("dmisc", s2.cnt["dmisc"])
            prfree = [None, None]
            t_lev = [None] * NCH
            for t8 in range(NTOK // 512):
                t_r1 = [None] * KT
                for m in range(KT):
                    pr = (pr0, pr1)[m % 2]
                    t_mm = None
                    for k in range(KT):
                        def f(e, m=m, k=k, t8=t8, pr=pr):
                            return e.matmul(pr[:, :], rw1_bf[:, k, m * 128:(m + 1) * 128],
                                            tT[:, k, t8 * 512:(t8 + 1) * 512],
                                            start=(k == 0), stop=(k == KT - 1))
                        aft = []
                        if k == 0:
                            aft = [t_rw1]
                            if prfree[m % 2] is not None:
                                aft.append(prfree[m % 2])
                        t_mm = s2.emit(T, f, after=aft)
                    t_r1[m] = s2.emit(S, lambda e, m=m, pr=pr: e.activation(
                        r1b[:, m, :], pr[:, :], AF.Silu,
                        bias=rb1_sb[:, m:m + 1]), after=[t_mm, t_rb1])
                    prfree[m % 2] = t_r1[m]
                for sub in range(4):
                    ci = t8 * 4 + sub
                    t_lmm = None
                    for k in range(KT):
                        def f(e, k=k, sub=sub):
                            return e.matmul(pl[:, :], r1b[:, k, sub * 128:(sub + 1) * 128],
                                            rw2_bf[:, k, :],
                                            start=(k == 0), stop=(k == KT - 1))
                        t_lmm = s2.emit(T, f,
                                        after=[t_r1[KT - 1], t_rw2] + ([t_lev[ci - 1]] if ci > 0 else [])
                                        if k == 0 else [])
                    t_lev[ci] = s2.emit(V, lambda e, ci=ci: e.tensor_tensor(
                        l_sb[:, ci, :], pl[:, :], rb2_sb[:, :], ALU.add),
                        after=[t_lmm, t_rb2])
            cm1, cnm1, cm2, cZ, crz = 0, 1, 2, 3, 4
            OFF = NEXP
            t_wT_ev = None
            t_wsb = [None] * NCH
            for ci in range(NCH):
                lap = l_sb[:, ci, :]
                t_m1 = s2.emit(V, lambda e, lap=lap: e.reduce_max(
                    tpw[:, cm1:cm1 + 1], lap, axis=AX.X), after=[t_lev[ci]])
                t_nm1 = s2.emit(V, lambda e: e.tensor_scalar(
                    tpw[:, cnm1:cnm1 + 1], tpw[:, cm1:cm1 + 1], -1.0, None,
                    ALU.mult), after=[t_m1])
                t_msk1 = s2.emit(V, lambda e, lap=lap: e.tensor_scalar(
                    tpw[:, OFF:OFF + NEXP], lap, tpw[:, cm1:cm1 + 1], None,
                    ALU.is_ge), after=[t_m1])
                t_masked = s2.emit(V, lambda e, lap=lap: e.scalar_tensor_tensor(
                    tpw[:, 2 * OFF:2 * OFF + NEXP], tpw[:, OFF:OFF + NEXP], -BIG,
                    lap, ALU.mult, ALU.add), after=[t_msk1])
                t_m2 = s2.emit(V, lambda e: e.reduce_max(
                    tpw[:, cm2:cm2 + 1], tpw[:, 2 * OFF:2 * OFF + NEXP],
                    axis=AX.X), after=[t_masked])
                t_sel = s2.emit(V, lambda e, lap=lap: e.tensor_scalar(
                    tpw[:, 3 * OFF:3 * OFF + NEXP], lap, tpw[:, cm2:cm2 + 1],
                    None, ALU.is_ge), after=[t_m2])
                t_e = s2.emit(S, lambda e, lap=lap: e.activation(
                    tpw[:, 4 * OFF:4 * OFF + NEXP], lap, AF.Exp,
                    bias=tpw[:, cnm1:cnm1 + 1]), after=[t_nm1, t_lev[ci]])
                t_es = s2.emit(V, lambda e: e.tensor_tensor(
                    tpw[:, 5 * OFF:5 * OFF + NEXP], tpw[:, 4 * OFF:4 * OFF + NEXP],
                    tpw[:, 3 * OFF:3 * OFF + NEXP], ALU.mult),
                    after=[t_e, t_sel])
                t_Z = s2.emit(V, lambda e: e.reduce_sum(
                    tpw[:, cZ:cZ + 1], tpw[:, 5 * OFF:5 * OFF + NEXP],
                    axis=AX.X), after=[t_es])
                t_rz = s2.emit(V, lambda e: e.reciprocal(
                    tpw[:, crz:crz + 1], tpw[:, cZ:cZ + 1]), after=[t_Z])
                t_wsb[ci] = s2.emit(V, lambda e, ci=ci: e.tensor_scalar(
                    wsb[:, ci, :], tpw[:, 5 * OFF:5 * OFF + NEXP],
                    tpw[:, crz:crz + 1], None, ALU.mult), after=[t_rz])
                t_wbf = s2.emit(V, lambda e, ci=ci: e.tensor_copy(
                    wbf[:, :], wsb[:, ci, :]), after=[t_wsb[ci], t_wT_ev]
                    if t_wT_ev else [t_wsb[ci]])
                t_wT = s2.emit(T, lambda e: e.transpose(
                    pwt[:, :], wbf[:, :], identbf[:, :]),
                    after=[t_wbf] + ([t_wT_ev] if t_wT_ev else []))
                t_wT_ev = s2.emit(S, lambda e, ci=ci: e.copy(
                    wTsb[:, ci * C:(ci + 1) * C], pwt[:, :]), after=[t_wT])
            if debug:
                t_wdump = s2.emit(SY, lambda e: e.dma_start(
                    dbg_w[:, :, :].rearrange("c p e -> p c e"), wsb[:, :, :]),
                    after=t_wsb, dma="ddbg")
            b_final = [t_wT_ev] + t_wsb + (
                [("ddbg", s2.cnt.get("ddbg", 0))] if debug else [])
            with nc.Block() as blockB:
                s2.flush(blockB, final_tokens=b_final)

        if "C" not in phases:
            return nc
        with ExitStack() as cctx:
            ce = cctx.enter_context
            ew1_bf = ce(nc.sbuf_tensor("ew1_bf", [128, KT, DH], BF16))
            ew2_bf = ce(nc.sbuf_tensor("ew2_bf", [128, KT2, E], BF16))
            ehb = ce(nc.sbuf_tensor("ehb", [128, KT2, 512], BF16))
            yac = ce(nc.sbuf_tensor("yac", [128, CPG, E], BF16))
            eb1_sb = ce(nc.sbuf_tensor("eb1_sb", [128, NEXP, KT2], F32))
            eb2_bf = ce(nc.sbuf_tensor("eb2_bf", [NEXP, E], BF16))
            x1c = ce(nc.sbuf_tensor("x1c", [128, E], F32))
            tc = ce(nc.sbuf_tensor("tc", [128, E], BF16))
            yf = ce(nc.sbuf_tensor("yf", [128, E], F32))
            ylnb = ce(nc.sbuf_tensor("ylnb", [128, E], BF16))
            sc3 = ce(nc.sbuf_tensor("sc3", [128, 16], F32))
            peh0 = ce(nc.psum_tensor("peh0", [128, 512], F32))
            peh1 = ce(nc.psum_tensor("peh1", [128, 512], F32))
            peo0 = ce(nc.psum_tensor("peo0", [128, 1024], F32))
            peo1 = ce(nc.psum_tensor("peo1", [128, 1024], F32))

            s3 = Sched(nc, cctx, "c")
            T, V, S, GP, SY = "tensor", "vector", "scalar", "gpsimd", "sync"

            t_eb1 = s3.emit(SY, lambda e: e.dma_start(eb1_sb[:, :, :],
                                                      eb1_fm[:, :, :]), dma="dmisc")
            t_eb2 = s3.emit(GP, lambda e: e.dma_start(eb2_bf[:, :], eb2[:, :]),
                            dma="dmiscg")
            t_eb1 = ("dmisc", s3.cnt["dmisc"])
            t_eb2 = ("dmiscg", s3.cnt["dmiscg"])

            def load_ew1(eidx, after):
                return s3.emit(GP, lambda e, eidx=eidx: e.dma_start(
                    ew1_bf[:, :, :],
                    ew1[eidx, :, :].rearrange("(k p) h -> p k h", p=128)),
                    after=after, dma="dw1")

            def load_ew2(eidx, after):
                return s3.emit(GP, lambda e, eidx=eidx: e.dma_start(
                    ew2_bf[:, :, :],
                    ew2[eidx, :, :].rearrange("(k p) n -> p k n", p=128)),
                    after=after, dma="dw2")

            t_ew1 = load_ew1(0, [])
            t_dbg_eh = None
            t_ew2 = None
            ew1_lastuse = None
            ew2_lastuse = None
            t_out_final = []

            MAGIC3 = float(0x5F3759DF)

            def emit_rsqrt3(colbuf, c_v, c_out, after):
                ci = colbuf[:, :].bitcast(I32)
                t = s3.emit(V, lambda e: e.tensor_scalar(
                    ci[:, 10:11], ci[:, c_v:c_v + 1], 1, None,
                    ALU.logical_shift_right), after=after)
                s3.emit(V, lambda e: e.tensor_copy(colbuf[:, 11:12], ci[:, 10:11]))
                s3.emit(V, lambda e: e.tensor_scalar(
                    colbuf[:, 12:13], colbuf[:, 11:12], -1.0, MAGIC3,
                    ALU.mult, ALU.add))
                s3.emit(V, lambda e: e.tensor_copy(ci[:, 13:14], colbuf[:, 12:13]))
                s3.emit(V, lambda e: e.tensor_copy(
                    colbuf[:, c_out:c_out + 1], ci[:, 13:14].bitcast(F32)[:, 0:1]))
                for _ in range(3):
                    s3.emit(V, lambda e: e.tensor_tensor(
                        colbuf[:, 10:11], colbuf[:, c_out:c_out + 1],
                        colbuf[:, c_out:c_out + 1], ALU.mult))
                    s3.emit(V, lambda e: e.tensor_tensor(
                        colbuf[:, 11:12], colbuf[:, c_v:c_v + 1],
                        colbuf[:, 10:11], ALU.mult))
                    s3.emit(V, lambda e: e.tensor_scalar(
                        colbuf[:, 12:13], colbuf[:, 11:12], -0.5, 1.5,
                        ALU.mult, ALU.add))
                    t = s3.emit(V, lambda e: e.tensor_tensor(
                        colbuf[:, c_out:c_out + 1], colbuf[:, c_out:c_out + 1],
                        colbuf[:, 12:13], ALU.mult))
                return t

            def emit_ln3(src_ap, after_tok):
                c_sx, c_sq, c_m, c_a, c_msq, c_b, c_rs = 0, 1, 2, 3, 4, 5, 6
                t_sx = s3.emit(V, lambda e: e.reduce_sum(
                    sc3[:, c_sx:c_sx + 1], src_ap, axis=AX.X), after=after_tok)
                t_sq = s3.emit(S, lambda e: e.activation(
                    tc[:, :], src_ap, AF.Square,
                    accum_out=sc3[:, c_sq:c_sq + 1]), after=after_tok)
                t_m = s3.emit(V, lambda e: e.tensor_scalar(
                    sc3[:, c_m:c_m + 1], sc3[:, c_sx:c_sx + 1], 1.0 / E, None,
                    ALU.mult), after=[t_sx])
                t_a = s3.emit(V, lambda e: e.tensor_scalar(
                    sc3[:, c_a:c_a + 1], sc3[:, c_sq:c_sq + 1], 1.0 / E, EPS,
                    ALU.mult, ALU.add), after=[t_sq])
                t_msq = s3.emit(V, lambda e: e.tensor_tensor(
                    sc3[:, c_msq:c_msq + 1], sc3[:, c_m:c_m + 1],
                    sc3[:, c_m:c_m + 1], ALU.mult), after=[t_m])
                t_b = s3.emit(V, lambda e: e.tensor_tensor(
                    sc3[:, c_b:c_b + 1], sc3[:, c_a:c_a + 1],
                    sc3[:, c_msq:c_msq + 1], ALU.subtract), after=[t_a, t_msq])
                t_rs = emit_rsqrt3(sc3, c_b, c_rs, [t_b])
                return ([t_rs], sc3[:, c_rs:c_rs + 1], sc3[:, c_m:c_m + 1])

            slot_guard = [None] * CPG
            peo_guard = [None, None]
            t_prev_out = None
            t_prev_od = None
            t_prev_dbg = None
            for g in range(NGRP):
                tok0 = g * TPG
                ch0 = g * CPG
                t_yinit = [None] * CPG
                pyfree = None
                for ci in range(CPG):
                    t_mm = None
                    for nh in range(2):
                        def f(e, ci=ci, nh=nh, ch0=ch0):
                            return e.matmul(
                                peo0[:, nh * 512:(nh + 1) * 512],
                                wTsb[:, (ch0 + ci) * C:(ch0 + ci + 1) * C],
                                eb2_bf[:, nh * 512:(nh + 1) * 512],
                                start=True, stop=True)
                        aft = []
                        if nh == 0:
                            aft = [t_eb2]
                            if pyfree is not None:
                                aft.append(pyfree)
                            elif peo_guard[0] is not None:
                                aft.append(peo_guard[0])
                        t_mm = s3.emit(T, f, after=aft)
                    aft_ev = [t_mm]
                    if slot_guard[ci] is not None:
                        aft_ev.append(slot_guard[ci])
                    t_yinit[ci] = s3.emit(S, lambda e, ci=ci: e.copy(
                        yac[:, ci, :], peo0[:, :]), after=aft_ev)
                    pyfree = t_yinit[ci]
                peo_free = [pyfree, pyfree]
                peh_free = [None, None]
                t_yacc_last = t_yinit

                elist = list(range(NEXP)) if g == 0 else list(range(NEXP - 1, -1, -1))
                for ei, eidx in enumerate(elist):
                    first = (g == 0 and ei == 0)
                    if not (g == 1 and ei == 0):
                        t_ew2 = load_ew2(eidx, [ew2_lastuse] if ew2_lastuse else [])
                    for t4 in range(TPG // 512):
                        tstart = tok0 + t4 * 512
                        t_ehev = [None] * KT2
                        for m in range(KT2):
                            ph = (peh0, peh1)[m % 2]
                            t_mm = None
                            for k in range(KT):
                                def f(e, m=m, k=k, ph=ph, tstart=tstart):
                                    return e.matmul(
                                        ph[:, :], ew1_bf[:, k, m * 128:(m + 1) * 128],
                                        tT[:, k, tstart:tstart + 512],
                                        start=(k == 0), stop=(k == KT - 1))
                                aft = []
                                if k == 0:
                                    aft = [t_ew1]
                                    if peh_free[m % 2] is not None:
                                        aft.append(peh_free[m % 2])
                                t_mm = s3.emit(T, f, after=aft)
                            t_ehev[m] = s3.emit(S, lambda e, m=m, ph=ph, eidx=eidx:
                                                e.activation(
                                ehb[:, m, :], ph[:, :], AF.Silu,
                                bias=eb1_sb[:, eidx, m:m + 1]),
                                after=[t_mm, t_eb1] +
                                ([t_dbg_eh] if t_dbg_eh else []))
                            peh_free[m % 2] = t_ehev[m]
                        ew1_lastuse = t_mm
                        if debug and g == 0 and ei == 0 and t4 == 0:
                            t_dbg_eh = s3.emit(SY, lambda e: e.dma_start(
                                dbg_eh[:, :, :], ehb[:, :, :]),
                                after=[t_ehev[KT2 - 1]], dma="ddbg")
                        if t4 == TPG // 512 - 1:
                            nxt = None
                            if ei + 1 < NEXP:
                                nxt = elist[ei + 1]
                            elif g == 0:
                                nxt = None
                            if nxt is not None:
                                t_ew1 = load_ew1(nxt, [ew1_lastuse])
                        for sub in range(4):
                            ci = t4 * 4 + sub
                            po = (peo0, peo1)[sub % 2]
                            t_mm = None
                            for nh in range(2):
                                for k in range(KT2):
                                    def f(e, sub=sub, nh=nh, k=k, po=po):
                                        return e.matmul(
                                            po[:, nh * 512:(nh + 1) * 512],
                                            ehb[:, k, sub * 128:(sub + 1) * 128],
                                            ew2_bf[:, k, nh * 512:(nh + 1) * 512],
                                            start=(k == 0), stop=(k == KT2 - 1))
                                    aft = []
                                    if nh == 0 and k == 0:
                                        aft = [t_ehev[KT2 - 1], t_ew2]
                                        if peo_free[sub % 2] is not None:
                                            aft.append(peo_free[sub % 2])
                                    t_mm = s3.emit(T, f, after=aft)
                            ew2_lastuse = t_mm
                            def facc(e, ci=ci, po=po, eidx=eidx, ch0=ch0):
                                return e.scalar_tensor_tensor(
                                    yac[:, ci, :], po[:, :],
                                    wsb[:, ch0 + ci, eidx:eidx + 1],
                                    yac[:, ci, :], ALU.mult, ALU.add)
                            t_yacc_last[ci] = s3.emit(V, facc,
                                                      after=[t_mm, t_yacc_last[ci]])
                            peo_free[sub % 2] = t_yacc_last[ci]
                    if debug and g == 0 and ei == 0 and t4 == TPG // 512 - 1:
                        t_dbgyac = s3.emit(SY, lambda e: e.dma_start(
                            dbg_yac[:, :, :], yac[:, :, :]),
                            after=[t_yacc_last[i] for i in range(CPG)],
                            dma="ddbg")
                        for i in range(CPG):
                            t_yacc_last[i] = t_dbgyac
                for ci in range(CPG):
                    cg = ch0 + ci
                    aft_in = ([t_prev_out] if t_prev_out else []) + (
                        [t_prev_od] if t_prev_od else [])
                    t_x1c = s3.emit(SY, lambda e, cg=cg: e.dma_start(
                        x1c[:, :], x1s[cg * C:(cg + 1) * C, :]),
                        after=aft_in, dma="dpcx")
                    t_tc = s3.emit(SY, lambda e, cg=cg: e.dma_start(
                        tc[:, :], ts_[cg * C:(cg + 1) * C, :]),
                        after=aft_in, dma="dpct")
                    t_y = s3.emit(V, lambda e, ci=ci: e.tensor_tensor(
                        yf[:, :], yac[:, ci, :], tc[:, :], ALU.add),
                        after=[t_tc, t_yacc_last[ci]] +
                              ([t_prev_dbg] if t_prev_dbg else []))
                    slot_guard[ci] = t_y
                    if debug:
                        t_prev_dbg = s3.emit(SY, lambda e, cg=cg: e.dma_start(
                            dbg_y[cg * C:(cg + 1) * C, :], yf[:, :]),
                            after=[t_y], dma="ddbg")
                    ln_toks, rs_ap, m_ap = emit_ln3(yf[:, :], [t_y])
                    t_yln = s3.emit(V, lambda e, rs_ap=rs_ap, m_ap=m_ap:
                                    e.tensor_scalar(ylnb[:, :], yf[:, :],
                                                    m_ap, rs_ap,
                                                    ALU.subtract, ALU.mult),
                                    after=ln_toks)
                    t_ob = s3.emit(V, lambda e: e.tensor_tensor(
                        x1c[:, :], ylnb[:, :], x1c[:, :], ALU.add),
                        after=[t_yln, t_x1c])
                    t_od = s3.emit(SY, lambda e, cg=cg: e.dma_start(
                        out[cg * C:(cg + 1) * C, :], x1c[:, :]),
                        after=[t_ob], dma="dout")
                    t_out_final.append(t_od)
                    t_prev_out = t_ob
                    t_prev_od = t_od
                peo_guard = list(peo_free)

            with nc.Block() as blockC:
                s3.flush(blockC, final_tokens=[("dout", s3.cnt["dout"])] + (
                    [("ddbg", s3.cnt.get("ddbg", 0))] if debug else []))

    return nc


_NC_CACHE = None


def _get_nc():
    global _NC_CACHE
    if _NC_CACHE is None:
        _NC_CACHE = build()
    return _NC_CACHE


def kernel(**inputs):
    cos_rep, sin_rep, triu, ident = _host_tables()
    f32 = lambda a: np.ascontiguousarray(np.asarray(a, dtype=np.float32))
    g = {k: f32(v) for k, v in inputs.items()}
    base = {
        "Wq": g["L0_attn_Wq"], "Wk": g["L0_attn_Wk"], "Wv": g["L0_attn_Wv"],
        "Wo": g["L0_attn_Wo"], "Wg1": g["L0_attn_Wg1"], "Wg2": g["L0_attn_Wg2"],
        "gamma_fm": f32(g["L0_attn_gamma"].reshape(KT, 128).T),
        "beta_fm": f32(g["L0_attn_beta"].reshape(KT, 128).T),
        "rw1": g["L0_ffn_rw1"],
        "rb1_fm": f32(g["L0_ffn_rb1"].reshape(KT, 128).T),
        "rw2": g["L0_ffn_rw2"],
        "rb2_b": f32(np.broadcast_to(g["L0_ffn_rb2"], (128, NEXP))),
        "ew1": g["L0_ffn_ew1"],
        "eb1_fm": f32(g["L0_ffn_eb1"].reshape(NEXP, KT2, 128).transpose(2, 0, 1)),
        "ew2": g["L0_ffn_ew2"],
        "eb2": g["L0_ffn_eb2"],
        "cosr": cos_rep, "sinr": sin_rep, "triu": triu, "ident": ident,
    }
    xfull = g["L0_attn_x"]
    in_maps = []
    for b in range(NB):
        m = dict(base)
        m["x"] = np.ascontiguousarray(xfull[b])
        in_maps.append(m)
    nc = _get_nc()
    res = run_bass_kernel_spmd(nc, in_maps, core_ids=list(range(NB)))
    outp = np.stack([np.asarray(res.results[b]["out"], dtype=np.float32)
                     for b in range(NB)], axis=0)
    return outp


# version 07
# speedup vs baseline: 1.0015x; 1.0015x over previous
import numpy as np
import ml_dtypes
from contextlib import ExitStack

import concourse.bass as bass
import concourse.mybir as mybir
from concourse.bass_utils import run_bass_kernel_spmd

F32 = mybir.dt.float32
BF16 = mybir.dt.bfloat16
I32 = mybir.dt.int32
AF = mybir.ActivationFunctionType
ALU = mybir.AluOpType
AX = mybir.AxisListType

NB, NTOK, E = 8, 4096, 1024
H, D = 16, 64
C = 128
NCH = NTOK // C
KT = E // 128
NEXP, DH = 8, 2048
KT2 = DH // 128
EPS = 1e-5
RBASE = 10000.0
NGRP = 2
CPG = NCH // NGRP
TPG = NTOK // NGRP
BIG = 1.0e30

ENGS = ("sync", "gpsimd", "vector", "scalar", "tensor")


class Sched:

    def __init__(self, nc, ctx, tag):
        self.nc = nc
        self.ctx = ctx
        self.tag = tag
        self.sem = {}
        self.cnt = {}
        for e in ENGS:
            self.sem[e] = ctx.enter_context(nc.semaphore(f"{tag}_{e}"))
            self.cnt[e] = 0
        self.recs = {e: [] for e in ENGS}
        self.pend_bar = {e: False for e in ENGS}
        self.bar_snapshot = None

    def _dma_sem(self, name):
        if name not in self.sem:
            self.sem[name] = self.ctx.enter_context(
                self.nc.semaphore(f"{self.tag}_{name}"))
            self.cnt[name] = 0
        return name

    SELF_CHAIN_HW = ("scalar", "vector", "gpsimd")
    SELF_CHAIN_SIM = ("vector", "scalar", "gpsimd")
    self_chain = False

    def emit(self, eng, fn, after=(), dma=None):
        chain = self.SELF_CHAIN_SIM if self.self_chain else self.SELF_CHAIN_HW
        waits = []
        if eng in chain and self.cnt[eng] > 0:
            waits.append((eng, self.cnt[eng]))
        if self.pend_bar[eng] and self.bar_snapshot:
            for s, v in self.bar_snapshot.items():
                if s != eng and v > 0:
                    waits.append((s, v))
            self.pend_bar[eng] = False
        for s, v in after:
            if s == eng:
                continue
            waits.append((s, v))
        if dma is not None:
            semname = self._dma_sem(dma)
            inc = 16
        else:
            semname = eng
            inc = 1
        self.cnt[semname] += inc
        tok = (semname, self.cnt[semname])
        self.recs[eng].append((waits, fn, semname, inc))
        return tok

    def barrier(self):
        self.bar_snapshot = {e: self.cnt[e] for e in ENGS}
        for e in ENGS:
            self.pend_bar[e] = True

    def flush(self, block, final_tokens=()):
        recs = self.recs
        sems = self.sem
        waited_vals = {}
        for ename in ENGS:
            for waits, fn, semname, inc in recs[ename]:
                for sname, v in waits:
                    waited_vals.setdefault(sname, set()).add(v)
        for sname, v in final_tokens:
            waited_vals.setdefault(sname, set()).add(v)
        remap = {}
        for ename in ENGS:
            vals = sorted(waited_vals.get(ename, ()))
            remap[ename] = ({v: i + 1 for i, v in enumerate(vals)}, set(vals))

        def map_tok(sname, v):
            if sname in remap:
                return remap[sname][0][v]
            return v

        def make_body(ename, extra_final):
            inc_set = remap[ename][1]

            def body(eng):
                waited = {}
                dense = 0
                for waits, fn, semname, inc in recs[ename]:
                    for sname, v in waits:
                        mv = map_tok(sname, v)
                        if waited.get(sname, 0) >= mv:
                            continue
                        eng.wait_ge(sems[sname], mv)
                        waited[sname] = mv
                    inst = fn(eng)
                    if semname == ename:
                        dense += 1
                        if dense in inc_set:
                            inst.then_inc(sems[semname], 1)
                    else:
                        inst.then_inc(sems[semname], inc)
                for sname, v in extra_final:
                    mv = map_tok(sname, v)
                    if waited.get(sname, 0) >= mv:
                        continue
                    eng.wait_ge(sems[sname], mv)
                    waited[sname] = mv
            return body

        block.sync(make_body("sync", list(final_tokens)))
        block.gpsimd(make_body("gpsimd", []))
        block.vector(make_body("vector", []))
        block.scalar(make_body("scalar", []))
        block.tensor(make_body("tensor", []))


def _host_tables():
    d2 = np.arange(D // 2, dtype=np.float64)
    inv = RBASE ** (-(d2 * 2.0) / D)
    pos = np.arange(NTOK, dtype=np.float64)
    ang = pos[:, None] * inv[None, :]
    cos = np.cos(ang).astype(np.float32)
    sin = np.sin(ang).astype(np.float32)
    cos_rep = np.ascontiguousarray(np.tile(cos, (1, H)))
    sin_rep = np.ascontiguousarray(np.tile(sin, (1, H)))
    triu = np.triu(np.ones((C, C), dtype=np.float32))
    ident = np.eye(C, dtype=np.float32)
    return cos_rep, sin_rep, triu, ident


def build(debug=False, n_chunks=NCH, phases="ABC", stage=999,
          self_chain=False):
    nc = bass.Bass("TRN2", target_bir_lowering=False)
    dt = nc.dram_tensor
    x = dt("x", [NTOK, E], F32, kind="ExternalInput")
    Wq = dt("Wq", [E, E], F32, kind="ExternalInput")
    Wk = dt("Wk", [E, E], F32, kind="ExternalInput")
    Wv = dt("Wv", [E, E], F32, kind="ExternalInput")
    Wo = dt("Wo", [E, E], F32, kind="ExternalInput")
    Wg1 = dt("Wg1", [E, D], F32, kind="ExternalInput")
    Wg2 = dt("Wg2", [D, E], F32, kind="ExternalInput")
    gamma_fm = dt("gamma_fm", [128, KT], F32, kind="ExternalInput")
    beta_fm = dt("beta_fm", [128, KT], F32, kind="ExternalInput")
    rw1 = dt("rw1", [E, E], F32, kind="ExternalInput")
    rb1_fm = dt("rb1_fm", [128, KT], F32, kind="ExternalInput")
    rw2 = dt("rw2", [E, NEXP], F32, kind="ExternalInput")
    rb2_b = dt("rb2_b", [128, NEXP], F32, kind="ExternalInput")
    ew1 = dt("ew1", [NEXP, E, DH], F32, kind="ExternalInput")
    eb1_fm = dt("eb1_fm", [128, NEXP, KT2], F32, kind="ExternalInput")
    ew2 = dt("ew2", [NEXP, DH, E], F32, kind="ExternalInput")
    eb2 = dt("eb2", [NEXP, E], F32, kind="ExternalInput")
    cosr = dt("cosr", [NTOK, H * 32], F32, kind="ExternalInput")
    sinr = dt("sinr", [NTOK, H * 32], F32, kind="ExternalInput")
    triu_in = dt("triu", [C, C], F32, kind="ExternalInput")
    ident_in = dt("ident", [C, C], F32, kind="ExternalInput")
    out = dt("out", [NTOK, E], F32, kind="ExternalOutput")
    skind = "ExternalOutput" if debug else "Internal"
    x1s = dt("x1s", [NTOK, E], F32, kind=skind)
    ts_ = dt("ts", [NTOK, E], BF16, kind=skind)
    dbg_w = dt("dbg_w", [NCH, 128, NEXP], F32, kind=skind) if debug else None
    dbg_y = dt("dbg_y", [NTOK, E], F32, kind=skind) if debug else None
    dbg_eh = dt("dbg_eh", [128, KT2, 512], BF16, kind=skind) if debug else None
    dbg_yac = dt("dbg_yac", [128, CPG, E], BF16, kind=skind) if debug else None

    with ExitStack() as octx:
        oe = octx.enter_context
        tT = oe(nc.sbuf_tensor("tT", [128, KT, NTOK], BF16))
        wsb = oe(nc.sbuf_tensor("wsb", [128, NCH, NEXP], F32))
        wTsb = oe(nc.sbuf_tensor("wTsb", [NEXP, NTOK], BF16))
        ones_row = oe(nc.sbuf_tensor("ones_row", [1, 128], BF16))
        bWo_row = oe(nc.sbuf_tensor("bWo_row", [1, E], BF16))
        identbf = oe(nc.sbuf_tensor("identbf", [C, C], BF16))
        trium = oe(nc.sbuf_tensor("trium", [C, C], F32))

        with ExitStack() as actx:
            ae = actx.enter_context
            Wq_bf = ae(nc.sbuf_tensor("Wq_bf", [128, KT, E], BF16))
            Wk_bf = ae(nc.sbuf_tensor("Wk_bf", [128, KT, E], BF16))
            Wv_bf = ae(nc.sbuf_tensor("Wv_bf", [128, KT, E], BF16))
            Wo_bf = ae(nc.sbuf_tensor("Wo_bf", [128, KT, E], BF16))
            Wg1_bf = ae(nc.sbuf_tensor("Wg1_bf", [128, KT, D], BF16))
            Wg2_bf = ae(nc.sbuf_tensor("Wg2_bf", [D, E], BF16))
            gamma_sb = ae(nc.sbuf_tensor("gamma_sb", [128, KT], F32))
            beta_sb = ae(nc.sbuf_tensor("beta_sb", [128, KT], F32))
            beta_bf = ae(nc.sbuf_tensor("beta_bf", [128, KT], BF16))
            cosb = ae(nc.sbuf_tensor("cosb", [128, 512], F32))
            sinb = ae(nc.sbuf_tensor("sinb", [128, 512], F32))
            xb = ae(nc.sbuf_tensor("xb", [128, 2, E], F32))
            ln1b = ae(nc.sbuf_tensor("ln1b", [128, E], BF16))
            ln1T = ae(nc.sbuf_tensor("ln1T", [128, KT, C], BF16))
            qs = ae(nc.sbuf_tensor("qs", [128, E], F32))
            ks = ae(nc.sbuf_tensor("ks", [128, E], F32))
            rt2 = ae(nc.sbuf_tensor("rt2", [128, 512], F32))
            kt2 = ae(nc.sbuf_tensor("kt2", [128, 512], F32))
            qro = ae(nc.sbuf_tensor("qro", [128, E], BF16))
            kro = ae(nc.sbuf_tensor("kro", [128, E], BF16))
            vb = ae(nc.sbuf_tensor("vb", [128, E], BF16))
            qT = ae(nc.sbuf_tensor("qT", [128, KT, C], BF16))
            kTt = ae(nc.sbuf_tensor("kTt", [128, KT, C], BF16))
            g1b = ae(nc.sbuf_tensor("g1b", [D, C], BF16))
            th = ae(nc.sbuf_tensor("th", [128, E], BF16))
            gateb = ae(nc.sbuf_tensor("gateb", [128, E], BF16))
            eT = ae(nc.sbuf_tensor("eT", [128, 2, C], BF16))
            Sf = ae(nc.sbuf_tensor("Sf", [128, 512], F32))
            Sbf = ae(nc.sbuf_tensor("Sbf", [128, 512], BF16))
            sb_o = ae(nc.sbuf_tensor("sb_o", [128, E], BF16))
            x1b = ae(nc.sbuf_tensor("x1b", [128, E], F32))
            dump = ae(nc.sbuf_tensor("dump", [128, E], BF16))
            og = ae(nc.sbuf_tensor("og", [128, 2, E], F32))
            scol = ae(nc.sbuf_tensor("scol", [128, 32], F32))
            scol2 = ae(nc.sbuf_tensor("scol2", [128, 32], F32))
            pbig0 = ae(nc.psum_tensor("pbig0", [128, 1024], F32))
            pbig1 = ae(nc.psum_tensor("pbig1", [128, 1024], F32))
            ptr = ae(nc.psum_tensor("ptr", [128, KT, C], BF16))
            pS = ae(nc.psum_tensor("pS", [128, 512], F32))
            pm6 = ae(nc.psum_tensor("pm6", [128, 512], F32))
            pm7 = ae(nc.psum_tensor("pm7", [128, 512], F32))

            s = Sched(nc, actx, "a")
            s.self_chain = self_chain
            T, V, S, GP, SY = "tensor", "vector", "scalar", "gpsimd", "sync"

            t_wq = s.emit(GP, lambda e: e.dma_start(
                Wq_bf[:, :, :], Wq[:, :].rearrange("(k p) n -> p k n", p=128)), dma="dw")
            t_wk = s.emit(GP, lambda e: e.dma_start(
                Wk_bf[:, :, :], Wk[:, :].rearrange("(k p) n -> p k n", p=128)), dma="dw")
            t_wv = s.emit(GP, lambda e: e.dma_start(
                Wv_bf[:, :, :], Wv[:, :].rearrange("(k p) n -> p k n", p=128)), dma="dw")
            t_wo = s.emit(GP, lambda e: e.dma_start(
                Wo_bf[:, :, :], Wo[:, :].rearrange("(k p) n -> p k n", p=128)), dma="dw")
            t_wg1 = s.emit(GP, lambda e: e.dma_start(
                Wg1_bf[:, :, :], Wg1[:, :].rearrange("(k p) n -> p k n", p=128)), dma="dw")
            t_wg2 = s.emit(GP, lambda e: e.dma_start(Wg2_bf[:, :], Wg2[:, :]), dma="dw")
            t_idb = s.emit(GP, lambda e: e.dma_start(identbf[:, :], ident_in[:, :]), dma="dw")
            tok_wall = ("dw", s.cnt["dw"])
            t_wq = t_wk = t_wv = t_wo = t_wg1 = t_wg2 = t_idb = tok_wall
            t_gam = s.emit(SY, lambda e: e.dma_start(gamma_sb[:, :], gamma_fm[:, :]), dma="dmisc")
            t_bet = s.emit(SY, lambda e: e.dma_start(beta_sb[:, :], beta_fm[:, :]), dma="dmisc")
            t_tri = s.emit(SY, lambda e: e.dma_start(trium[:, :], triu_in[:, :]), dma="dmisc")
            tok_misc = ("dmisc", s.cnt["dmisc"])
            t_gam = t_bet = t_tri = tok_misc
            t_bbf = s.emit(V, lambda e: e.tensor_copy(beta_bf[:, :], beta_sb[:, :]),
                           after=[t_bet])
            t_ones = s.emit(V, lambda e: e.memset(ones_row[:, :], 1.0))
            t_sf0 = s.emit(V, lambda e: e.memset(Sf[:, :], 0.0))
            t_sbf0 = s.emit(V, lambda e: e.memset(Sbf[:, :], 0.0), after=[t_sf0])
            t_bwo_mm = None
            t_bwo_ev = None
            for half in range(2):
                for k in range(KT):
                    def f(e, half=half, k=k):
                        return e.matmul(pm7[0:1, 0:512],
                                        beta_bf[:, k:k + 1],
                                        Wo_bf[:, k, half * 512:(half + 1) * 512],
                                        start=(k == 0), stop=(k == KT - 1))
                    aft = []
                    if k == 0:
                        aft = [t_wo, t_bbf]
                        if t_bwo_ev is not None:
                            aft.append(t_bwo_ev)
                    t_bwo_mm = s.emit(T, f, after=aft)
                t_bwo_ev = s.emit(S, lambda e, half=half: e.copy(
                    bWo_row[0:1, half * 512:(half + 1) * 512], pm7[0:1, 0:512]),
                    after=[t_bwo_mm])
            t_wog = []
            for k in range(KT):
                t_wog.append(s.emit(V, lambda e, k=k: e.tensor_scalar(
                    Wo_bf[:, k, :], Wo_bf[:, k, :], gamma_sb[:, k:k + 1], None,
                    ALU.mult), after=[t_bwo_ev, t_gam, t_wo]))

            tok_x = [None] * (NCH + 2)
            tok_x[0] = s.emit(SY, lambda e: e.dma_start(xb[:, 0, :], x[0:C, :]), dma="dx0")
            if n_chunks > 1:
                tok_x[1] = s.emit(SY, lambda e: e.dma_start(
                    xb[:, 1, :], x[C:2 * C, :]), dma="dx1")
            t_cos = s.emit(SY, lambda e: e.dma_start(cosb[:, :], cosr[0:C, :]), dma="dtab")
            t_sin = s.emit(SY, lambda e: e.dma_start(sinb[:, :], sinr[0:C, :]), dma="dtab")
            t_cos = t_sin = ("dtab", s.cnt["dtab"])

            prev = {
                "x1_store": None,
                "t_store": None,
                "ptr_free": t_idb,
                "pbig0_free": t_bwo_ev,
                "pbig1_free": t_bwo_ev,
                "Sbf_ready": t_sbf0,
                "Sf_ready": t_sf0,
                "pS_free": t_sf0,
                "rope_done_v": t_ones,
                "rope_done_g": t_idb,
                "x_free": [t_bbf],
                "tT_slot": None,
            }

            MAGIC = float(0x5F3759DF)

            def emit_rsqrt(colbuf, c_v, c_out, after):
                ci = colbuf[:, :].bitcast(I32)
                t = s.emit(V, lambda e: e.tensor_scalar(
                    ci[:, 20:21], ci[:, c_v:c_v + 1], 1, None,
                    ALU.logical_shift_right), after=after)
                s.emit(V, lambda e: e.tensor_copy(colbuf[:, 21:22], ci[:, 20:21]))
                s.emit(V, lambda e: e.tensor_scalar(
                    colbuf[:, 22:23], colbuf[:, 21:22], -1.0, MAGIC,
                    ALU.mult, ALU.add))
                s.emit(V, lambda e: e.tensor_copy(ci[:, 23:24], colbuf[:, 22:23]))
                s.emit(V, lambda e: e.tensor_copy(
                    colbuf[:, c_out:c_out + 1], ci[:, 23:24].bitcast(F32)[:, 0:1]))
                for _ in range(3):
                    s.emit(V, lambda e: e.tensor_tensor(
                        colbuf[:, 20:21], colbuf[:, c_out:c_out + 1],
                        colbuf[:, c_out:c_out + 1], ALU.mult))
                    s.emit(V, lambda e: e.tensor_tensor(
                        colbuf[:, 21:22], colbuf[:, c_v:c_v + 1],
                        colbuf[:, 20:21], ALU.mult))
                    s.emit(V, lambda e: e.tensor_scalar(
                        colbuf[:, 22:23], colbuf[:, 21:22], -0.5, 1.5,
                        ALU.mult, ALU.add))
                    t = s.emit(V, lambda e: e.tensor_tensor(
                        colbuf[:, c_out:c_out + 1], colbuf[:, c_out:c_out + 1],
                        colbuf[:, 22:23], ALU.mult))
                return t

            def emit_ln_stats(colbuf, src_ap, after_tok, psum_src=False,
                              sq_eng="S"):
                c_sx, c_sq, c_m, c_a, c_msq, c_b, c_rs = 0, 1, 2, 3, 4, 5, 6
                t_sx = s.emit(V, lambda e: e.reduce_sum(
                    colbuf[:, c_sx:c_sx + 1], src_ap, axis=AX.X), after=after_tok)
                if sq_eng == "S":
                    t_sq = s.emit(S, lambda e: e.activation(
                        dump[:, :], src_ap, AF.Square,
                        accum_out=colbuf[:, c_sq:c_sq + 1]),
                        after=after_tok + ([t_sx] if psum_src else []))
                else:
                    t_sq = s.emit(V, lambda e: e.scalar_tensor_tensor(
                        dump[:, :], src_ap, 1.0, src_ap, ALU.mult, ALU.mult,
                        accum_out=colbuf[:, c_sq:c_sq + 1]), after=after_tok)
                t_m = s.emit(V, lambda e: e.tensor_scalar(
                    colbuf[:, c_m:c_m + 1], colbuf[:, c_sx:c_sx + 1], 1.0 / E,
                    None, ALU.mult), after=[t_sx])
                t_a = s.emit(V, lambda e: e.tensor_scalar(
                    colbuf[:, c_a:c_a + 1], colbuf[:, c_sq:c_sq + 1], 1.0 / E,
                    EPS, ALU.mult, ALU.add), after=[t_sq])
                t_msq = s.emit(V, lambda e: e.tensor_tensor(
                    colbuf[:, c_msq:c_msq + 1], colbuf[:, c_m:c_m + 1],
                    colbuf[:, c_m:c_m + 1], ALU.mult), after=[t_m])
                t_b = s.emit(V, lambda e: e.tensor_tensor(
                    colbuf[:, c_b:c_b + 1], colbuf[:, c_a:c_a + 1],
                    colbuf[:, c_msq:c_msq + 1], ALU.subtract),
                    after=[t_a, t_msq])
                t_rs = emit_rsqrt(colbuf, c_b, c_rs, [t_b])
                return ([t_rs], colbuf[:, c_rs:c_rs + 1],
                        colbuf[:, c_m:c_m + 1])

            def emit_nmrs(colbuf, toks):
                c_m, c_rs, c_mrs, c_nmrs = 2, 6, 7, 8
                s.emit(V, lambda e: e.tensor_tensor(
                    colbuf[:, c_mrs:c_mrs + 1], colbuf[:, c_m:c_m + 1],
                    colbuf[:, c_rs:c_rs + 1], ALU.mult), after=toks)
                t = s.emit(V, lambda e: e.tensor_scalar(
                    colbuf[:, c_nmrs:c_nmrs + 1], colbuf[:, c_mrs:c_mrs + 1],
                    -1.0, None, ALU.mult))
                return ([t], colbuf[:, c_rs:c_rs + 1],
                        colbuf[:, c_nmrs:c_nmrs + 1])

            cos_v = cosb[:, :].rearrange("p (h d) -> p h d", h=H)
            sin_v = sinb[:, :].rearrange("p (h d) -> p h d", h=H)

            def emit_rope(eng, src, scr2, dst, after_toks):
                sv = src[:, :].rearrange("p (h d) -> p h d", h=H)
                dv = dst[:, :].rearrange("p (h d) -> p h d", h=H)
                s2 = scr2[:, :].rearrange("p (h d) -> p h d", h=H)
                x1v, x2v = sv[:, :, 0:32], sv[:, :, 32:64]
                d1, d2 = dv[:, :, 0:32], dv[:, :, 32:64]
                s.emit(eng, lambda e: e.tensor_tensor(
                    s2[:, :, :], x2v, sin_v, ALU.mult), after=after_toks)
                s.emit(eng, lambda e: e.tensor_tensor(
                    d1, x1v, cos_v, ALU.mult))
                s.emit(eng, lambda e: e.tensor_tensor(
                    d1, d1, s2[:, :, :], ALU.subtract))
                s.emit(eng, lambda e: e.tensor_tensor(
                    s2[:, :, :], x2v, cos_v, ALU.mult))
                s.emit(eng, lambda e: e.tensor_tensor(
                    d2, x1v, sin_v, ALU.mult))
                t6 = s.emit(eng, lambda e: e.tensor_tensor(
                    d2, d2, s2[:, :, :], ALU.add))
                return t6

            prev.update({
                "scol_rd": None, "scol2_rd": None, "ln1b_rd": None,
                "ln1T_rd": None, "qs_rd": None, "ks_rd": None,
                "qro_rd": None, "kro_rd": [], "vb_rd": [],
                "qT_rd": [], "kT_rd": [], "th_rd": None,
                "sb_o_rd": None, "outT_rd": None, "og_rd2": [None, None],
                "t_mask": [None, None],
            })

            def emit_front(c):
                f = {}
                sl = c % 2
                if c > 0:
                    s.emit(SY, lambda e, c=c: e.dma_start(
                        cosb[:, :], cosr[c * C:(c + 1) * C, :]),
                        after=[prev["rope_done_v"], prev["rope_done_g"]],
                        dma="dtab")
                    s.emit(SY, lambda e, c=c: e.dma_start(
                        sinb[:, :], sinr[c * C:(c + 1) * C, :]),
                        after=[prev["rope_done_v"], prev["rope_done_g"]],
                        dma="dtab")
                    f["tab"] = ("dtab", s.cnt["dtab"])
                else:
                    f["tab"] = t_cos
                xc = xb[:, sl, :]
                f["xc"] = xc
                f["c"] = c
                ln_toks, rs_ap, m_ap = emit_ln_stats(
                    scol, xc, [tok_x[c]] +
                    ([prev["scol_rd"]] if prev["scol_rd"] else []))
                ln_toks, rs_ap, nmrs_ap = emit_nmrs(scol, ln_toks)
                t_ln1 = s.emit(S, lambda e, rs_ap=rs_ap, nmrs_ap=nmrs_ap, xc=xc:
                               e.activation(ln1b[:, :], xc, AF.Identity,
                                            bias=nmrs_ap, scale=rs_ap),
                               after=ln_toks +
                               ([prev["ln1b_rd"]] if prev["ln1b_rd"] else []))
                prev["scol_rd"] = t_ln1
                t_tr = None
                for j in range(KT):
                    t_tr = s.emit(T, lambda e, j=j: e.transpose(
                        ptr[:, j, :], ln1b[:, j * 128:(j + 1) * 128],
                        identbf[:, :]),
                        after=[t_ln1, prev["ptr_free"], t_idb] if j == 0 else [])
                prev["ln1b_rd"] = t_tr
                t_ln1T = s.emit(S, lambda e: e.copy(
                    ln1T[:, :, :], ptr[:, :, :]),
                    after=[t_tr] +
                    ([prev["ln1T_rd"]] if prev["ln1T_rd"] else []))
                prev["ptr_free"] = t_ln1T
                t_qmm = None
                for nh in range(2):
                    for k in range(KT):
                        def fq(e, nh=nh, k=k):
                            return e.matmul(pbig0[:, nh * 512:(nh + 1) * 512],
                                            ln1T[:, k, :],
                                            Wq_bf[:, k, nh * 512:(nh + 1) * 512],
                                            start=(k == 0), stop=(k == KT - 1))
                        t_qmm = s.emit(T, fq,
                                       after=[t_ln1T, prev["pbig0_free"], t_wq]
                                       if (nh == 0 and k == 0) else [])
                t_kmm = None
                for nh in range(2):
                    for k in range(KT):
                        def fk(e, nh=nh, k=k):
                            return e.matmul(pbig1[:, nh * 512:(nh + 1) * 512],
                                            ln1T[:, k, :],
                                            Wk_bf[:, k, nh * 512:(nh + 1) * 512],
                                            start=(k == 0), stop=(k == KT - 1))
                        t_kmm = s.emit(T, fk,
                                       after=[t_ln1T, prev["pbig1_free"], t_wk]
                                       if (nh == 0 and k == 0) else [])
                t_sq_ = s.emit(S, lambda e: e.activation(qs[:, :], pbig0[:, :],
                                                         AF.Silu),
                               after=[t_qmm] +
                               ([prev["qs_rd"]] if prev["qs_rd"] else []))
                t_sk_ = s.emit(S, lambda e: e.activation(ks[:, :], pbig1[:, :],
                                                         AF.Silu),
                               after=[t_kmm] +
                               ([prev["ks_rd"]] if prev["ks_rd"] else []))
                t_vmm = None
                for nh in range(2):
                    for k in range(KT):
                        def fv(e, nh=nh, k=k):
                            return e.matmul(pbig0[:, nh * 512:(nh + 1) * 512],
                                            ln1T[:, k, :],
                                            Wv_bf[:, k, nh * 512:(nh + 1) * 512],
                                            start=(k == 0), stop=(k == KT - 1))
                        t_vmm = s.emit(T, fv, after=[t_sq_, t_wv]
                                       if (nh == 0 and k == 0) else [])
                t_g1 = None
                for k in range(KT):
                    def fg(e, k=k):
                        return e.matmul(pm6[0:64, 0:128], Wg1_bf[:, k, :],
                                        ln1T[:, k, :],
                                        start=(k == 0), stop=(k == KT - 1))
                    aft = []
                    if k == 0:
                        aft = [t_ln1T, t_wg1]
                        if prev["t_mask"][0] is not None:
                            aft.append(prev["t_mask"][0])
                    t_g1 = s.emit(T, fg, after=aft)
                prev["ln1T_rd"] = t_g1
                t_g1e = s.emit(S, lambda e: e.copy(g1b[:, :], pm6[0:64, 0:128]),
                               after=[t_g1])
                t_g2 = None
                for nh in range(2):
                    def fg2(e, nh=nh):
                        return e.matmul(pbig1[:, nh * 512:(nh + 1) * 512],
                                        g1b[:, :],
                                        Wg2_bf[:, nh * 512:(nh + 1) * 512],
                                        start=True, stop=True)
                    t_g2 = s.emit(T, fg2, after=[t_sk_, t_g1e, t_wg2]
                                  if nh == 0 else [])
                t_th = s.emit(S, lambda e: e.activation(th[:, :], pbig1[:, :],
                                                        AF.Tanh, scale=0.5),
                              after=[t_g2] +
                              ([prev["th_rd"]] if prev["th_rd"] else []))
                t_vb = s.emit(V, lambda e: e.tensor_copy(vb[:, :], pbig0[:, :]),
                              after=[t_vmm] + prev["vb_rd"])
                f["t_vb"] = t_vb
                t_ropeq = emit_rope(GP, qs, rt2, qro,
                                    [f["tab"], t_sq_] +
                                    ([prev["qro_rd"]] if prev["qro_rd"] else []))
                t_ropek = emit_rope(GP, ks, kt2, kro,
                                    [f["tab"], t_sk_] + prev["kro_rd"])
                prev["rope_done_v"] = t_ropeq
                prev["rope_done_g"] = t_ropek
                prev["qs_rd"] = t_ropeq
                prev["ks_rd"] = t_ropek
                t_gate = s.emit(V, lambda e: e.tensor_scalar(
                    gateb[:, :], th[:, :], 0.5, 0.5, ALU.mult, ALU.add),
                    after=[t_th])
                prev["th_rd"] = t_gate
                t_trq = None
                for j in range(KT):
                    t_trq = s.emit(T, lambda e, j=j: e.transpose(
                        ptr[:, j, :], qro[:, j * 128:(j + 1) * 128], identbf[:, :]),
                        after=[t_ropeq, prev["ptr_free"]] if j == 0 else [])
                prev["qro_rd"] = t_trq
                t_qT = s.emit(S, lambda e: e.copy(qT[:, :, :], ptr[:, :, :]),
                              after=[t_trq] + prev["qT_rd"])
                prev["ptr_free"] = t_qT
                t_trk = None
                for j in range(KT):
                    t_trk = s.emit(T, lambda e, j=j: e.transpose(
                        ptr[:, j, :], kro[:, j * 128:(j + 1) * 128], identbf[:, :]),
                        after=[t_ropek, prev["ptr_free"]] if j == 0 else [])
                t_kT = s.emit(S, lambda e: e.copy(kTt[:, :, :], ptr[:, :, :]),
                              after=[t_trk] + prev["kT_rd"])
                prev["ptr_free"] = t_kT
                t_mask = prev["t_mask"]
                t_intra_last = None
                t_inter_last = None
                t_state_last = None
                for h in range(H):
                    po = 64 * (h % 2)
                    tile = h // 2
                    slot = h % 2
                    pe_slot = (pm6, pm7)[slot]
                    t_en = s.emit(T, lambda e, po=po, tile=tile, pe_slot=pe_slot:
                                  e.matmul(pe_slot[:, 128:256],
                                           kTt[po:po + 64, tile, :],
                                           qT[po:po + 64, tile, :],
                                           start=True, stop=True),
                                  after=([t_qT, t_kT, t_th] if h == 0 else []) +
                                        ([t_mask[slot]] if t_mask[slot] else []))
                    t_mask[slot] = s.emit(V, lambda e, slot=slot, pe_slot=pe_slot:
                                          e.tensor_tensor(
                                              eT[:, slot, :],
                                              pe_slot[:, 128:256],
                                              trium[:, :], ALU.mult),
                                          after=[t_en, t_tri])
                    t_intra_last = s.emit(T, lambda e, h=h, slot=slot: e.matmul(
                        pbig1[:, h * 64:(h + 1) * 64], eT[:, slot, :],
                        vb[:, h * 64:(h + 1) * 64], start=True, stop=False),
                        after=[t_mask[slot], t_vb] + ([t_th] if h == 0 else []))
                    t_inter_last = s.emit(T, lambda e, h=h, po=po, tile=tile:
                                          e.matmul(
                        pbig1[:, h * 64:(h + 1) * 64],
                        qT[po:po + 64, tile, :],
                        Sbf[po:po + 64, (h // 2) * 64:(h // 2) * 64 + 64],
                        start=False, stop=True),
                        after=[prev["Sbf_ready"]] if h == 0 else [])
                    t_state_last = s.emit(T, lambda e, h=h, po=po: e.matmul(
                        pS[po:po + 64, (h // 2) * 64:(h // 2) * 64 + 64],
                        kro[:, h * 64:(h + 1) * 64],
                        vb[:, h * 64:(h + 1) * 64], start=True, stop=True),
                        after=[t_ropek, t_vb, prev["pS_free"]] if h == 0 else [])
                prev["kro_rd"] = [t_trk, t_state_last]
                prev["vb_rd"] = [t_intra_last, t_state_last]
                prev["kT_rd"] = [t_inter_last]
                t_sfu = s.emit(V, lambda e: e.tensor_tensor(
                    Sf[:, :], Sf[:, :], pS[:, :], ALU.add),
                    after=[t_state_last, prev["Sf_ready"]])
                prev["pS_free"] = t_sfu
                prev["Sf_ready"] = t_sfu
                t_sbf = s.emit(V, lambda e: e.tensor_copy(Sbf[:, :], Sf[:, :]),
                               after=[t_sfu, t_inter_last])
                prev["Sbf_ready"] = t_sbf
                t_og = s.emit(V, lambda e, sl=sl: e.tensor_tensor(
                    og[:, sl, :], pbig1[:, :], gateb[:, :], ALU.mult),
                    after=[t_intra_last, t_inter_last, t_gate] +
                          ([prev["og_rd2"][sl]] if prev["og_rd2"][sl] else []))
                prev["pbig1_free"] = t_og
                f["t_og"] = t_og
                f["t_inter_last"] = t_inter_last
                f["t_vb"] = t_vb
                f["t_th"] = t_th
                return f

            def emit_back(c, f):
                xc = f["xc"]
                sl = c % 2
                ogs = og[:, sl, :]
                ln_toks, rs_ap, m_ap = emit_ln_stats(
                    scol2, ogs, [f["t_og"]] +
                    ([prev["scol2_rd"]] if prev["scol2_rd"] else []),
                    sq_eng="V")
                t_oln = s.emit(V, lambda e, rs_ap=rs_ap, m_ap=m_ap, ogs=ogs:
                               e.tensor_scalar(sb_o[:, :], ogs, m_ap, rs_ap,
                                               ALU.subtract, ALU.mult),
                               after=ln_toks +
                               ([prev["sb_o_rd"]] if prev["sb_o_rd"] else []) +
                               ([prev["t_store"]] if prev["t_store"] else []))
                prev["og_rd2"][sl] = t_oln
                t_tro = None
                for j in range(KT):
                    t_tro = s.emit(T, lambda e, j=j: e.transpose(
                        ptr[:, j, :], sb_o[:, j * 128:(j + 1) * 128], identbf[:, :]),
                        after=[t_oln, prev["ptr_free"]] if j == 0 else [])
                t_oT = s.emit(S, lambda e: e.copy(qT[:, :, :], ptr[:, :, :]),
                              after=[t_tro])
                prev["ptr_free"] = t_oT
                t_ao = None
                for nh in range(2):
                    for k in range(KT):
                        def fa(e, nh=nh, k=k):
                            return e.matmul(pbig0[:, nh * 512:(nh + 1) * 512],
                                            qT[:, k, :],
                                            Wo_bf[:, k, nh * 512:(nh + 1) * 512],
                                            start=(k == 0), stop=False)
                        t_ao = s.emit(T, fa,
                                      after=[t_oT, f["t_vb"]] + t_wog
                                      if (nh == 0 and k == 0) else [])
                    def fb(e, nh=nh):
                        return e.matmul(pbig0[:, nh * 512:(nh + 1) * 512],
                                        ones_row[:, :],
                                        bWo_row[0:1, nh * 512:(nh + 1) * 512],
                                        start=False, stop=True)
                    t_ao = s.emit(T, fb, after=[t_ones] if nh == 0 else [])
                prev["qT_rd"] = [t_ao]
                ln_toks, rs_ap, m_ap = emit_ln_stats(scol2, pbig0[:, :],
                                                     [t_ao], psum_src=True)
                t_aoln = s.emit(V, lambda e, rs_ap=rs_ap, m_ap=m_ap:
                                e.tensor_scalar(sb_o[:, :], pbig0[:, :],
                                                m_ap, rs_ap,
                                                ALU.subtract, ALU.mult),
                                after=ln_toks + [t_oT])
                prev["pbig0_free"] = t_aoln
                t_x1 = s.emit(V, lambda e, xc=xc: e.tensor_tensor(
                    x1b[:, :], sb_o[:, :], xc, ALU.add),
                    after=[t_aoln] +
                    ([prev["x1_store"]] if prev["x1_store"] else []))
                t_x1d = s.emit(SY, lambda e, c=c: e.dma_start(
                    x1s[c * C:(c + 1) * C, :], x1b[:, :]), after=[t_x1],
                    dma="dx1s")
                prev["x1_store"] = t_x1d
                if c + 2 < n_chunks:
                    tok_x[c + 2] = s.emit(SY, lambda e, c=c: e.dma_start(
                        xb[:, c % 2, :], x[(c + 2) * C:(c + 3) * C, :]),
                        after=[t_x1], dma="dx0" if (c + 2) % 2 == 0 else "dx1")
                ln_toks, rs_ap, m_ap = emit_ln_stats(scol2, x1b[:, :],
                                                     [t_x1], sq_eng="V")
                t_tb = s.emit(V, lambda e, rs_ap=rs_ap, m_ap=m_ap:
                              e.tensor_scalar(sb_o[:, :], x1b[:, :],
                                              m_ap, rs_ap,
                                              ALU.subtract, ALU.mult),
                              after=ln_toks + [t_x1])
                prev["scol2_rd"] = t_tb
                t_tbd = s.emit(SY, lambda e, c=c: e.dma_start(
                    ts_[c * C:(c + 1) * C, :], sb_o[:, :]), after=[t_tb],
                    dma="dts")
                prev["t_store"] = t_tbd
                t_trt = None
                for j in range(KT):
                    t_trt = s.emit(T, lambda e, j=j: e.transpose(
                        ptr[:, j, :], sb_o[:, j * 128:(j + 1) * 128], identbf[:, :]),
                        after=[t_tb, prev["ptr_free"]] if j == 0 else [])
                prev["sb_o_rd"] = t_trt
                t_tT = s.emit(S, lambda e, c=c: e.copy(
                    tT[:, :, c * C:(c + 1) * C], ptr[:, :, :]), after=[t_trt])
                prev["ptr_free"] = t_tT
                prev["outT_rd"] = t_ao
                prev["tT_slot"] = t_tT

            fstate = {}
            for i in range(n_chunks + 1):
                if i < n_chunks:
                    fstate[i] = emit_front(i)
                if i >= 1:
                    emit_back(i - 1, fstate.pop(i - 1))

            a_final = []
            if prev["tT_slot"] is not None:
                a_final.append(prev["tT_slot"])
            for nm in ("dx1s", "dts"):
                if s.cnt.get(nm, 0) > 0:
                    a_final.append((nm, s.cnt[nm]))
            with nc.Block() as blockA:
                s.flush(blockA, final_tokens=a_final)

        if "B" not in phases:
            return nc
        with ExitStack() as bctx:
            be = bctx.enter_context
            rw1_bf = be(nc.sbuf_tensor("rw1_bf", [128, KT, E], BF16))
            rw2_bf = be(nc.sbuf_tensor("rw2_bf", [128, KT, NEXP], BF16))
            rb1_sb = be(nc.sbuf_tensor("rb1_sb", [128, KT], F32))
            rb2_sb = be(nc.sbuf_tensor("rb2_sb", [128, NEXP], F32))
            r1b = be(nc.sbuf_tensor("r1b", [128, KT, 512], BF16))
            l_sb = be(nc.sbuf_tensor("l_sb", [128, NCH, NEXP], F32))
            tpw = be(nc.sbuf_tensor("tpw", [128, 6 * NEXP], F32))
            wbf = be(nc.sbuf_tensor("wbf", [128, NEXP], BF16))
            pr0 = be(nc.psum_tensor("pr0", [128, 512], F32))
            pr1 = be(nc.psum_tensor("pr1", [128, 512], F32))
            pl = be(nc.psum_tensor("pl", [128, NEXP], F32))
            pwt = be(nc.psum_tensor("pwt", [NEXP, 128], BF16))

            s2 = Sched(nc, bctx, "b")
            s2.self_chain = self_chain
            T, V, S, GP, SY = "tensor", "vector", "scalar", "gpsimd", "sync"
            t_rw1 = s2.emit(GP, lambda e: e.dma_start(
                rw1_bf[:, :, :], rw1[:, :].rearrange("(k p) n -> p k n", p=128)),
                dma="dw")
            t_rw2 = s2.emit(GP, lambda e: e.dma_start(
                rw2_bf[:, :, :], rw2[:, :].rearrange("(k p) n -> p k n", p=128)),
                dma="dw")
            t_rw1 = t_rw2 = ("dw", s2.cnt["dw"])
            t_rb1 = s2.emit(SY, lambda e: e.dma_start(rb1_sb[:, :], rb1_fm[:, :]),
                            dma="dmisc")
            t_rb2 = s2.emit(SY, lambda e: e.dma_start(rb2_sb[:, :], rb2_b[:, :]),
                            dma="dmisc")
            t_rb1 = t_rb2 = ("dmisc", s2.cnt["dmisc"])
            prfree = [None, None]
            t_lev = [None] * NCH
            for t8 in range(NTOK // 512):
                t_r1 = [None] * KT
                for m in range(KT):
                    pr = (pr0, pr1)[m % 2]
                    t_mm = None
                    for k in range(KT):
                        def f(e, m=m, k=k, t8=t8, pr=pr):
                            return e.matmul(pr[:, :], rw1_bf[:, k, m * 128:(m + 1) * 128],
                                            tT[:, k, t8 * 512:(t8 + 1) * 512],
                                            start=(k == 0), stop=(k == KT - 1))
                        aft = []
                        if k == 0:
                            aft = [t_rw1]
                            if prfree[m % 2] is not None:
                                aft.append(prfree[m % 2])
                        t_mm = s2.emit(T, f, after=aft)
                    t_r1[m] = s2.emit(S, lambda e, m=m, pr=pr: e.activation(
                        r1b[:, m, :], pr[:, :], AF.Silu,
                        bias=rb1_sb[:, m:m + 1]), after=[t_mm, t_rb1])
                    prfree[m % 2] = t_r1[m]
                for sub in range(4):
                    ci = t8 * 4 + sub
                    t_lmm = None
                    for k in range(KT):
                        def f(e, k=k, sub=sub):
                            return e.matmul(pl[:, :], r1b[:, k, sub * 128:(sub + 1) * 128],
                                            rw2_bf[:, k, :],
                                            start=(k == 0), stop=(k == KT - 1))
                        t_lmm = s2.emit(T, f,
                                        after=[t_r1[KT - 1], t_rw2] + ([t_lev[ci - 1]] if ci > 0 else [])
                                        if k == 0 else [])
                    t_lev[ci] = s2.emit(V, lambda e, ci=ci: e.tensor_tensor(
                        l_sb[:, ci, :], pl[:, :], rb2_sb[:, :], ALU.add),
                        after=[t_lmm, t_rb2])
            cm1, cnm1, cm2, cZ, crz = 0, 1, 2, 3, 4
            OFF = NEXP
            t_wT_ev = None
            t_wsb = [None] * NCH
            for ci in range(NCH):
                lap = l_sb[:, ci, :]
                t_m1 = s2.emit(V, lambda e, lap=lap: e.reduce_max(
                    tpw[:, cm1:cm1 + 1], lap, axis=AX.X), after=[t_lev[ci]])
                t_nm1 = s2.emit(V, lambda e: e.tensor_scalar(
                    tpw[:, cnm1:cnm1 + 1], tpw[:, cm1:cm1 + 1], -1.0, None,
                    ALU.mult), after=[t_m1])
                t_msk1 = s2.emit(V, lambda e, lap=lap: e.tensor_scalar(
                    tpw[:, OFF:OFF + NEXP], lap, tpw[:, cm1:cm1 + 1], None,
                    ALU.is_ge), after=[t_m1])
                t_masked = s2.emit(V, lambda e, lap=lap: e.scalar_tensor_tensor(
                    tpw[:, 2 * OFF:2 * OFF + NEXP], tpw[:, OFF:OFF + NEXP], -BIG,
                    lap, ALU.mult, ALU.add), after=[t_msk1])
                t_m2 = s2.emit(V, lambda e: e.reduce_max(
                    tpw[:, cm2:cm2 + 1], tpw[:, 2 * OFF:2 * OFF + NEXP],
                    axis=AX.X), after=[t_masked])
                t_sel = s2.emit(V, lambda e, lap=lap: e.tensor_scalar(
                    tpw[:, 3 * OFF:3 * OFF + NEXP], lap, tpw[:, cm2:cm2 + 1],
                    None, ALU.is_ge), after=[t_m2])
                t_e = s2.emit(S, lambda e, lap=lap: e.activation(
                    tpw[:, 4 * OFF:4 * OFF + NEXP], lap, AF.Exp,
                    bias=tpw[:, cnm1:cnm1 + 1]), after=[t_nm1, t_lev[ci]])
                t_es = s2.emit(V, lambda e: e.tensor_tensor(
                    tpw[:, 5 * OFF:5 * OFF + NEXP], tpw[:, 4 * OFF:4 * OFF + NEXP],
                    tpw[:, 3 * OFF:3 * OFF + NEXP], ALU.mult),
                    after=[t_e, t_sel])
                t_Z = s2.emit(V, lambda e: e.reduce_sum(
                    tpw[:, cZ:cZ + 1], tpw[:, 5 * OFF:5 * OFF + NEXP],
                    axis=AX.X), after=[t_es])
                t_rz = s2.emit(V, lambda e: e.reciprocal(
                    tpw[:, crz:crz + 1], tpw[:, cZ:cZ + 1]), after=[t_Z])
                t_wsb[ci] = s2.emit(V, lambda e, ci=ci: e.tensor_scalar(
                    wsb[:, ci, :], tpw[:, 5 * OFF:5 * OFF + NEXP],
                    tpw[:, crz:crz + 1], None, ALU.mult), after=[t_rz])
                t_wbf = s2.emit(V, lambda e, ci=ci: e.tensor_copy(
                    wbf[:, :], wsb[:, ci, :]), after=[t_wsb[ci], t_wT_ev]
                    if t_wT_ev else [t_wsb[ci]])
                t_wT = s2.emit(T, lambda e: e.transpose(
                    pwt[:, :], wbf[:, :], identbf[:, :]),
                    after=[t_wbf] + ([t_wT_ev] if t_wT_ev else []))
                t_wT_ev = s2.emit(S, lambda e, ci=ci: e.copy(
                    wTsb[:, ci * C:(ci + 1) * C], pwt[:, :]), after=[t_wT])
            if debug:
                t_wdump = s2.emit(SY, lambda e: e.dma_start(
                    dbg_w[:, :, :].rearrange("c p e -> p c e"), wsb[:, :, :]),
                    after=t_wsb, dma="ddbg")
            b_final = [t_wT_ev] + t_wsb + (
                [("ddbg", s2.cnt.get("ddbg", 0))] if debug else [])
            with nc.Block() as blockB:
                s2.flush(blockB, final_tokens=b_final)

        if "C" not in phases:
            return nc
        with ExitStack() as cctx:
            ce = cctx.enter_context
            ew1_bf = ce(nc.sbuf_tensor("ew1_bf", [128, KT, DH], BF16))
            ew2_bf = ce(nc.sbuf_tensor("ew2_bf", [128, KT2, E], BF16))
            ehb = ce(nc.sbuf_tensor("ehb", [128, KT2, 512], BF16))
            yac = ce(nc.sbuf_tensor("yac", [128, CPG, E], BF16))
            eb1_sb = ce(nc.sbuf_tensor("eb1_sb", [128, NEXP, KT2], F32))
            eb2_bf = ce(nc.sbuf_tensor("eb2_bf", [NEXP, E], BF16))
            x1c = ce(nc.sbuf_tensor("x1c", [128, E], F32))
            tc = ce(nc.sbuf_tensor("tc", [128, E], BF16))
            yf = ce(nc.sbuf_tensor("yf", [128, E], F32))
            ylnb = ce(nc.sbuf_tensor("ylnb", [128, E], BF16))
            sc3 = ce(nc.sbuf_tensor("sc3", [128, 16], F32))
            peh0 = ce(nc.psum_tensor("peh0", [128, 512], F32))
            peh1 = ce(nc.psum_tensor("peh1", [128, 512], F32))
            peo0 = ce(nc.psum_tensor("peo0", [128, 1024], F32))
            peo1 = ce(nc.psum_tensor("peo1", [128, 1024], F32))

            s3 = Sched(nc, cctx, "c")
            s3.self_chain = self_chain
            T, V, S, GP, SY = "tensor", "vector", "scalar", "gpsimd", "sync"

            t_eb1 = s3.emit(SY, lambda e: e.dma_start(eb1_sb[:, :, :],
                                                      eb1_fm[:, :, :]), dma="dmisc")
            t_eb2 = s3.emit(GP, lambda e: e.dma_start(eb2_bf[:, :], eb2[:, :]),
                            dma="dmiscg")
            t_eb1 = ("dmisc", s3.cnt["dmisc"])
            t_eb2 = ("dmiscg", s3.cnt["dmiscg"])

            def load_ew1(eidx, after):
                return s3.emit(GP, lambda e, eidx=eidx: e.dma_start(
                    ew1_bf[:, :, :],
                    ew1[eidx, :, :].rearrange("(k p) h -> p k h", p=128)),
                    after=after, dma="dw1")

            def load_ew2(eidx, after):
                return s3.emit(GP, lambda e, eidx=eidx: e.dma_start(
                    ew2_bf[:, :, :],
                    ew2[eidx, :, :].rearrange("(k p) n -> p k n", p=128)),
                    after=after, dma="dw2")

            t_ew1 = load_ew1(0, [])
            t_dbg_eh = None
            t_ew2 = None
            ew1_lastuse = None
            ew2_lastuse = None
            t_out_final = []

            MAGIC3 = float(0x5F3759DF)

            def emit_rsqrt3(colbuf, c_v, c_out, after):
                ci = colbuf[:, :].bitcast(I32)
                t = s3.emit(V, lambda e: e.tensor_scalar(
                    ci[:, 10:11], ci[:, c_v:c_v + 1], 1, None,
                    ALU.logical_shift_right), after=after)
                s3.emit(V, lambda e: e.tensor_copy(colbuf[:, 11:12], ci[:, 10:11]))
                s3.emit(V, lambda e: e.tensor_scalar(
                    colbuf[:, 12:13], colbuf[:, 11:12], -1.0, MAGIC3,
                    ALU.mult, ALU.add))
                s3.emit(V, lambda e: e.tensor_copy(ci[:, 13:14], colbuf[:, 12:13]))
                s3.emit(V, lambda e: e.tensor_copy(
                    colbuf[:, c_out:c_out + 1], ci[:, 13:14].bitcast(F32)[:, 0:1]))
                for _ in range(3):
                    s3.emit(V, lambda e: e.tensor_tensor(
                        colbuf[:, 10:11], colbuf[:, c_out:c_out + 1],
                        colbuf[:, c_out:c_out + 1], ALU.mult))
                    s3.emit(V, lambda e: e.tensor_tensor(
                        colbuf[:, 11:12], colbuf[:, c_v:c_v + 1],
                        colbuf[:, 10:11], ALU.mult))
                    s3.emit(V, lambda e: e.tensor_scalar(
                        colbuf[:, 12:13], colbuf[:, 11:12], -0.5, 1.5,
                        ALU.mult, ALU.add))
                    t = s3.emit(V, lambda e: e.tensor_tensor(
                        colbuf[:, c_out:c_out + 1], colbuf[:, c_out:c_out + 1],
                        colbuf[:, 12:13], ALU.mult))
                return t

            def emit_ln3(src_ap, after_tok):
                c_sx, c_sq, c_m, c_a, c_msq, c_b, c_rs = 0, 1, 2, 3, 4, 5, 6
                t_sx = s3.emit(V, lambda e: e.reduce_sum(
                    sc3[:, c_sx:c_sx + 1], src_ap, axis=AX.X), after=after_tok)
                t_sq = s3.emit(S, lambda e: e.activation(
                    tc[:, :], src_ap, AF.Square,
                    accum_out=sc3[:, c_sq:c_sq + 1]), after=after_tok)
                t_m = s3.emit(V, lambda e: e.tensor_scalar(
                    sc3[:, c_m:c_m + 1], sc3[:, c_sx:c_sx + 1], 1.0 / E, None,
                    ALU.mult), after=[t_sx])
                t_a = s3.emit(V, lambda e: e.tensor_scalar(
                    sc3[:, c_a:c_a + 1], sc3[:, c_sq:c_sq + 1], 1.0 / E, EPS,
                    ALU.mult, ALU.add), after=[t_sq])
                t_msq = s3.emit(V, lambda e: e.tensor_tensor(
                    sc3[:, c_msq:c_msq + 1], sc3[:, c_m:c_m + 1],
                    sc3[:, c_m:c_m + 1], ALU.mult), after=[t_m])
                t_b = s3.emit(V, lambda e: e.tensor_tensor(
                    sc3[:, c_b:c_b + 1], sc3[:, c_a:c_a + 1],
                    sc3[:, c_msq:c_msq + 1], ALU.subtract), after=[t_a, t_msq])
                t_rs = emit_rsqrt3(sc3, c_b, c_rs, [t_b])
                return ([t_rs], sc3[:, c_rs:c_rs + 1], sc3[:, c_m:c_m + 1])

            slot_guard = [None] * CPG
            peo_guard = [None, None]
            t_prev_out = None
            t_prev_od = None
            t_prev_dbg = None
            for g in range(NGRP):
                tok0 = g * TPG
                ch0 = g * CPG
                t_yinit = [None] * CPG
                pyfree = None
                for ci in range(CPG):
                    t_mm = None
                    for nh in range(2):
                        def f(e, ci=ci, nh=nh, ch0=ch0):
                            return e.matmul(
                                peo0[:, nh * 512:(nh + 1) * 512],
                                wTsb[:, (ch0 + ci) * C:(ch0 + ci + 1) * C],
                                eb2_bf[:, nh * 512:(nh + 1) * 512],
                                start=True, stop=True)
                        aft = []
                        if nh == 0:
                            aft = [t_eb2]
                            if pyfree is not None:
                                aft.append(pyfree)
                            elif peo_guard[0] is not None:
                                aft.append(peo_guard[0])
                        t_mm = s3.emit(T, f, after=aft)
                    aft_ev = [t_mm]
                    if slot_guard[ci] is not None:
                        aft_ev.append(slot_guard[ci])
                    t_yinit[ci] = s3.emit(S, lambda e, ci=ci: e.copy(
                        yac[:, ci, :], peo0[:, :]), after=aft_ev)
                    pyfree = t_yinit[ci]
                peo_free = [pyfree, pyfree]
                peh_free = [None, None]
                t_yacc_last = t_yinit

                elist = list(range(NEXP)) if g == 0 else list(range(NEXP - 1, -1, -1))
                for ei, eidx in enumerate(elist):
                    first = (g == 0 and ei == 0)
                    if not (g == 1 and ei == 0):
                        t_ew2 = load_ew2(eidx, [ew2_lastuse] if ew2_lastuse else [])
                    for t4 in range(TPG // 512):
                        tstart = tok0 + t4 * 512
                        t_ehev = [None] * KT2
                        for m in range(KT2):
                            ph = (peh0, peh1)[m % 2]
                            t_mm = None
                            for k in range(KT):
                                def f(e, m=m, k=k, ph=ph, tstart=tstart):
                                    return e.matmul(
                                        ph[:, :], ew1_bf[:, k, m * 128:(m + 1) * 128],
                                        tT[:, k, tstart:tstart + 512],
                                        start=(k == 0), stop=(k == KT - 1))
                                aft = []
                                if k == 0:
                                    aft = [t_ew1]
                                    if peh_free[m % 2] is not None:
                                        aft.append(peh_free[m % 2])
                                t_mm = s3.emit(T, f, after=aft)
                            t_ehev[m] = s3.emit(S, lambda e, m=m, ph=ph, eidx=eidx:
                                                e.activation(
                                ehb[:, m, :], ph[:, :], AF.Silu,
                                bias=eb1_sb[:, eidx, m:m + 1]),
                                after=[t_mm, t_eb1] +
                                ([t_dbg_eh] if t_dbg_eh else []))
                            peh_free[m % 2] = t_ehev[m]
                        ew1_lastuse = t_mm
                        if debug and g == 0 and ei == 0 and t4 == 0:
                            t_dbg_eh = s3.emit(SY, lambda e: e.dma_start(
                                dbg_eh[:, :, :], ehb[:, :, :]),
                                after=[t_ehev[KT2 - 1]], dma="ddbg")
                        if t4 == TPG // 512 - 1:
                            nxt = None
                            if ei + 1 < NEXP:
                                nxt = elist[ei + 1]
                            elif g == 0:
                                nxt = None
                            if nxt is not None:
                                t_ew1 = load_ew1(nxt, [ew1_lastuse])
                        for sub in range(4):
                            ci = t4 * 4 + sub
                            po = (peo0, peo1)[sub % 2]
                            t_mm = None
                            for nh in range(2):
                                for k in range(KT2):
                                    def f(e, sub=sub, nh=nh, k=k, po=po):
                                        return e.matmul(
                                            po[:, nh * 512:(nh + 1) * 512],
                                            ehb[:, k, sub * 128:(sub + 1) * 128],
                                            ew2_bf[:, k, nh * 512:(nh + 1) * 512],
                                            start=(k == 0), stop=(k == KT2 - 1))
                                    aft = []
                                    if nh == 0 and k == 0:
                                        aft = [t_ehev[KT2 - 1], t_ew2]
                                        if peo_free[sub % 2] is not None:
                                            aft.append(peo_free[sub % 2])
                                    t_mm = s3.emit(T, f, after=aft)
                            ew2_lastuse = t_mm
                            def facc(e, ci=ci, po=po, eidx=eidx, ch0=ch0):
                                return e.scalar_tensor_tensor(
                                    yac[:, ci, :], po[:, :],
                                    wsb[:, ch0 + ci, eidx:eidx + 1],
                                    yac[:, ci, :], ALU.mult, ALU.add)
                            t_yacc_last[ci] = s3.emit(V, facc,
                                                      after=[t_mm, t_yacc_last[ci]])
                            peo_free[sub % 2] = t_yacc_last[ci]
                    if debug and g == 0 and ei == 0 and t4 == TPG // 512 - 1:
                        t_dbgyac = s3.emit(SY, lambda e: e.dma_start(
                            dbg_yac[:, :, :], yac[:, :, :]),
                            after=[t_yacc_last[i] for i in range(CPG)],
                            dma="ddbg")
                        for i in range(CPG):
                            t_yacc_last[i] = t_dbgyac
                for ci in range(CPG):
                    cg = ch0 + ci
                    aft_in = ([t_prev_out] if t_prev_out else []) + (
                        [t_prev_od] if t_prev_od else [])
                    t_x1c = s3.emit(SY, lambda e, cg=cg: e.dma_start(
                        x1c[:, :], x1s[cg * C:(cg + 1) * C, :]),
                        after=aft_in, dma="dpcx")
                    t_tc = s3.emit(SY, lambda e, cg=cg: e.dma_start(
                        tc[:, :], ts_[cg * C:(cg + 1) * C, :]),
                        after=aft_in, dma="dpct")
                    t_y = s3.emit(V, lambda e, ci=ci: e.tensor_tensor(
                        yf[:, :], yac[:, ci, :], tc[:, :], ALU.add),
                        after=[t_tc, t_yacc_last[ci]] +
                              ([t_prev_dbg] if t_prev_dbg else []))
                    slot_guard[ci] = t_y
                    if debug:
                        t_prev_dbg = s3.emit(SY, lambda e, cg=cg: e.dma_start(
                            dbg_y[cg * C:(cg + 1) * C, :], yf[:, :]),
                            after=[t_y], dma="ddbg")
                    ln_toks, rs_ap, m_ap = emit_ln3(yf[:, :], [t_y])
                    t_yln = s3.emit(V, lambda e, rs_ap=rs_ap, m_ap=m_ap:
                                    e.tensor_scalar(ylnb[:, :], yf[:, :],
                                                    m_ap, rs_ap,
                                                    ALU.subtract, ALU.mult),
                                    after=ln_toks)
                    t_ob = s3.emit(V, lambda e: e.tensor_tensor(
                        x1c[:, :], ylnb[:, :], x1c[:, :], ALU.add),
                        after=[t_yln, t_x1c])
                    t_od = s3.emit(SY, lambda e, cg=cg: e.dma_start(
                        out[cg * C:(cg + 1) * C, :], x1c[:, :]),
                        after=[t_ob], dma="dout")
                    t_out_final.append(t_od)
                    t_prev_out = t_ob
                    t_prev_od = t_od
                peo_guard = list(peo_free)

            with nc.Block() as blockC:
                s3.flush(blockC, final_tokens=[("dout", s3.cnt["dout"])] + (
                    [("ddbg", s3.cnt.get("ddbg", 0))] if debug else []))

    return nc


_NC_CACHE = None


def _get_nc():
    global _NC_CACHE
    if _NC_CACHE is None:
        _NC_CACHE = build()
    return _NC_CACHE


def kernel(**inputs):
    cos_rep, sin_rep, triu, ident = _host_tables()
    f32 = lambda a: np.ascontiguousarray(np.asarray(a, dtype=np.float32))
    g = {k: f32(v) for k, v in inputs.items()}
    base = {
        "Wq": g["L0_attn_Wq"], "Wk": g["L0_attn_Wk"], "Wv": g["L0_attn_Wv"],
        "Wo": g["L0_attn_Wo"], "Wg1": g["L0_attn_Wg1"], "Wg2": g["L0_attn_Wg2"],
        "gamma_fm": f32(g["L0_attn_gamma"].reshape(KT, 128).T),
        "beta_fm": f32(g["L0_attn_beta"].reshape(KT, 128).T),
        "rw1": g["L0_ffn_rw1"],
        "rb1_fm": f32(g["L0_ffn_rb1"].reshape(KT, 128).T),
        "rw2": g["L0_ffn_rw2"],
        "rb2_b": f32(np.broadcast_to(g["L0_ffn_rb2"], (128, NEXP))),
        "ew1": g["L0_ffn_ew1"],
        "eb1_fm": f32(g["L0_ffn_eb1"].reshape(NEXP, KT2, 128).transpose(2, 0, 1)),
        "ew2": g["L0_ffn_ew2"],
        "eb2": g["L0_ffn_eb2"],
        "cosr": cos_rep, "sinr": sin_rep, "triu": triu, "ident": ident,
    }
    xfull = g["L0_attn_x"]
    in_maps = []
    for b in range(NB):
        m = dict(base)
        m["x"] = np.ascontiguousarray(xfull[b])
        in_maps.append(m)
    nc = _get_nc()
    res = run_bass_kernel_spmd(nc, in_maps, core_ids=list(range(NB)))
    outp = np.stack([np.asarray(res.results[b]["out"], dtype=np.float32)
                     for b in range(NB)], axis=0)
    return outp


# version 11 (final)
# speedup vs baseline: 1.2475x; 1.2456x over previous
import numpy as np
import ml_dtypes
from contextlib import ExitStack

import concourse.bass as bass
import concourse.mybir as mybir
from concourse.bass_utils import run_bass_kernel_spmd

F32 = mybir.dt.float32
BF16 = mybir.dt.bfloat16
I32 = mybir.dt.int32
AF = mybir.ActivationFunctionType
ALU = mybir.AluOpType
AX = mybir.AxisListType

NB, NTOK, E = 8, 4096, 1024
H, D = 16, 64
C = 128
NCH = NTOK // C
KT = E // 128
NEXP, DH = 8, 2048
KT2 = DH // 128
EPS = 1e-5
RBASE = 10000.0
NGRP = 2
CPG = NCH // NGRP
TPG = NTOK // NGRP
BIG = 1.0e30

ENGS = ("sync", "gpsimd", "vector", "scalar", "tensor")


class Sched:

    def __init__(self, nc, ctx, tag):
        self.nc = nc
        self.ctx = ctx
        self.tag = tag
        self.sem = {}
        self.cnt = {}
        for e in ENGS:
            self.sem[e] = ctx.enter_context(nc.semaphore(f"{tag}_{e}"))
            self.cnt[e] = 0
        self.recs = {e: [] for e in ENGS}
        self.pend_bar = {e: False for e in ENGS}
        self.bar_snapshot = None

    def _dma_sem(self, name):
        if name not in self.sem:
            self.sem[name] = self.ctx.enter_context(
                self.nc.semaphore(f"{self.tag}_{name}"))
            self.cnt[name] = 0
        return name

    SELF_CHAIN_HW = ("scalar", "vector", "gpsimd")
    SELF_CHAIN_SIM = ("vector", "scalar", "gpsimd")
    self_chain = False

    def emit(self, eng, fn, after=(), dma=None):
        chain = self.SELF_CHAIN_SIM if self.self_chain else self.SELF_CHAIN_HW
        waits = []
        if eng in chain and self.cnt[eng] > 0:
            waits.append((eng, self.cnt[eng]))
        if self.pend_bar[eng] and self.bar_snapshot:
            for s, v in self.bar_snapshot.items():
                if s != eng and v > 0:
                    waits.append((s, v))
            self.pend_bar[eng] = False
        for s, v in after:
            if s == eng:
                continue
            waits.append((s, v))
        if dma is not None:
            semname = self._dma_sem(dma)
            inc = 16
        else:
            semname = eng
            inc = 1
        self.cnt[semname] += inc
        tok = (semname, self.cnt[semname])
        self.recs[eng].append((waits, fn, semname, inc))
        return tok

    def barrier(self):
        self.bar_snapshot = {e: self.cnt[e] for e in ENGS}
        for e in ENGS:
            self.pend_bar[e] = True

    def flush(self, block, final_tokens=()):
        recs = self.recs
        sems = self.sem
        waited_vals = {}
        for ename in ENGS:
            for waits, fn, semname, inc in recs[ename]:
                for sname, v in waits:
                    waited_vals.setdefault(sname, set()).add(v)
        for sname, v in final_tokens:
            waited_vals.setdefault(sname, set()).add(v)
        remap = {}
        for ename in ENGS:
            vals = sorted(waited_vals.get(ename, ()))
            remap[ename] = ({v: i + 1 for i, v in enumerate(vals)}, set(vals))

        def map_tok(sname, v):
            if sname in remap:
                return remap[sname][0][v]
            return v

        def make_body(ename, extra_final):
            inc_set = remap[ename][1]

            def body(eng):
                waited = {}
                dense = 0
                for waits, fn, semname, inc in recs[ename]:
                    for sname, v in waits:
                        mv = map_tok(sname, v)
                        if waited.get(sname, 0) >= mv:
                            continue
                        eng.wait_ge(sems[sname], mv)
                        waited[sname] = mv
                    inst = fn(eng)
                    if semname == ename:
                        dense += 1
                        if dense in inc_set:
                            inst.then_inc(sems[semname], 1)
                    else:
                        inst.then_inc(sems[semname], inc)
                for sname, v in extra_final:
                    mv = map_tok(sname, v)
                    if waited.get(sname, 0) >= mv:
                        continue
                    eng.wait_ge(sems[sname], mv)
                    waited[sname] = mv
            return body

        block.sync(make_body("sync", list(final_tokens)))
        block.gpsimd(make_body("gpsimd", []))
        block.vector(make_body("vector", []))
        block.scalar(make_body("scalar", []))
        block.tensor(make_body("tensor", []))


def _host_tables():
    d2 = np.arange(D // 2, dtype=np.float64)
    inv = RBASE ** (-(d2 * 2.0) / D)
    pos = np.arange(NTOK, dtype=np.float64)
    ang = pos[:, None] * inv[None, :]
    cos = np.cos(ang).astype(np.float32)
    sin = np.sin(ang).astype(np.float32)
    cos_rep = np.ascontiguousarray(np.tile(cos, (1, H)))
    sin_rep = np.ascontiguousarray(np.tile(sin, (1, H)))
    triu = np.ascontiguousarray(np.tile(
        np.triu(np.ones((C, C), dtype=np.float32)), (1, 2))).astype(
        ml_dtypes.bfloat16)
    ident = np.eye(C, dtype=np.float32)
    return cos_rep, sin_rep, triu, ident


def build(debug=False, n_chunks=NCH, phases="ABC", stage=999,
          self_chain=False):
    nc = bass.Bass("TRN2", target_bir_lowering=False)
    dt = nc.dram_tensor
    x = dt("x", [NTOK, E], F32, kind="ExternalInput")
    Wq = dt("Wq", [E, E], F32, kind="ExternalInput")
    Wk = dt("Wk", [E, E], F32, kind="ExternalInput")
    Wv = dt("Wv", [E, E], F32, kind="ExternalInput")
    Wo = dt("Wo", [E, E], F32, kind="ExternalInput")
    Wg1 = dt("Wg1", [E, D], F32, kind="ExternalInput")
    Wg2 = dt("Wg2", [D, E], F32, kind="ExternalInput")
    gamma_fm = dt("gamma_fm", [128, KT], F32, kind="ExternalInput")
    beta_fm = dt("beta_fm", [128, KT], F32, kind="ExternalInput")
    rw1 = dt("rw1", [E, E], F32, kind="ExternalInput")
    rb1_fm = dt("rb1_fm", [128, KT], F32, kind="ExternalInput")
    rw2 = dt("rw2", [E, NEXP], F32, kind="ExternalInput")
    rb2_b = dt("rb2_b", [128, NEXP], F32, kind="ExternalInput")
    ew1 = dt("ew1", [NEXP, E, DH], F32, kind="ExternalInput")
    eb1_fm = dt("eb1_fm", [128, NEXP, KT2], F32, kind="ExternalInput")
    ew2 = dt("ew2", [NEXP, DH, E], F32, kind="ExternalInput")
    eb2 = dt("eb2", [NEXP, E], F32, kind="ExternalInput")
    cosr = dt("cosr", [NTOK, H * 32], F32, kind="ExternalInput")
    sinr = dt("sinr", [NTOK, H * 32], F32, kind="ExternalInput")
    triu_in = dt("triu", [C, 2 * C], BF16, kind="ExternalInput")
    ident_in = dt("ident", [C, C], F32, kind="ExternalInput")
    out = dt("out", [NTOK, E], F32, kind="ExternalOutput")
    skind = "ExternalOutput" if debug else "Internal"
    x1s = dt("x1s", [NTOK, E], F32, kind=skind)
    ts_ = dt("ts", [NTOK, E], BF16, kind=skind)
    dbg_w = dt("dbg_w", [NCH, 128, NEXP], F32, kind=skind) if debug else None
    dbg_y = dt("dbg_y", [NTOK, E], F32, kind=skind) if debug else None
    dbg_eh = dt("dbg_eh", [128, KT2, 512], BF16, kind=skind) if debug else None
    dbg_yac = dt("dbg_yac", [128, CPG, E], BF16, kind=skind) if debug else None

    with ExitStack() as octx:
        oe = octx.enter_context
        tT = oe(nc.sbuf_tensor("tT", [128, KT, NTOK], BF16))
        wsb = oe(nc.sbuf_tensor("wsb", [128, NCH, NEXP], F32))
        wTsb = oe(nc.sbuf_tensor("wTsb", [NEXP, NTOK], BF16))
        ones_row = oe(nc.sbuf_tensor("ones_row", [1, 128], BF16))
        bWo_row = oe(nc.sbuf_tensor("bWo_row", [1, E], BF16))
        identbf = oe(nc.sbuf_tensor("identbf", [C, C], BF16))
        trium = oe(nc.sbuf_tensor("trium", [C, 2 * C], BF16))

        with ExitStack() as actx:
            ae = actx.enter_context
            Wq_bf = ae(nc.sbuf_tensor("Wq_bf", [128, KT, E], BF16))
            Wk_bf = ae(nc.sbuf_tensor("Wk_bf", [128, KT, E], BF16))
            Wv_bf = ae(nc.sbuf_tensor("Wv_bf", [128, KT, E], BF16))
            Wo_bf = ae(nc.sbuf_tensor("Wo_bf", [128, KT, E], BF16))
            Wg1_bf = ae(nc.sbuf_tensor("Wg1_bf", [128, KT, D], BF16))
            Wg2_bf = ae(nc.sbuf_tensor("Wg2_bf", [D, E], BF16))
            gamma_sb = ae(nc.sbuf_tensor("gamma_sb", [128, KT], F32))
            beta_sb = ae(nc.sbuf_tensor("beta_sb", [128, KT], F32))
            beta_bf = ae(nc.sbuf_tensor("beta_bf", [128, KT], BF16))
            cosb = ae(nc.sbuf_tensor("cosb", [128, 512], F32))
            sinb = ae(nc.sbuf_tensor("sinb", [128, 512], F32))
            xb = ae(nc.sbuf_tensor("xb", [128, 2, E], F32))
            ln1b = ae(nc.sbuf_tensor("ln1b", [128, E], BF16))
            ln1T = ae(nc.sbuf_tensor("ln1T", [128, KT, C], BF16))
            qs = ae(nc.sbuf_tensor("qs", [128, E], F32))
            ks = ae(nc.sbuf_tensor("ks", [128, E], F32))
            ao_sb = ae(nc.sbuf_tensor("ao_sb", [128, E], BF16))
            rt2 = ae(nc.sbuf_tensor("rt2", [128, 512], F32))
            kt2 = ae(nc.sbuf_tensor("kt2", [128, 512], F32))
            qro = ae(nc.sbuf_tensor("qro", [128, E], BF16))
            kro = ae(nc.sbuf_tensor("kro", [128, E], BF16))
            vb = ae(nc.sbuf_tensor("vb", [128, E], BF16))
            qT = ae(nc.sbuf_tensor("qT", [128, KT, C], BF16))
            kTt = ae(nc.sbuf_tensor("kTt", [128, KT, C], BF16))
            g1b = ae(nc.sbuf_tensor("g1b", [D, C], BF16))
            th = ae(nc.sbuf_tensor("th", [128, E], BF16))
            eT = ae(nc.sbuf_tensor("eT", [128, 2, 2 * C], BF16))
            Sf = ae(nc.sbuf_tensor("Sf", [128, 512], F32))
            Sbf = ae(nc.sbuf_tensor("Sbf", [128, 512], BF16))
            sb_o = ae(nc.sbuf_tensor("sb_o", [128, E], BF16))
            x1b = ae(nc.sbuf_tensor("x1b", [128, E], F32))
            dump = ae(nc.sbuf_tensor("dump", [128, E], BF16))
            og = ae(nc.sbuf_tensor("og", [128, 2, E], BF16))
            scol = ae(nc.sbuf_tensor("scol", [128, 32], F32))
            scol2 = ae(nc.sbuf_tensor("scol2", [128, 32], F32))
            pbig0 = ae(nc.psum_tensor("pbig0", [128, 1024], F32))
            pbig1 = ae(nc.psum_tensor("pbig1", [128, 1024], F32))
            ptr = ae(nc.psum_tensor("ptr", [128, KT, C], BF16))
            pS = ae(nc.psum_tensor("pS", [128, 512], F32))
            pm6 = ae(nc.psum_tensor("pm6", [128, 512], F32))
            pm7 = ae(nc.psum_tensor("pm7", [128, 512], F32))

            s = Sched(nc, actx, "a")
            s.self_chain = self_chain
            T, V, S, GP, SY = "tensor", "vector", "scalar", "gpsimd", "sync"

            t_wq = s.emit(GP, lambda e: e.dma_start(
                Wq_bf[:, :, :], Wq[:, :].rearrange("(k p) n -> p k n", p=128)), dma="dw")
            t_wk = s.emit(GP, lambda e: e.dma_start(
                Wk_bf[:, :, :], Wk[:, :].rearrange("(k p) n -> p k n", p=128)), dma="dw")
            t_wv = s.emit(GP, lambda e: e.dma_start(
                Wv_bf[:, :, :], Wv[:, :].rearrange("(k p) n -> p k n", p=128)), dma="dw")
            t_wo = s.emit(GP, lambda e: e.dma_start(
                Wo_bf[:, :, :], Wo[:, :].rearrange("(k p) n -> p k n", p=128)), dma="dw")
            t_wg1 = s.emit(GP, lambda e: e.dma_start(
                Wg1_bf[:, :, :], Wg1[:, :].rearrange("(k p) n -> p k n", p=128)), dma="dw")
            t_wg2 = s.emit(GP, lambda e: e.dma_start(Wg2_bf[:, :], Wg2[:, :]), dma="dw")
            t_idb = s.emit(GP, lambda e: e.dma_start(identbf[:, :], ident_in[:, :]), dma="dw")
            tok_wall = ("dw", s.cnt["dw"])
            t_wq = t_wk = t_wv = t_wo = t_wg1 = t_wg2 = t_idb = tok_wall
            t_gam = s.emit(SY, lambda e: e.dma_start(gamma_sb[:, :], gamma_fm[:, :]), dma="dmisc")
            t_bet = s.emit(SY, lambda e: e.dma_start(beta_sb[:, :], beta_fm[:, :]), dma="dmisc")
            t_tri = s.emit(SY, lambda e: e.dma_start(trium[:, :], triu_in[:, :]), dma="dmisc")
            tok_misc = ("dmisc", s.cnt["dmisc"])
            t_gam = t_bet = t_tri = tok_misc
            t_bbf = s.emit(V, lambda e: e.tensor_copy(beta_bf[:, :], beta_sb[:, :]),
                           after=[t_bet])
            t_ones = s.emit(V, lambda e: e.memset(ones_row[:, :], 1.0))
            t_sf0 = s.emit(V, lambda e: e.memset(Sf[:, :], 0.0))
            t_sbf0 = s.emit(V, lambda e: e.memset(Sbf[:, :], 0.0), after=[t_sf0])
            t_bwo_mm = None
            t_bwo_ev = None
            for half in range(2):
                for k in range(KT):
                    def f(e, half=half, k=k):
                        return e.matmul(pm7[0:1, 0:512],
                                        beta_bf[:, k:k + 1],
                                        Wo_bf[:, k, half * 512:(half + 1) * 512],
                                        start=(k == 0), stop=(k == KT - 1))
                    aft = []
                    if k == 0:
                        aft = [t_wo, t_bbf]
                        if t_bwo_ev is not None:
                            aft.append(t_bwo_ev)
                    t_bwo_mm = s.emit(T, f, after=aft)
                t_bwo_ev = s.emit(S, lambda e, half=half: e.copy(
                    bWo_row[0:1, half * 512:(half + 1) * 512], pm7[0:1, 0:512]),
                    after=[t_bwo_mm])
            t_wog = []
            for k in range(KT):
                t_wog.append(s.emit(V, lambda e, k=k: e.tensor_scalar(
                    Wo_bf[:, k, :], Wo_bf[:, k, :], gamma_sb[:, k:k + 1], None,
                    ALU.mult), after=[t_bwo_ev, t_gam, t_wo]))

            tok_x = [None] * (NCH + 2)
            tok_x[0] = s.emit(SY, lambda e: e.dma_start(xb[:, 0, :], x[0:C, :]), dma="dx0")
            if n_chunks > 1:
                tok_x[1] = s.emit(SY, lambda e: e.dma_start(
                    xb[:, 1, :], x[C:2 * C, :]), dma="dx1")
            t_cos = s.emit(SY, lambda e: e.dma_start(cosb[:, :], cosr[0:C, :]), dma="dtab")
            t_sin = s.emit(SY, lambda e: e.dma_start(sinb[:, :], sinr[0:C, :]), dma="dtab")
            t_cos = t_sin = ("dtab", s.cnt["dtab"])

            prev = {
                "x1_store": None,
                "t_store": None,
                "ptr_free": t_idb,
                "pbig0_free": t_bwo_ev,
                "pbig1_free": t_bwo_ev,
                "Sbf_ready": t_sbf0,
                "Sf_ready": t_sf0,
                "pS_free": t_sf0,
                "rope_done_v": t_ones,
                "rope_done_g": t_idb,
                "x_free": [t_bbf],
                "tT_slot": None,
            }

            def emit_ln_stats(colbuf, src_ap, after_tok, psum_src=False):
                c_sx, c_sq, c_m, c_a, c_msq, c_b, c_s, c_rs = range(8)
                t_sx = s.emit(V, lambda e: e.reduce_sum(
                    colbuf[:, c_sx:c_sx + 1], src_ap, axis=AX.X), after=after_tok)
                t_sq = s.emit(S, lambda e: e.activation(
                    dump[:, :], src_ap, AF.Square,
                    accum_out=colbuf[:, c_sq:c_sq + 1]),
                    after=after_tok + ([t_sx] if psum_src else []))
                t_m = s.emit(V, lambda e: e.tensor_scalar(
                    colbuf[:, c_m:c_m + 1], colbuf[:, c_sx:c_sx + 1], 1.0 / E,
                    None, ALU.mult), after=[t_sx])
                t_a = s.emit(V, lambda e: e.tensor_scalar(
                    colbuf[:, c_a:c_a + 1], colbuf[:, c_sq:c_sq + 1], 1.0 / E,
                    EPS, ALU.mult, ALU.add), after=[t_sq])
                t_msq = s.emit(V, lambda e: e.tensor_tensor(
                    colbuf[:, c_msq:c_msq + 1], colbuf[:, c_m:c_m + 1],
                    colbuf[:, c_m:c_m + 1], ALU.mult), after=[t_m])
                t_b = s.emit(V, lambda e: e.tensor_tensor(
                    colbuf[:, c_b:c_b + 1], colbuf[:, c_a:c_a + 1],
                    colbuf[:, c_msq:c_msq + 1], ALU.subtract),
                    after=[t_a, t_msq])
                t_s = s.emit(S, lambda e: e.activation(
                    colbuf[:, c_s:c_s + 1], colbuf[:, c_b:c_b + 1], AF.Sqrt),
                    after=[t_b])
                t_rs = s.emit(V, lambda e: e.reciprocal(
                    colbuf[:, c_rs:c_rs + 1], colbuf[:, c_s:c_s + 1]),
                    after=[t_s])
                return ([t_rs], colbuf[:, c_rs:c_rs + 1],
                        colbuf[:, c_m:c_m + 1])

            cos_v = cosb[:, :].rearrange("p (h d) -> p h d", h=H)
            sin_v = sinb[:, :].rearrange("p (h d) -> p h d", h=H)

            def emit_rope(eng, src, scr2, dst, after_toks):
                sv = src[:, :].rearrange("p (h d) -> p h d", h=H)
                dv = dst[:, :].rearrange("p (h d) -> p h d", h=H)
                s2 = scr2[:, :].rearrange("p (h d) -> p h d", h=H)
                x1v, x2v = sv[:, :, 0:32], sv[:, :, 32:64]
                d1, d2 = dv[:, :, 0:32], dv[:, :, 32:64]
                s.emit(eng, lambda e: e.tensor_tensor(
                    s2[:, :, :], x2v, sin_v, ALU.mult), after=after_toks)
                s.emit(eng, lambda e: e.tensor_tensor(
                    d1, x1v, cos_v, ALU.mult))
                s.emit(eng, lambda e: e.tensor_tensor(
                    d1, d1, s2[:, :, :], ALU.subtract))
                s.emit(eng, lambda e: e.tensor_tensor(
                    s2[:, :, :], x2v, cos_v, ALU.mult))
                s.emit(eng, lambda e: e.tensor_tensor(
                    d2, x1v, sin_v, ALU.mult))
                t6 = s.emit(eng, lambda e: e.tensor_tensor(
                    d2, d2, s2[:, :, :], ALU.add))
                return t6

            prev.update({
                "scol_rd": None, "scol2_rd": None, "ln1b_rd": None,
                "ln1T_rd": None, "qs_rd": None, "ks_rd": None,
                "qro_rd": None, "kro_rd": [], "vb_rd": [],
                "qT_rd": [], "kT_rd": [], "th_rd": None,
                "sb_o_rd": None, "outT_rd": None, "og_rd2": [None, None],
                "t_mask": [None, None],
            })

            def emit_front(c):
                f = {}
                sl = c % 2
                if c > 0:
                    s.emit(SY, lambda e, c=c: e.dma_start(
                        cosb[:, :], cosr[c * C:(c + 1) * C, :]),
                        after=[prev["rope_done_v"], prev["rope_done_g"]],
                        dma="dtab")
                    s.emit(SY, lambda e, c=c: e.dma_start(
                        sinb[:, :], sinr[c * C:(c + 1) * C, :]),
                        after=[prev["rope_done_v"], prev["rope_done_g"]],
                        dma="dtab")
                    f["tab"] = ("dtab", s.cnt["dtab"])
                else:
                    f["tab"] = t_cos
                xc = xb[:, sl, :]
                f["xc"] = xc
                f["c"] = c
                ln_toks, rs_ap, m_ap = emit_ln_stats(
                    scol, xc, [tok_x[c]] +
                    ([prev["scol_rd"]] if prev["scol_rd"] else []))
                t_ln1 = s.emit(V, lambda e, rs_ap=rs_ap, m_ap=m_ap, xc=xc:
                               e.tensor_scalar(ln1b[:, :], xc, m_ap, rs_ap,
                                               ALU.subtract, ALU.mult),
                               after=ln_toks +
                               ([prev["ln1b_rd"]] if prev["ln1b_rd"] else []))
                prev["scol_rd"] = t_ln1
                t_tr = None
                for j in range(KT):
                    t_tr = s.emit(T, lambda e, j=j: e.transpose(
                        ptr[:, j, :], ln1b[:, j * 128:(j + 1) * 128],
                        identbf[:, :]),
                        after=[t_ln1, prev["ptr_free"], t_idb] if j == 0 else [])
                prev["ln1b_rd"] = t_tr
                t_ln1T = s.emit(S, lambda e: e.copy(
                    ln1T[:, :, :], ptr[:, :, :]),
                    after=[t_tr] +
                    ([prev["ln1T_rd"]] if prev["ln1T_rd"] else []))
                prev["ptr_free"] = t_ln1T
                t_qmm = None
                for nh in range(2):
                    for k in range(KT):
                        def fq(e, nh=nh, k=k):
                            return e.matmul(pbig0[:, nh * 512:(nh + 1) * 512],
                                            ln1T[:, k, :],
                                            Wq_bf[:, k, nh * 512:(nh + 1) * 512],
                                            start=(k == 0), stop=(k == KT - 1))
                        t_qmm = s.emit(T, fq,
                                       after=[t_ln1T, prev["pbig0_free"], t_wq]
                                       if (nh == 0 and k == 0) else [])
                t_kmm = None
                for nh in range(2):
                    for k in range(KT):
                        def fk(e, nh=nh, k=k):
                            return e.matmul(pbig1[:, nh * 512:(nh + 1) * 512],
                                            ln1T[:, k, :],
                                            Wk_bf[:, k, nh * 512:(nh + 1) * 512],
                                            start=(k == 0), stop=(k == KT - 1))
                        t_kmm = s.emit(T, fk,
                                       after=[t_ln1T, prev["pbig1_free"], t_wk]
                                       if (nh == 0 and k == 0) else [])
                t_sq_ = s.emit(S, lambda e: e.activation(qs[:, :], pbig0[:, :],
                                                         AF.Silu),
                               after=[t_qmm] +
                               ([prev["qs_rd"]] if prev["qs_rd"] else []))
                t_sk_ = s.emit(S, lambda e: e.activation(ks[:, :], pbig1[:, :],
                                                         AF.Silu),
                               after=[t_kmm] +
                               ([prev["ks_rd"]] if prev["ks_rd"] else []))
                t_vmm = None
                for nh in range(2):
                    for k in range(KT):
                        def fv(e, nh=nh, k=k):
                            return e.matmul(pbig0[:, nh * 512:(nh + 1) * 512],
                                            ln1T[:, k, :],
                                            Wv_bf[:, k, nh * 512:(nh + 1) * 512],
                                            start=(k == 0), stop=(k == KT - 1))
                        t_vmm = s.emit(T, fv, after=[t_sq_, t_wv]
                                       if (nh == 0 and k == 0) else [])
                t_g1 = None
                for k in range(KT):
                    def fg(e, k=k):
                        return e.matmul(pm6[0:64, 0:128], Wg1_bf[:, k, :],
                                        ln1T[:, k, :],
                                        start=(k == 0), stop=(k == KT - 1))
                    aft = []
                    if k == 0:
                        aft = [t_ln1T, t_wg1]
                        if prev["t_mask"][0] is not None:
                            aft.append(prev["t_mask"][0])
                    t_g1 = s.emit(T, fg, after=aft)
                prev["ln1T_rd"] = t_g1
                t_g1e = s.emit(S, lambda e: e.copy(g1b[:, :], pm6[0:64, 0:128]),
                               after=[t_g1])
                t_g2 = None
                for nh in range(2):
                    def fg2(e, nh=nh):
                        return e.matmul(pbig1[:, nh * 512:(nh + 1) * 512],
                                        g1b[:, :],
                                        Wg2_bf[:, nh * 512:(nh + 1) * 512],
                                        start=True, stop=True)
                    t_g2 = s.emit(T, fg2, after=[t_sk_, t_g1e, t_wg2]
                                  if nh == 0 else [])
                t_th = s.emit(S, lambda e: e.activation(th[:, :], pbig1[:, :],
                                                        AF.Tanh, scale=0.5),
                              after=[t_g2] +
                              ([prev["th_rd"]] if prev["th_rd"] else []))
                t_vb = s.emit(V, lambda e: e.tensor_copy(vb[:, :], pbig0[:, :]),
                              after=[t_vmm] + prev["vb_rd"])
                f["t_vb"] = t_vb
                t_ropeq = emit_rope(GP, qs, rt2, qro,
                                    [f["tab"], t_sq_] +
                                    ([prev["qro_rd"]] if prev["qro_rd"] else []))
                t_ropek = emit_rope(GP, ks, kt2, kro,
                                    [f["tab"], t_sk_] + prev["kro_rd"])
                prev["rope_done_v"] = t_ropeq
                prev["rope_done_g"] = t_ropek
                prev["qs_rd"] = t_ropeq
                prev["ks_rd"] = t_ropek
                t_gate = s.emit(V, lambda e: e.tensor_scalar(
                    th[:, :], th[:, :], 0.5, 0.5, ALU.mult, ALU.add),
                    after=[t_th])
                t_trq = None
                for j in range(KT):
                    t_trq = s.emit(T, lambda e, j=j: e.transpose(
                        ptr[:, j, :], qro[:, j * 128:(j + 1) * 128], identbf[:, :]),
                        after=[t_ropeq, prev["ptr_free"]] if j == 0 else [])
                prev["qro_rd"] = t_trq
                t_qT = s.emit(S, lambda e: e.copy(qT[:, :, :], ptr[:, :, :]),
                              after=[t_trq] + prev["qT_rd"])
                prev["ptr_free"] = t_qT
                t_trk = None
                for j in range(KT):
                    t_trk = s.emit(T, lambda e, j=j: e.transpose(
                        ptr[:, j, :], kro[:, j * 128:(j + 1) * 128], identbf[:, :]),
                        after=[t_ropek, prev["ptr_free"]] if j == 0 else [])
                t_kT = s.emit(S, lambda e: e.copy(kTt[:, :, :], ptr[:, :, :]),
                              after=[t_trk] + prev["kT_rd"])
                prev["ptr_free"] = t_kT
                t_maskp = prev["t_mask"]
                t_intra_last = None
                t_inter_last = None
                t_state_last = None
                for h in range(H):
                    po = 64 * (h % 2)
                    tile = h // 2
                    slot = h % 2
                    pe_slot = (pm6, pm7)[slot]
                    t_en = s.emit(T, lambda e, po=po, tile=tile, pe_slot=pe_slot:
                                  e.matmul(pe_slot[:, 128:256],
                                           kTt[po:po + 64, tile, :],
                                           qT[po:po + 64, tile, :],
                                           start=True, stop=True),
                                  after=([t_qT, t_kT, t_th] if h == 0 else []) +
                                        ([t_maskp[slot]] if t_maskp[slot] else []))
                    t_maskp[slot] = s.emit(V, lambda e, slot=slot, pe_slot=pe_slot:
                                          e.tensor_tensor(
                                              eT[:, slot, 0:128],
                                              pe_slot[:, 128:256],
                                              trium[:, 0:128], ALU.mult),
                                          after=[t_en, t_tri])
                    t_intra_last = s.emit(T, lambda e, h=h, slot=slot: e.matmul(
                        pbig1[:, h * 64:(h + 1) * 64], eT[:, slot, 0:128],
                        vb[:, h * 64:(h + 1) * 64], start=True, stop=False),
                        after=[t_maskp[slot], t_vb] + ([t_th] if h == 0 else []))
                    t_inter_last = s.emit(T, lambda e, h=h, po=po, tile=tile:
                                          e.matmul(
                        pbig1[:, h * 64:(h + 1) * 64],
                        qT[po:po + 64, tile, :],
                        Sbf[po:po + 64, (h // 2) * 64:(h // 2) * 64 + 64],
                        start=False, stop=True),
                        after=[prev["Sbf_ready"]] if h == 0 else [])
                    t_state_last = s.emit(T, lambda e, h=h, po=po: e.matmul(
                        pS[po:po + 64, (h // 2) * 64:(h // 2) * 64 + 64],
                        kro[:, h * 64:(h + 1) * 64],
                        vb[:, h * 64:(h + 1) * 64], start=True, stop=True),
                        after=[t_ropek, t_vb, prev["pS_free"]] if h == 0 else [])
                prev["kro_rd"] = [t_trk, t_state_last]
                prev["vb_rd"] = [t_intra_last, t_state_last]
                prev["kT_rd"] = [t_inter_last]
                t_sfu = s.emit(V, lambda e: e.tensor_tensor(
                    Sf[:, :], Sf[:, :], pS[:, :], ALU.add),
                    after=[t_state_last, prev["Sf_ready"]])
                prev["pS_free"] = t_sfu
                prev["Sf_ready"] = t_sfu
                t_sbf = s.emit(V, lambda e: e.tensor_copy(Sbf[:, :], Sf[:, :]),
                               after=[t_sfu, t_inter_last])
                prev["Sbf_ready"] = t_sbf
                t_og = s.emit(V, lambda e, sl=sl: e.tensor_tensor(
                    og[:, sl, :], pbig1[:, :], th[:, :], ALU.mult),
                    after=[t_intra_last, t_inter_last, t_gate] +
                          ([prev["og_rd2"][sl]] if prev["og_rd2"][sl] else []))
                prev["th_rd"] = t_og
                prev["pbig1_free"] = t_og
                f["t_og"] = t_og
                f["t_inter_last"] = t_inter_last
                f["t_vb"] = t_vb
                f["t_th"] = t_th
                return f

            def emit_back(c, f):
                xc = f["xc"]
                sl = c % 2
                ogs = og[:, sl, :]
                ln_toks, rs_ap, m_ap = emit_ln_stats(
                    scol2, ogs, [f["t_og"]] +
                    ([prev["scol2_rd"]] if prev["scol2_rd"] else []))
                t_oln = s.emit(V, lambda e, rs_ap=rs_ap, m_ap=m_ap, ogs=ogs:
                               e.tensor_scalar(sb_o[:, :], ogs, m_ap, rs_ap,
                                               ALU.subtract, ALU.mult),
                               after=ln_toks +
                               ([prev["sb_o_rd"]] if prev["sb_o_rd"] else []) +
                               ([prev["t_store"]] if prev["t_store"] else []))
                prev["og_rd2"][sl] = t_oln
                t_tro = None
                for j in range(KT):
                    t_tro = s.emit(T, lambda e, j=j: e.transpose(
                        ptr[:, j, :], sb_o[:, j * 128:(j + 1) * 128], identbf[:, :]),
                        after=[t_oln, prev["ptr_free"]] if j == 0 else [])
                t_oT = s.emit(S, lambda e: e.copy(qT[:, :, :], ptr[:, :, :]),
                              after=[t_tro])
                prev["ptr_free"] = t_oT
                t_ao = None
                for nh in range(2):
                    for k in range(KT):
                        def fa(e, nh=nh, k=k):
                            return e.matmul(pbig0[:, nh * 512:(nh + 1) * 512],
                                            qT[:, k, :],
                                            Wo_bf[:, k, nh * 512:(nh + 1) * 512],
                                            start=(k == 0), stop=False)
                        t_ao = s.emit(T, fa,
                                      after=[t_oT, f["t_vb"]] + t_wog
                                      if (nh == 0 and k == 0) else [])
                    def fb(e, nh=nh):
                        return e.matmul(pbig0[:, nh * 512:(nh + 1) * 512],
                                        ones_row[:, :],
                                        bWo_row[0:1, nh * 512:(nh + 1) * 512],
                                        start=False, stop=True)
                    t_ao = s.emit(T, fb, after=[t_ones] if nh == 0 else [])
                prev["qT_rd"] = [t_ao]
                t_aoev = s.emit(V, lambda e: e.tensor_copy(
                    ao_sb[:, :], pbig0[:, :]), after=[t_ao])
                prev["pbig0_free"] = t_aoev
                ln_toks, rs_ap, m_ap = emit_ln_stats(scol2, ao_sb[:, :],
                                                     [t_aoev])
                t_aoln = s.emit(V, lambda e, rs_ap=rs_ap, m_ap=m_ap:
                                e.tensor_scalar(sb_o[:, :], ao_sb[:, :],
                                                m_ap, rs_ap,
                                                ALU.subtract, ALU.mult),
                                after=ln_toks + [t_oT])
                t_x1 = s.emit(V, lambda e, xc=xc: e.tensor_tensor(
                    x1b[:, :], sb_o[:, :], xc, ALU.add),
                    after=[t_aoln] +
                    ([prev["x1_store"]] if prev["x1_store"] else []))
                t_x1d = s.emit(SY, lambda e, c=c: e.dma_start(
                    x1s[c * C:(c + 1) * C, :], x1b[:, :]), after=[t_x1],
                    dma="dx1s")
                prev["x1_store"] = t_x1d
                if c + 2 < n_chunks:
                    tok_x[c + 2] = s.emit(SY, lambda e, c=c: e.dma_start(
                        xb[:, c % 2, :], x[(c + 2) * C:(c + 3) * C, :]),
                        after=[t_x1], dma="dx0" if (c + 2) % 2 == 0 else "dx1")
                ln_toks, rs_ap, m_ap = emit_ln_stats(scol2, x1b[:, :],
                                                     [t_x1])
                t_tb = s.emit(V, lambda e, rs_ap=rs_ap, m_ap=m_ap:
                              e.tensor_scalar(sb_o[:, :], x1b[:, :],
                                              m_ap, rs_ap,
                                              ALU.subtract, ALU.mult),
                              after=ln_toks + [t_x1])
                prev["scol2_rd"] = t_tb
                t_tbd = s.emit(SY, lambda e, c=c: e.dma_start(
                    ts_[c * C:(c + 1) * C, :], sb_o[:, :]), after=[t_tb],
                    dma="dts")
                prev["t_store"] = t_tbd
                t_trt = None
                for j in range(KT):
                    t_trt = s.emit(T, lambda e, j=j: e.transpose(
                        ptr[:, j, :], sb_o[:, j * 128:(j + 1) * 128], identbf[:, :]),
                        after=[t_tb, prev["ptr_free"]] if j == 0 else [])
                prev["sb_o_rd"] = t_trt
                t_tT = s.emit(S, lambda e, c=c: e.copy(
                    tT[:, :, c * C:(c + 1) * C], ptr[:, :, :]), after=[t_trt])
                prev["ptr_free"] = t_tT
                prev["outT_rd"] = t_ao
                prev["tT_slot"] = t_tT

            fstate = {}
            for i in range(n_chunks + 1):
                if i < n_chunks:
                    fstate[i] = emit_front(i)
                if i >= 1:
                    emit_back(i - 1, fstate.pop(i - 1))

            a_final = []
            if prev["tT_slot"] is not None:
                a_final.append(prev["tT_slot"])
            for nm in ("dx1s", "dts"):
                if s.cnt.get(nm, 0) > 0:
                    a_final.append((nm, s.cnt[nm]))
            with nc.Block() as blockA:
                s.flush(blockA, final_tokens=a_final)

        if "B" not in phases:
            return nc
        with ExitStack() as bctx:
            be = bctx.enter_context
            rw1_bf = be(nc.sbuf_tensor("rw1_bf", [128, KT, E], BF16))
            rw2_bf = be(nc.sbuf_tensor("rw2_bf", [128, KT, NEXP], BF16))
            rb1_sb = be(nc.sbuf_tensor("rb1_sb", [128, KT], F32))
            rb2_sb = be(nc.sbuf_tensor("rb2_sb", [128, NEXP], F32))
            r1b = be(nc.sbuf_tensor("r1b", [128, KT, 512], BF16))
            l_sb = be(nc.sbuf_tensor("l_sb", [128, NCH, NEXP], F32))
            tpw = be(nc.sbuf_tensor("tpw", [128, 6 * NEXP], F32))
            wbf = be(nc.sbuf_tensor("wbf", [128, NEXP], BF16))
            pr0 = be(nc.psum_tensor("pr0", [128, 512], F32))
            pr1 = be(nc.psum_tensor("pr1", [128, 512], F32))
            pl = be(nc.psum_tensor("pl", [128, NEXP], F32))
            pwt = be(nc.psum_tensor("pwt", [NEXP, 128], BF16))

            s2 = Sched(nc, bctx, "b")
            s2.self_chain = self_chain
            T, V, S, GP, SY = "tensor", "vector", "scalar", "gpsimd", "sync"
            t_rw1 = s2.emit(GP, lambda e: e.dma_start(
                rw1_bf[:, :, :], rw1[:, :].rearrange("(k p) n -> p k n", p=128)),
                dma="dw")
            t_rw2 = s2.emit(GP, lambda e: e.dma_start(
                rw2_bf[:, :, :], rw2[:, :].rearrange("(k p) n -> p k n", p=128)),
                dma="dw")
            t_rw1 = t_rw2 = ("dw", s2.cnt["dw"])
            t_rb1 = s2.emit(SY, lambda e: e.dma_start(rb1_sb[:, :], rb1_fm[:, :]),
                            dma="dmisc")
            t_rb2 = s2.emit(SY, lambda e: e.dma_start(rb2_sb[:, :], rb2_b[:, :]),
                            dma="dmisc")
            t_rb1 = t_rb2 = ("dmisc", s2.cnt["dmisc"])
            prfree = [None, None]
            t_lev = [None] * NCH
            for t8 in range(NTOK // 512):
                t_r1 = [None] * KT
                for m in range(KT):
                    pr = (pr0, pr1)[m % 2]
                    t_mm = None
                    for k in range(KT):
                        def f(e, m=m, k=k, t8=t8, pr=pr):
                            return e.matmul(pr[:, :], rw1_bf[:, k, m * 128:(m + 1) * 128],
                                            tT[:, k, t8 * 512:(t8 + 1) * 512],
                                            start=(k == 0), stop=(k == KT - 1))
                        aft = []
                        if k == 0:
                            aft = [t_rw1]
                            if prfree[m % 2] is not None:
                                aft.append(prfree[m % 2])
                        t_mm = s2.emit(T, f, after=aft)
                    t_r1[m] = s2.emit(S, lambda e, m=m, pr=pr: e.activation(
                        r1b[:, m, :], pr[:, :], AF.Silu,
                        bias=rb1_sb[:, m:m + 1]), after=[t_mm, t_rb1])
                    prfree[m % 2] = t_r1[m]
                for sub in range(4):
                    ci = t8 * 4 + sub
                    t_lmm = None
                    for k in range(KT):
                        def f(e, k=k, sub=sub):
                            return e.matmul(pl[:, :], r1b[:, k, sub * 128:(sub + 1) * 128],
                                            rw2_bf[:, k, :],
                                            start=(k == 0), stop=(k == KT - 1))
                        t_lmm = s2.emit(T, f,
                                        after=[t_r1[KT - 1], t_rw2] + ([t_lev[ci - 1]] if ci > 0 else [])
                                        if k == 0 else [])
                    t_lev[ci] = s2.emit(V, lambda e, ci=ci: e.tensor_tensor(
                        l_sb[:, ci, :], pl[:, :], rb2_sb[:, :], ALU.add),
                        after=[t_lmm, t_rb2])
            cm1, cnm1, cm2, cZ, crz = 0, 1, 2, 3, 4
            OFF = NEXP
            t_wT_ev = None
            t_wsb = [None] * NCH
            for ci in range(NCH):
                lap = l_sb[:, ci, :]
                t_m1 = s2.emit(V, lambda e, lap=lap: e.reduce_max(
                    tpw[:, cm1:cm1 + 1], lap, axis=AX.X), after=[t_lev[ci]])
                t_nm1 = s2.emit(V, lambda e: e.tensor_scalar(
                    tpw[:, cnm1:cnm1 + 1], tpw[:, cm1:cm1 + 1], -1.0, None,
                    ALU.mult), after=[t_m1])
                t_msk1 = s2.emit(V, lambda e, lap=lap: e.tensor_scalar(
                    tpw[:, OFF:OFF + NEXP], lap, tpw[:, cm1:cm1 + 1], None,
                    ALU.is_ge), after=[t_m1])
                t_masked = s2.emit(V, lambda e, lap=lap: e.scalar_tensor_tensor(
                    tpw[:, 2 * OFF:2 * OFF + NEXP], tpw[:, OFF:OFF + NEXP], -BIG,
                    lap, ALU.mult, ALU.add), after=[t_msk1])
                t_m2 = s2.emit(V, lambda e: e.reduce_max(
                    tpw[:, cm2:cm2 + 1], tpw[:, 2 * OFF:2 * OFF + NEXP],
                    axis=AX.X), after=[t_masked])
                t_sel = s2.emit(V, lambda e, lap=lap: e.tensor_scalar(
                    tpw[:, 3 * OFF:3 * OFF + NEXP], lap, tpw[:, cm2:cm2 + 1],
                    None, ALU.is_ge), after=[t_m2])
                t_e = s2.emit(S, lambda e, lap=lap: e.activation(
                    tpw[:, 4 * OFF:4 * OFF + NEXP], lap, AF.Exp,
                    bias=tpw[:, cnm1:cnm1 + 1]), after=[t_nm1, t_lev[ci]])
                t_es = s2.emit(V, lambda e: e.tensor_tensor(
                    tpw[:, 5 * OFF:5 * OFF + NEXP], tpw[:, 4 * OFF:4 * OFF + NEXP],
                    tpw[:, 3 * OFF:3 * OFF + NEXP], ALU.mult),
                    after=[t_e, t_sel])
                t_Z = s2.emit(V, lambda e: e.reduce_sum(
                    tpw[:, cZ:cZ + 1], tpw[:, 5 * OFF:5 * OFF + NEXP],
                    axis=AX.X), after=[t_es])
                t_rz = s2.emit(V, lambda e: e.reciprocal(
                    tpw[:, crz:crz + 1], tpw[:, cZ:cZ + 1]), after=[t_Z])
                t_wsb[ci] = s2.emit(V, lambda e, ci=ci: e.tensor_scalar(
                    wsb[:, ci, :], tpw[:, 5 * OFF:5 * OFF + NEXP],
                    tpw[:, crz:crz + 1], None, ALU.mult), after=[t_rz])
                t_wbf = s2.emit(V, lambda e, ci=ci: e.tensor_copy(
                    wbf[:, :], wsb[:, ci, :]), after=[t_wsb[ci], t_wT_ev]
                    if t_wT_ev else [t_wsb[ci]])
                t_wT = s2.emit(T, lambda e: e.transpose(
                    pwt[:, :], wbf[:, :], identbf[:, :]),
                    after=[t_wbf] + ([t_wT_ev] if t_wT_ev else []))
                t_wT_ev = s2.emit(S, lambda e, ci=ci: e.copy(
                    wTsb[:, ci * C:(ci + 1) * C], pwt[:, :]), after=[t_wT])
            if debug:
                t_wdump = s2.emit(SY, lambda e: e.dma_start(
                    dbg_w[:, :, :].rearrange("c p e -> p c e"), wsb[:, :, :]),
                    after=t_wsb, dma="ddbg")
            b_final = [t_wT_ev] + t_wsb + (
                [("ddbg", s2.cnt.get("ddbg", 0))] if debug else [])
            with nc.Block() as blockB:
                s2.flush(blockB, final_tokens=b_final)

        if "C" not in phases:
            return nc
        with ExitStack() as cctx:
            ce = cctx.enter_context
            ew1_bf = ce(nc.sbuf_tensor("ew1_bf", [128, KT, DH], BF16))
            ew2_bf = ce(nc.sbuf_tensor("ew2_bf", [128, KT2, E], BF16))
            ehb = ce(nc.sbuf_tensor("ehb", [128, KT2, 512], BF16))
            yac = ce(nc.sbuf_tensor("yac", [128, CPG, E], BF16))
            eb1_sb = ce(nc.sbuf_tensor("eb1_sb", [128, NEXP, KT2], F32))
            eb2_bf = ce(nc.sbuf_tensor("eb2_bf", [NEXP, E], BF16))
            x1c = ce(nc.sbuf_tensor("x1c", [128, E], F32))
            tc = ce(nc.sbuf_tensor("tc", [128, E], BF16))
            yf = ce(nc.sbuf_tensor("yf", [128, E], F32))
            ylnb = ce(nc.sbuf_tensor("ylnb", [128, E], BF16))
            sc3 = ce(nc.sbuf_tensor("sc3", [128, 16], F32))
            peh0 = ce(nc.psum_tensor("peh0", [128, 512], F32))
            peh1 = ce(nc.psum_tensor("peh1", [128, 512], F32))
            peo0 = ce(nc.psum_tensor("peo0", [128, 1024], F32))
            peo1 = ce(nc.psum_tensor("peo1", [128, 1024], F32))

            s3 = Sched(nc, cctx, "c")
            s3.self_chain = self_chain
            T, V, S, GP, SY = "tensor", "vector", "scalar", "gpsimd", "sync"

            t_eb1 = s3.emit(SY, lambda e: e.dma_start(eb1_sb[:, :, :],
                                                      eb1_fm[:, :, :]), dma="dmisc")
            t_eb2 = s3.emit(GP, lambda e: e.dma_start(eb2_bf[:, :], eb2[:, :]),
                            dma="dmiscg")
            t_eb1 = ("dmisc", s3.cnt["dmisc"])
            t_eb2 = ("dmiscg", s3.cnt["dmiscg"])

            def load_ew1(eidx, after):
                return s3.emit(GP, lambda e, eidx=eidx: e.dma_start(
                    ew1_bf[:, :, :],
                    ew1[eidx, :, :].rearrange("(k p) h -> p k h", p=128)),
                    after=after, dma="dw1")

            def load_ew2(eidx, after):
                return s3.emit(GP, lambda e, eidx=eidx: e.dma_start(
                    ew2_bf[:, :, :],
                    ew2[eidx, :, :].rearrange("(k p) n -> p k n", p=128)),
                    after=after, dma="dw2")

            t_ew1 = load_ew1(0, [])
            t_dbg_eh = None
            t_ew2 = None
            ew1_lastuse = None
            ew2_lastuse = None
            t_out_final = []

            def emit_ln3(src_ap, after_tok):
                c_sx, c_sq, c_m, c_a, c_msq, c_b, c_s, c_rs = range(8)
                t_sx = s3.emit(V, lambda e: e.reduce_sum(
                    sc3[:, c_sx:c_sx + 1], src_ap, axis=AX.X), after=after_tok)
                t_sq = s3.emit(S, lambda e: e.activation(
                    tc[:, :], src_ap, AF.Square,
                    accum_out=sc3[:, c_sq:c_sq + 1]), after=after_tok)
                t_m = s3.emit(V, lambda e: e.tensor_scalar(
                    sc3[:, c_m:c_m + 1], sc3[:, c_sx:c_sx + 1], 1.0 / E, None,
                    ALU.mult), after=[t_sx])
                t_a = s3.emit(V, lambda e: e.tensor_scalar(
                    sc3[:, c_a:c_a + 1], sc3[:, c_sq:c_sq + 1], 1.0 / E, EPS,
                    ALU.mult, ALU.add), after=[t_sq])
                t_msq = s3.emit(V, lambda e: e.tensor_tensor(
                    sc3[:, c_msq:c_msq + 1], sc3[:, c_m:c_m + 1],
                    sc3[:, c_m:c_m + 1], ALU.mult), after=[t_m])
                t_b = s3.emit(V, lambda e: e.tensor_tensor(
                    sc3[:, c_b:c_b + 1], sc3[:, c_a:c_a + 1],
                    sc3[:, c_msq:c_msq + 1], ALU.subtract), after=[t_a, t_msq])
                t_s = s3.emit(S, lambda e: e.activation(
                    sc3[:, c_s:c_s + 1], sc3[:, c_b:c_b + 1], AF.Sqrt),
                    after=[t_b])
                t_rs = s3.emit(V, lambda e: e.reciprocal(
                    sc3[:, c_rs:c_rs + 1], sc3[:, c_s:c_s + 1]), after=[t_s])
                return ([t_rs], sc3[:, c_rs:c_rs + 1], sc3[:, c_m:c_m + 1])

            slot_guard = [None] * CPG
            peo_guard = [None, None]
            t_prev_out = None
            t_prev_od = None
            t_prev_dbg = None
            for g in range(NGRP):
                tok0 = g * TPG
                ch0 = g * CPG
                t_yinit = [None] * CPG
                pyfree = None
                for ci in range(CPG):
                    t_mm = None
                    for nh in range(2):
                        def f(e, ci=ci, nh=nh, ch0=ch0):
                            return e.matmul(
                                peo0[:, nh * 512:(nh + 1) * 512],
                                wTsb[:, (ch0 + ci) * C:(ch0 + ci + 1) * C],
                                eb2_bf[:, nh * 512:(nh + 1) * 512],
                                start=True, stop=True)
                        aft = []
                        if nh == 0:
                            aft = [t_eb2]
                            if pyfree is not None:
                                aft.append(pyfree)
                            elif peo_guard[0] is not None:
                                aft.append(peo_guard[0])
                        t_mm = s3.emit(T, f, after=aft)
                    aft_ev = [t_mm]
                    if slot_guard[ci] is not None:
                        aft_ev.append(slot_guard[ci])
                    t_yinit[ci] = s3.emit(S, lambda e, ci=ci: e.copy(
                        yac[:, ci, :], peo0[:, :]), after=aft_ev)
                    pyfree = t_yinit[ci]
                peo_free = [pyfree, pyfree]
                peh_free = [None, None]
                t_yacc_last = t_yinit

                elist = list(range(NEXP)) if g == 0 else list(range(NEXP - 1, -1, -1))
                for ei, eidx in enumerate(elist):
                    first = (g == 0 and ei == 0)
                    if not (g == 1 and ei == 0):
                        t_ew2 = load_ew2(eidx, [ew2_lastuse] if ew2_lastuse else [])
                    for t4 in range(TPG // 512):
                        tstart = tok0 + t4 * 512
                        t_ehev = [None] * KT2
                        for m in range(KT2):
                            ph = (peh0, peh1)[m % 2]
                            t_mm = None
                            for k in range(KT):
                                def f(e, m=m, k=k, ph=ph, tstart=tstart):
                                    return e.matmul(
                                        ph[:, :], ew1_bf[:, k, m * 128:(m + 1) * 128],
                                        tT[:, k, tstart:tstart + 512],
                                        start=(k == 0), stop=(k == KT - 1))
                                aft = []
                                if k == 0:
                                    aft = [t_ew1]
                                    if peh_free[m % 2] is not None:
                                        aft.append(peh_free[m % 2])
                                t_mm = s3.emit(T, f, after=aft)
                            t_ehev[m] = s3.emit(S, lambda e, m=m, ph=ph, eidx=eidx:
                                                e.activation(
                                ehb[:, m, :], ph[:, :], AF.Silu,
                                bias=eb1_sb[:, eidx, m:m + 1]),
                                after=[t_mm, t_eb1] +
                                ([t_dbg_eh] if t_dbg_eh else []))
                            peh_free[m % 2] = t_ehev[m]
                        ew1_lastuse = t_mm
                        if debug and g == 0 and ei == 0 and t4 == 0:
                            t_dbg_eh = s3.emit(SY, lambda e: e.dma_start(
                                dbg_eh[:, :, :], ehb[:, :, :]),
                                after=[t_ehev[KT2 - 1]], dma="ddbg")
                        if t4 == TPG // 512 - 1:
                            nxt = None
                            if ei + 1 < NEXP:
                                nxt = elist[ei + 1]
                            elif g == 0:
                                nxt = None
                            if nxt is not None:
                                t_ew1 = load_ew1(nxt, [ew1_lastuse])
                        for sub in range(4):
                            ci = t4 * 4 + sub
                            po = (peo0, peo1)[sub % 2]
                            t_mm = None
                            for nh in range(2):
                                for k in range(KT2):
                                    def f(e, sub=sub, nh=nh, k=k, po=po):
                                        return e.matmul(
                                            po[:, nh * 512:(nh + 1) * 512],
                                            ehb[:, k, sub * 128:(sub + 1) * 128],
                                            ew2_bf[:, k, nh * 512:(nh + 1) * 512],
                                            start=(k == 0), stop=(k == KT2 - 1))
                                    aft = []
                                    if nh == 0 and k == 0:
                                        aft = [t_ehev[KT2 - 1], t_ew2]
                                        if peo_free[sub % 2] is not None:
                                            aft.append(peo_free[sub % 2])
                                    t_mm = s3.emit(T, f, after=aft)
                            ew2_lastuse = t_mm
                            def facc(e, ci=ci, po=po, eidx=eidx, ch0=ch0):
                                return e.scalar_tensor_tensor(
                                    yac[:, ci, :], po[:, :],
                                    wsb[:, ch0 + ci, eidx:eidx + 1],
                                    yac[:, ci, :], ALU.mult, ALU.add)
                            t_yacc_last[ci] = s3.emit(V, facc,
                                                      after=[t_mm, t_yacc_last[ci]])
                            peo_free[sub % 2] = t_yacc_last[ci]
                    if debug and g == 0 and ei == 0 and t4 == TPG // 512 - 1:
                        t_dbgyac = s3.emit(SY, lambda e: e.dma_start(
                            dbg_yac[:, :, :], yac[:, :, :]),
                            after=[t_yacc_last[i] for i in range(CPG)],
                            dma="ddbg")
                        for i in range(CPG):
                            t_yacc_last[i] = t_dbgyac
                for ci in range(CPG):
                    cg = ch0 + ci
                    aft_in = ([t_prev_out] if t_prev_out else []) + (
                        [t_prev_od] if t_prev_od else [])
                    t_x1c = s3.emit(SY, lambda e, cg=cg: e.dma_start(
                        x1c[:, :], x1s[cg * C:(cg + 1) * C, :]),
                        after=aft_in, dma="dpcx")
                    t_tc = s3.emit(SY, lambda e, cg=cg: e.dma_start(
                        tc[:, :], ts_[cg * C:(cg + 1) * C, :]),
                        after=aft_in, dma="dpct")
                    t_y = s3.emit(V, lambda e, ci=ci: e.tensor_tensor(
                        yf[:, :], yac[:, ci, :], tc[:, :], ALU.add),
                        after=[t_tc, t_yacc_last[ci]] +
                              ([t_prev_dbg] if t_prev_dbg else []))
                    slot_guard[ci] = t_y
                    if debug:
                        t_prev_dbg = s3.emit(SY, lambda e, cg=cg: e.dma_start(
                            dbg_y[cg * C:(cg + 1) * C, :], yf[:, :]),
                            after=[t_y], dma="ddbg")
                    ln_toks, rs_ap, m_ap = emit_ln3(yf[:, :], [t_y])
                    t_yln = s3.emit(V, lambda e, rs_ap=rs_ap, m_ap=m_ap:
                                    e.tensor_scalar(ylnb[:, :], yf[:, :],
                                                    m_ap, rs_ap,
                                                    ALU.subtract, ALU.mult),
                                    after=ln_toks)
                    t_ob = s3.emit(V, lambda e: e.tensor_tensor(
                        x1c[:, :], ylnb[:, :], x1c[:, :], ALU.add),
                        after=[t_yln, t_x1c])
                    t_od = s3.emit(SY, lambda e, cg=cg: e.dma_start(
                        out[cg * C:(cg + 1) * C, :], x1c[:, :]),
                        after=[t_ob], dma="dout")
                    t_out_final.append(t_od)
                    t_prev_out = t_ob
                    t_prev_od = t_od
                peo_guard = list(peo_free)

            with nc.Block() as blockC:
                s3.flush(blockC, final_tokens=[("dout", s3.cnt["dout"])] + (
                    [("ddbg", s3.cnt.get("ddbg", 0))] if debug else []))

    return nc


_NC_CACHE = None


def _get_nc():
    global _NC_CACHE
    if _NC_CACHE is None:
        _NC_CACHE = build()
    return _NC_CACHE


def kernel(**inputs):
    cos_rep, sin_rep, triu, ident = _host_tables()
    f32 = lambda a: np.ascontiguousarray(np.asarray(a, dtype=np.float32))
    g = {k: f32(v) for k, v in inputs.items()}
    base = {
        "Wq": g["L0_attn_Wq"], "Wk": g["L0_attn_Wk"], "Wv": g["L0_attn_Wv"],
        "Wo": g["L0_attn_Wo"], "Wg1": g["L0_attn_Wg1"], "Wg2": g["L0_attn_Wg2"],
        "gamma_fm": f32(g["L0_attn_gamma"].reshape(KT, 128).T),
        "beta_fm": f32(g["L0_attn_beta"].reshape(KT, 128).T),
        "rw1": g["L0_ffn_rw1"],
        "rb1_fm": f32(g["L0_ffn_rb1"].reshape(KT, 128).T),
        "rw2": g["L0_ffn_rw2"],
        "rb2_b": f32(np.broadcast_to(g["L0_ffn_rb2"], (128, NEXP))),
        "ew1": g["L0_ffn_ew1"],
        "eb1_fm": f32(g["L0_ffn_eb1"].reshape(NEXP, KT2, 128).transpose(2, 0, 1)),
        "ew2": g["L0_ffn_ew2"],
        "eb2": g["L0_ffn_eb2"],
        "cosr": cos_rep, "sinr": sin_rep, "triu": triu, "ident": ident,
    }
    xfull = g["L0_attn_x"]
    in_maps = []
    for b in range(NB):
        m = dict(base)
        m["x"] = np.ascontiguousarray(xfull[b])
        in_maps.append(m)
    nc = _get_nc()
    res = run_bass_kernel_spmd(nc, in_maps, core_ids=list(range(NB)))
    outp = np.stack([np.asarray(res.results[b]["out"], dtype=np.float32)
                     for b in range(NB)], axis=0)
    return outp


# version 12
# speedup vs baseline: 1.3101x; 1.0502x over previous
import numpy as np
import ml_dtypes
from contextlib import ExitStack

import concourse.bass as bass
import concourse.mybir as mybir
from concourse.bass_utils import run_bass_kernel_spmd

F32 = mybir.dt.float32
BF16 = mybir.dt.bfloat16
I32 = mybir.dt.int32
AF = mybir.ActivationFunctionType
ALU = mybir.AluOpType
AX = mybir.AxisListType

NB, NTOK, E = 8, 4096, 1024
H, D = 16, 64
C = 128
NCH = NTOK // C
KT = E // 128
NEXP, DH = 8, 2048
KT2 = DH // 128
EPS = 1e-5
RBASE = 10000.0
NGRP = 2
CPG = NCH // NGRP
TPG = NTOK // NGRP
BIG = 1.0e30

ENGS = ("sync", "gpsimd", "vector", "scalar", "tensor")


class Sched:

    def __init__(self, nc, ctx, tag):
        self.nc = nc
        self.ctx = ctx
        self.tag = tag
        self.sem = {}
        self.cnt = {}
        for e in ENGS:
            self.sem[e] = ctx.enter_context(nc.semaphore(f"{tag}_{e}"))
            self.cnt[e] = 0
        self.recs = {e: [] for e in ENGS}
        self.pend_bar = {e: False for e in ENGS}
        self.bar_snapshot = None

    def _dma_sem(self, name):
        if name not in self.sem:
            self.sem[name] = self.ctx.enter_context(
                self.nc.semaphore(f"{self.tag}_{name}"))
            self.cnt[name] = 0
        return name

    SELF_CHAIN_HW = ("scalar", "vector", "gpsimd")
    SELF_CHAIN_SIM = ("vector", "scalar", "gpsimd")
    self_chain = False

    def emit(self, eng, fn, after=(), dma=None):
        chain = self.SELF_CHAIN_SIM if self.self_chain else self.SELF_CHAIN_HW
        waits = []
        if eng in chain and self.cnt[eng] > 0:
            waits.append((eng, self.cnt[eng]))
        if self.pend_bar[eng] and self.bar_snapshot:
            for s, v in self.bar_snapshot.items():
                if s != eng and v > 0:
                    waits.append((s, v))
            self.pend_bar[eng] = False
        for s, v in after:
            if s == eng:
                continue
            waits.append((s, v))
        if dma is not None:
            semname = self._dma_sem(dma)
            inc = 16
        else:
            semname = eng
            inc = 1
        self.cnt[semname] += inc
        tok = (semname, self.cnt[semname])
        self.recs[eng].append((waits, fn, semname, inc))
        return tok

    def barrier(self):
        self.bar_snapshot = {e: self.cnt[e] for e in ENGS}
        for e in ENGS:
            self.pend_bar[e] = True

    def flush(self, block, final_tokens=()):
        recs = self.recs
        sems = self.sem
        waited_vals = {}
        for ename in ENGS:
            for waits, fn, semname, inc in recs[ename]:
                for sname, v in waits:
                    waited_vals.setdefault(sname, set()).add(v)
        for sname, v in final_tokens:
            waited_vals.setdefault(sname, set()).add(v)
        remap = {}
        for ename in ENGS:
            vals = sorted(waited_vals.get(ename, ()))
            remap[ename] = ({v: i + 1 for i, v in enumerate(vals)}, set(vals))

        def map_tok(sname, v):
            if sname in remap:
                return remap[sname][0][v]
            return v

        def make_body(ename, extra_final):
            inc_set = remap[ename][1]

            def body(eng):
                waited = {}
                dense = 0
                for waits, fn, semname, inc in recs[ename]:
                    for sname, v in waits:
                        mv = map_tok(sname, v)
                        if waited.get(sname, 0) >= mv:
                            continue
                        eng.wait_ge(sems[sname], mv)
                        waited[sname] = mv
                    inst = fn(eng)
                    if semname == ename:
                        dense += 1
                        if dense in inc_set:
                            inst.then_inc(sems[semname], 1)
                    else:
                        inst.then_inc(sems[semname], inc)
                for sname, v in extra_final:
                    mv = map_tok(sname, v)
                    if waited.get(sname, 0) >= mv:
                        continue
                    eng.wait_ge(sems[sname], mv)
                    waited[sname] = mv
            return body

        block.sync(make_body("sync", list(final_tokens)))
        block.gpsimd(make_body("gpsimd", []))
        block.vector(make_body("vector", []))
        block.scalar(make_body("scalar", []))
        block.tensor(make_body("tensor", []))


def _host_tables():
    d2 = np.arange(D // 2, dtype=np.float64)
    inv = RBASE ** (-(d2 * 2.0) / D)
    pos = np.arange(NTOK, dtype=np.float64)
    ang = pos[:, None] * inv[None, :]
    cos = np.cos(ang).astype(np.float32)
    sin = np.sin(ang).astype(np.float32)
    cos_rep = np.ascontiguousarray(np.tile(cos, (1, H)))
    sin_rep = np.ascontiguousarray(np.tile(sin, (1, H)))
    triu = np.ascontiguousarray(np.tile(
        np.triu(np.ones((C, C), dtype=np.float32)), (1, 2))).astype(
        ml_dtypes.bfloat16)
    ident = np.eye(C, dtype=np.float32)
    return cos_rep, sin_rep, triu, ident


def build(debug=False, n_chunks=NCH, phases="ABC", stage=999,
          self_chain=False):
    nc = bass.Bass("TRN2", target_bir_lowering=False)
    dt = nc.dram_tensor
    x = dt("x", [NTOK, E], F32, kind="ExternalInput")
    Wq = dt("Wq", [E, E], F32, kind="ExternalInput")
    Wk = dt("Wk", [E, E], F32, kind="ExternalInput")
    Wv = dt("Wv", [E, E], F32, kind="ExternalInput")
    Wo = dt("Wo", [E, E], F32, kind="ExternalInput")
    Wg1 = dt("Wg1", [E, D], F32, kind="ExternalInput")
    Wg2 = dt("Wg2", [D, E], F32, kind="ExternalInput")
    gamma_fm = dt("gamma_fm", [128, KT], F32, kind="ExternalInput")
    beta_fm = dt("beta_fm", [128, KT], F32, kind="ExternalInput")
    rw1 = dt("rw1", [E, E], F32, kind="ExternalInput")
    rb1_fm = dt("rb1_fm", [128, KT], F32, kind="ExternalInput")
    rw2 = dt("rw2", [E, NEXP], F32, kind="ExternalInput")
    rb2_b = dt("rb2_b", [128, NEXP], F32, kind="ExternalInput")
    ew1 = dt("ew1", [NEXP, E, DH], F32, kind="ExternalInput")
    eb1_fm = dt("eb1_fm", [128, NEXP, KT2], F32, kind="ExternalInput")
    ew2 = dt("ew2", [NEXP, DH, E], F32, kind="ExternalInput")
    eb2 = dt("eb2", [NEXP, E], F32, kind="ExternalInput")
    cosr = dt("cosr", [NTOK, H * 32], F32, kind="ExternalInput")
    sinr = dt("sinr", [NTOK, H * 32], F32, kind="ExternalInput")
    triu_in = dt("triu", [C, 2 * C], BF16, kind="ExternalInput")
    ident_in = dt("ident", [C, C], F32, kind="ExternalInput")
    out = dt("out", [NTOK, E], F32, kind="ExternalOutput")
    skind = "ExternalOutput" if debug else "Internal"
    x1s = dt("x1s", [NTOK, E], F32, kind=skind)
    ts_ = dt("ts", [NTOK, E], BF16, kind=skind)
    dbg_w = dt("dbg_w", [NCH, 128, NEXP], F32, kind=skind) if debug else None
    dbg_y = dt("dbg_y", [NTOK, E], F32, kind=skind) if debug else None
    dbg_eh = dt("dbg_eh", [128, KT2, 512], BF16, kind=skind) if debug else None
    dbg_yac = dt("dbg_yac", [128, CPG, E], BF16, kind=skind) if debug else None

    with ExitStack() as octx:
        oe = octx.enter_context
        tT = oe(nc.sbuf_tensor("tT", [128, KT, NTOK], BF16))
        wsb = oe(nc.sbuf_tensor("wsb", [128, NCH, NEXP], F32))
        wTsb = oe(nc.sbuf_tensor("wTsb", [NEXP, NTOK], BF16))
        ones_row = oe(nc.sbuf_tensor("ones_row", [1, 128], BF16))
        bWo_row = oe(nc.sbuf_tensor("bWo_row", [1, E], BF16))
        identbf = oe(nc.sbuf_tensor("identbf", [C, C], BF16))
        trium = oe(nc.sbuf_tensor("trium", [C, 2 * C], BF16))

        with ExitStack() as actx:
            ae = actx.enter_context
            Wq_bf = ae(nc.sbuf_tensor("Wq_bf", [128, KT, E], BF16))
            Wk_bf = ae(nc.sbuf_tensor("Wk_bf", [128, KT, E], BF16))
            Wv_bf = ae(nc.sbuf_tensor("Wv_bf", [128, KT, E], BF16))
            Wo_bf = ae(nc.sbuf_tensor("Wo_bf", [128, KT, E], BF16))
            Wg1_bf = ae(nc.sbuf_tensor("Wg1_bf", [128, KT, D], BF16))
            Wg2_bf = ae(nc.sbuf_tensor("Wg2_bf", [D, E], BF16))
            gamma_sb = ae(nc.sbuf_tensor("gamma_sb", [128, KT], F32))
            beta_sb = ae(nc.sbuf_tensor("beta_sb", [128, KT], F32))
            beta_bf = ae(nc.sbuf_tensor("beta_bf", [128, KT], BF16))
            cosb = ae(nc.sbuf_tensor("cosb", [128, 512], F32))
            sinb = ae(nc.sbuf_tensor("sinb", [128, 512], F32))
            xb = ae(nc.sbuf_tensor("xb", [128, 2, E], F32))
            ln1b = ae(nc.sbuf_tensor("ln1b", [128, 2, E], BF16))
            ln1T = ae(nc.sbuf_tensor("ln1T", [128, KT, C], BF16))
            qs = ae(nc.sbuf_tensor("qs", [128, E], F32))
            ks = ae(nc.sbuf_tensor("ks", [128, E], F32))
            ao_sb = ae(nc.sbuf_tensor("ao_sb", [128, E], BF16))
            rt2 = ae(nc.sbuf_tensor("rt2", [128, 512], F32))
            kt2 = rt2
            qro = ae(nc.sbuf_tensor("qro", [128, E], BF16))
            kro = ae(nc.sbuf_tensor("kro", [128, E], BF16))
            vb = ae(nc.sbuf_tensor("vb", [128, E], BF16))
            qT = ae(nc.sbuf_tensor("qT", [128, KT, C], BF16))
            kTt = ae(nc.sbuf_tensor("kTt", [128, KT, C], BF16))
            g1b = ae(nc.sbuf_tensor("g1b", [D, C], BF16))
            th = ae(nc.sbuf_tensor("th", [128, E], BF16))
            eT = ae(nc.sbuf_tensor("eT", [128, 2, 2 * C], BF16))
            Sf = ae(nc.sbuf_tensor("Sf", [128, 512], F32))
            Sbf = ae(nc.sbuf_tensor("Sbf", [128, 512], BF16))
            sb_o = ae(nc.sbuf_tensor("sb_o", [128, E], BF16))
            x1b = ae(nc.sbuf_tensor("x1b", [128, E], F32))
            dump = ae(nc.sbuf_tensor("dump", [128, E], BF16))
            og = ae(nc.sbuf_tensor("og", [128, 2, E], BF16))
            scol = ae(nc.sbuf_tensor("scol", [128, 32], F32))
            scol2 = ae(nc.sbuf_tensor("scol2", [128, 32], F32))
            pbig0 = ae(nc.psum_tensor("pbig0", [128, 1024], F32))
            pbig1 = ae(nc.psum_tensor("pbig1", [128, 1024], F32))
            ptr = ae(nc.psum_tensor("ptr", [128, KT, C], BF16))
            pS = ae(nc.psum_tensor("pS", [128, 512], F32))
            pm6 = ae(nc.psum_tensor("pm6", [128, 512], F32))
            pm7 = ae(nc.psum_tensor("pm7", [128, 512], F32))

            s = Sched(nc, actx, "a")
            s.self_chain = self_chain
            T, V, S, GP, SY = "tensor", "vector", "scalar", "gpsimd", "sync"

            t_wq = s.emit(GP, lambda e: e.dma_start(
                Wq_bf[:, :, :], Wq[:, :].rearrange("(k p) n -> p k n", p=128)), dma="dw")
            t_wk = s.emit(GP, lambda e: e.dma_start(
                Wk_bf[:, :, :], Wk[:, :].rearrange("(k p) n -> p k n", p=128)), dma="dw")
            t_wv = s.emit(GP, lambda e: e.dma_start(
                Wv_bf[:, :, :], Wv[:, :].rearrange("(k p) n -> p k n", p=128)), dma="dw")
            t_wo = s.emit(GP, lambda e: e.dma_start(
                Wo_bf[:, :, :], Wo[:, :].rearrange("(k p) n -> p k n", p=128)), dma="dw")
            t_wg1 = s.emit(GP, lambda e: e.dma_start(
                Wg1_bf[:, :, :], Wg1[:, :].rearrange("(k p) n -> p k n", p=128)), dma="dw")
            t_wg2 = s.emit(GP, lambda e: e.dma_start(Wg2_bf[:, :], Wg2[:, :]), dma="dw")
            t_idb = s.emit(GP, lambda e: e.dma_start(identbf[:, :], ident_in[:, :]), dma="dw")
            tok_wall = ("dw", s.cnt["dw"])
            t_wq = t_wk = t_wv = t_wo = t_wg1 = t_wg2 = t_idb = tok_wall
            t_gam = s.emit(SY, lambda e: e.dma_start(gamma_sb[:, :], gamma_fm[:, :]), dma="dmisc")
            t_bet = s.emit(SY, lambda e: e.dma_start(beta_sb[:, :], beta_fm[:, :]), dma="dmisc")
            t_tri = s.emit(SY, lambda e: e.dma_start(trium[:, :], triu_in[:, :]), dma="dmisc")
            tok_misc = ("dmisc", s.cnt["dmisc"])
            t_gam = t_bet = t_tri = tok_misc
            t_bbf = s.emit(V, lambda e: e.tensor_copy(beta_bf[:, :], beta_sb[:, :]),
                           after=[t_bet])
            t_ones = s.emit(V, lambda e: e.memset(ones_row[:, :], 1.0))
            t_sf0 = s.emit(V, lambda e: e.memset(Sf[:, :], 0.0))
            t_sbf0 = s.emit(V, lambda e: e.memset(Sbf[:, :], 0.0), after=[t_sf0])
            t_bwo_mm = None
            t_bwo_ev = None
            for half in range(2):
                for k in range(KT):
                    def f(e, half=half, k=k):
                        return e.matmul(pm7[0:1, 0:512],
                                        beta_bf[:, k:k + 1],
                                        Wo_bf[:, k, half * 512:(half + 1) * 512],
                                        start=(k == 0), stop=(k == KT - 1))
                    aft = []
                    if k == 0:
                        aft = [t_wo, t_bbf]
                        if t_bwo_ev is not None:
                            aft.append(t_bwo_ev)
                    t_bwo_mm = s.emit(T, f, after=aft)
                t_bwo_ev = s.emit(S, lambda e, half=half: e.copy(
                    bWo_row[0:1, half * 512:(half + 1) * 512], pm7[0:1, 0:512]),
                    after=[t_bwo_mm])
            t_wog = []
            for k in range(KT):
                t_wog.append(s.emit(V, lambda e, k=k: e.tensor_scalar(
                    Wo_bf[:, k, :], Wo_bf[:, k, :], gamma_sb[:, k:k + 1], None,
                    ALU.mult), after=[t_bwo_ev, t_gam, t_wo]))

            tok_x = [None] * (NCH + 2)
            tok_x[0] = s.emit(SY, lambda e: e.dma_start(xb[:, 0, :], x[0:C, :]), dma="dx0")
            if n_chunks > 1:
                tok_x[1] = s.emit(SY, lambda e: e.dma_start(
                    xb[:, 1, :], x[C:2 * C, :]), dma="dx1")
            t_cos = s.emit(SY, lambda e: e.dma_start(cosb[:, :], cosr[0:C, :]), dma="dtab")
            t_sin = s.emit(SY, lambda e: e.dma_start(sinb[:, :], sinr[0:C, :]), dma="dtab")
            t_cos = t_sin = ("dtab", s.cnt["dtab"])

            prev = {
                "x1_store": None,
                "t_store": None,
                "ptr_free": t_idb,
                "pbig0_free": t_bwo_ev,
                "pbig1_free": t_bwo_ev,
                "Sbf_ready": t_sbf0,
                "Sf_ready": t_sf0,
                "pS_free": t_sf0,
                "rope_done_v": t_ones,
                "rope_done_g": t_idb,
                "x_free": [t_bbf],
                "tT_slot": None,
            }

            def emit_ln_stats(colbuf, src_ap, after_tok, psum_src=False):
                c_sx, c_sq, c_m, c_a, c_msq, c_b, c_s, c_rs = range(8)
                t_sx = s.emit(V, lambda e: e.reduce_sum(
                    colbuf[:, c_sx:c_sx + 1], src_ap, axis=AX.X), after=after_tok)
                t_sq = s.emit(S, lambda e: e.activation(
                    dump[:, :], src_ap, AF.Square,
                    accum_out=colbuf[:, c_sq:c_sq + 1]),
                    after=after_tok + ([t_sx] if psum_src else []))
                t_m = s.emit(V, lambda e: e.tensor_scalar(
                    colbuf[:, c_m:c_m + 1], colbuf[:, c_sx:c_sx + 1], 1.0 / E,
                    None, ALU.mult), after=[t_sx])
                t_a = s.emit(V, lambda e: e.tensor_scalar(
                    colbuf[:, c_a:c_a + 1], colbuf[:, c_sq:c_sq + 1], 1.0 / E,
                    EPS, ALU.mult, ALU.add), after=[t_sq])
                t_msq = s.emit(V, lambda e: e.tensor_tensor(
                    colbuf[:, c_msq:c_msq + 1], colbuf[:, c_m:c_m + 1],
                    colbuf[:, c_m:c_m + 1], ALU.mult), after=[t_m])
                t_b = s.emit(V, lambda e: e.tensor_tensor(
                    colbuf[:, c_b:c_b + 1], colbuf[:, c_a:c_a + 1],
                    colbuf[:, c_msq:c_msq + 1], ALU.subtract),
                    after=[t_a, t_msq])
                t_s = s.emit(S, lambda e: e.activation(
                    colbuf[:, c_s:c_s + 1], colbuf[:, c_b:c_b + 1], AF.Sqrt),
                    after=[t_b])
                t_rs = s.emit(V, lambda e: e.reciprocal(
                    colbuf[:, c_rs:c_rs + 1], colbuf[:, c_s:c_s + 1]),
                    after=[t_s])
                return ([t_rs], colbuf[:, c_rs:c_rs + 1],
                        colbuf[:, c_m:c_m + 1], t_sq)

            cos_v = cosb[:, :].rearrange("p (h d) -> p h d", h=H)
            sin_v = sinb[:, :].rearrange("p (h d) -> p h d", h=H)

            def emit_rope(eng, src, scr2, dst, after_toks):
                sv = src[:, :].rearrange("p (h d) -> p h d", h=H)
                dv = dst[:, :].rearrange("p (h d) -> p h d", h=H)
                s2 = scr2[:, :].rearrange("p (h d) -> p h d", h=H)
                x1v, x2v = sv[:, :, 0:32], sv[:, :, 32:64]
                d1, d2 = dv[:, :, 0:32], dv[:, :, 32:64]
                s.emit(eng, lambda e: e.tensor_tensor(
                    s2[:, :, :], x2v, sin_v, ALU.mult), after=after_toks)
                s.emit(eng, lambda e: e.tensor_tensor(
                    d1, x1v, cos_v, ALU.mult))
                s.emit(eng, lambda e: e.tensor_tensor(
                    d1, d1, s2[:, :, :], ALU.subtract))
                s.emit(eng, lambda e: e.tensor_tensor(
                    s2[:, :, :], x2v, cos_v, ALU.mult))
                s.emit(eng, lambda e: e.tensor_tensor(
                    d2, x1v, sin_v, ALU.mult))
                t6 = s.emit(eng, lambda e: e.tensor_tensor(
                    d2, d2, s2[:, :, :], ALU.add))
                return t6

            prev.update({
                "scol_rd": None, "scol2_rd": None, "ln1b_rd": [None, None],
                "x_sq": [None, None],
                "ln1T_rd": None, "qs_rd": None, "ks_rd": None,
                "qro_rd": None, "kro_rd": [], "vb_rd": [],
                "qT_rd": [], "kT_rd": [], "th_rd": None,
                "sb_o_rd": None, "outT_rd": None, "og_rd2": [None, None],
                "t_mask": [None, None],
            })

            def emit_pre(c):
                sl = c % 2
                xc = xb[:, sl, :]
                ln_toks, rs_ap, m_ap, t_lnsq = emit_ln_stats(
                    scol, xc, [tok_x[c]] +
                    ([prev["scol_rd"]] if prev["scol_rd"] else []))
                t_ln1 = s.emit(V, lambda e, rs_ap=rs_ap, m_ap=m_ap, xc=xc,
                               sl=sl:
                               e.tensor_scalar(ln1b[:, sl, :], xc, m_ap, rs_ap,
                                               ALU.subtract, ALU.mult),
                               after=ln_toks +
                               ([prev["ln1b_rd"][sl]]
                                if prev["ln1b_rd"][sl] else []))
                prev["scol_rd"] = t_ln1
                prev["x_sq"][sl] = t_lnsq
                tok_ln1[c] = t_ln1

            def emit_front(c):
                f = {}
                sl = c % 2
                if c > 0:
                    s.emit(SY, lambda e, c=c: e.dma_start(
                        cosb[:, :], cosr[c * C:(c + 1) * C, :]),
                        after=[prev["rope_done_v"], prev["rope_done_g"]],
                        dma="dtab")
                    s.emit(SY, lambda e, c=c: e.dma_start(
                        sinb[:, :], sinr[c * C:(c + 1) * C, :]),
                        after=[prev["rope_done_v"], prev["rope_done_g"]],
                        dma="dtab")
                    f["tab"] = ("dtab", s.cnt["dtab"])
                else:
                    f["tab"] = t_cos
                xc = xb[:, sl, :]
                f["xc"] = xc
                f["c"] = c
                t_ln1 = tok_ln1[c]
                t_tr = None
                for j in range(KT):
                    t_tr = s.emit(T, lambda e, j=j, sl=sl: e.transpose(
                        ptr[:, j, :], ln1b[:, sl, j * 128:(j + 1) * 128],
                        identbf[:, :]),
                        after=[t_ln1, prev["ptr_free"], t_idb] if j == 0 else [])
                prev["ln1b_rd"][sl] = t_tr
                t_ln1T = s.emit(S, lambda e: e.copy(
                    ln1T[:, :, :], ptr[:, :, :]),
                    after=[t_tr] +
                    ([prev["ln1T_rd"]] if prev["ln1T_rd"] else []))
                prev["ptr_free"] = t_ln1T
                t_qmm = None
                for nh in range(2):
                    for k in range(KT):
                        def fq(e, nh=nh, k=k):
                            return e.matmul(pbig0[:, nh * 512:(nh + 1) * 512],
                                            ln1T[:, k, :],
                                            Wq_bf[:, k, nh * 512:(nh + 1) * 512],
                                            start=(k == 0), stop=(k == KT - 1))
                        t_qmm = s.emit(T, fq,
                                       after=[t_ln1T, prev["pbig0_free"], t_wq]
                                       if (nh == 0 and k == 0) else [])
                t_kmm = None
                for nh in range(2):
                    for k in range(KT):
                        def fk(e, nh=nh, k=k):
                            return e.matmul(pbig1[:, nh * 512:(nh + 1) * 512],
                                            ln1T[:, k, :],
                                            Wk_bf[:, k, nh * 512:(nh + 1) * 512],
                                            start=(k == 0), stop=(k == KT - 1))
                        t_kmm = s.emit(T, fk,
                                       after=[t_ln1T, prev["pbig1_free"], t_wk]
                                       if (nh == 0 and k == 0) else [])
                t_sq_ = s.emit(S, lambda e: e.activation(qs[:, :], pbig0[:, :],
                                                         AF.Silu),
                               after=[t_qmm] +
                               ([prev["qs_rd"]] if prev["qs_rd"] else []))
                t_sk_ = s.emit(S, lambda e: e.activation(ks[:, :], pbig1[:, :],
                                                         AF.Silu),
                               after=[t_kmm] +
                               ([prev["ks_rd"]] if prev["ks_rd"] else []))
                t_vmm = None
                for nh in range(2):
                    for k in range(KT):
                        def fv(e, nh=nh, k=k):
                            return e.matmul(pbig0[:, nh * 512:(nh + 1) * 512],
                                            ln1T[:, k, :],
                                            Wv_bf[:, k, nh * 512:(nh + 1) * 512],
                                            start=(k == 0), stop=(k == KT - 1))
                        t_vmm = s.emit(T, fv, after=[t_sq_, t_wv]
                                       if (nh == 0 and k == 0) else [])
                t_g1 = None
                for k in range(KT):
                    def fg(e, k=k):
                        return e.matmul(pm6[0:64, 0:128], Wg1_bf[:, k, :],
                                        ln1T[:, k, :],
                                        start=(k == 0), stop=(k == KT - 1))
                    aft = []
                    if k == 0:
                        aft = [t_ln1T, t_wg1]
                        if prev["t_mask"][0] is not None:
                            aft.append(prev["t_mask"][0])
                    t_g1 = s.emit(T, fg, after=aft)
                prev["ln1T_rd"] = t_g1
                t_g1e = s.emit(S, lambda e: e.copy(g1b[:, :], pm6[0:64, 0:128]),
                               after=[t_g1])
                t_g2 = None
                for nh in range(2):
                    def fg2(e, nh=nh):
                        return e.matmul(pbig1[:, nh * 512:(nh + 1) * 512],
                                        g1b[:, :],
                                        Wg2_bf[:, nh * 512:(nh + 1) * 512],
                                        start=True, stop=True)
                    t_g2 = s.emit(T, fg2, after=[t_sk_, t_g1e, t_wg2]
                                  if nh == 0 else [])
                t_th = s.emit(S, lambda e: e.activation(th[:, :], pbig1[:, :],
                                                        AF.Tanh, scale=0.5),
                              after=[t_g2] +
                              ([prev["th_rd"]] if prev["th_rd"] else []))
                t_vb = s.emit(V, lambda e: e.tensor_copy(vb[:, :], pbig0[:, :]),
                              after=[t_vmm] + prev["vb_rd"])
                f["t_vb"] = t_vb
                t_ropeq = emit_rope(GP, qs, rt2, qro,
                                    [f["tab"], t_sq_] +
                                    ([prev["qro_rd"]] if prev["qro_rd"] else []))
                t_ropek = emit_rope(GP, ks, kt2, kro,
                                    [f["tab"], t_sk_] + prev["kro_rd"])
                prev["rope_done_v"] = t_ropeq
                prev["rope_done_g"] = t_ropek
                prev["qs_rd"] = t_ropeq
                prev["ks_rd"] = t_ropek
                t_gate = s.emit(V, lambda e: e.tensor_scalar(
                    th[:, :], th[:, :], 0.5, 0.5, ALU.mult, ALU.add),
                    after=[t_th])
                t_trq = None
                for j in range(KT):
                    t_trq = s.emit(T, lambda e, j=j: e.transpose(
                        ptr[:, j, :], qro[:, j * 128:(j + 1) * 128], identbf[:, :]),
                        after=[t_ropeq, prev["ptr_free"]] if j == 0 else [])
                prev["qro_rd"] = t_trq
                t_qT = s.emit(S, lambda e: e.copy(qT[:, :, :], ptr[:, :, :]),
                              after=[t_trq] + prev["qT_rd"])
                prev["ptr_free"] = t_qT
                t_trk = None
                for j in range(KT):
                    t_trk = s.emit(T, lambda e, j=j: e.transpose(
                        ptr[:, j, :], kro[:, j * 128:(j + 1) * 128], identbf[:, :]),
                        after=[t_ropek, prev["ptr_free"]] if j == 0 else [])
                t_kT = s.emit(S, lambda e: e.copy(kTt[:, :, :], ptr[:, :, :]),
                              after=[t_trk] + prev["kT_rd"])
                prev["ptr_free"] = t_kT
                t_maskp = prev["t_mask"]
                t_intra_last = None
                t_inter_last = None
                t_state_last = None
                for h in range(H):
                    po = 64 * (h % 2)
                    tile = h // 2
                    slot = h % 2
                    pe_slot = (pm6, pm7)[slot]
                    t_en = s.emit(T, lambda e, po=po, tile=tile, pe_slot=pe_slot:
                                  e.matmul(pe_slot[:, 128:256],
                                           kTt[po:po + 64, tile, :],
                                           qT[po:po + 64, tile, :],
                                           start=True, stop=True),
                                  after=([t_qT, t_kT, t_th] if h == 0 else []) +
                                        ([t_maskp[slot]] if t_maskp[slot] else []))
                    t_maskp[slot] = s.emit(V, lambda e, slot=slot, pe_slot=pe_slot:
                                          e.tensor_tensor(
                                              eT[:, slot, 0:128],
                                              pe_slot[:, 128:256],
                                              trium[:, 0:128], ALU.mult),
                                          after=[t_en, t_tri])
                    t_intra_last = s.emit(T, lambda e, h=h, slot=slot: e.matmul(
                        pbig1[:, h * 64:(h + 1) * 64], eT[:, slot, 0:128],
                        vb[:, h * 64:(h + 1) * 64], start=True, stop=False),
                        after=[t_maskp[slot], t_vb] + ([t_th] if h == 0 else []))
                    t_inter_last = s.emit(T, lambda e, h=h, po=po, tile=tile:
                                          e.matmul(
                        pbig1[:, h * 64:(h + 1) * 64],
                        qT[po:po + 64, tile, :],
                        Sbf[po:po + 64, (h // 2) * 64:(h // 2) * 64 + 64],
                        start=False, stop=True),
                        after=[prev["Sbf_ready"]] if h == 0 else [])
                    t_state_last = s.emit(T, lambda e, h=h, po=po: e.matmul(
                        pS[po:po + 64, (h // 2) * 64:(h // 2) * 64 + 64],
                        kro[:, h * 64:(h + 1) * 64],
                        vb[:, h * 64:(h + 1) * 64], start=True, stop=True),
                        after=[t_ropek, t_vb, prev["pS_free"]] if h == 0 else [])
                prev["kro_rd"] = [t_trk, t_state_last]
                prev["vb_rd"] = [t_intra_last, t_state_last]
                prev["kT_rd"] = [t_inter_last]
                t_sfu = s.emit(V, lambda e: e.tensor_tensor(
                    Sf[:, :], Sf[:, :], pS[:, :], ALU.add),
                    after=[t_state_last, prev["Sf_ready"]])
                prev["pS_free"] = t_sfu
                prev["Sf_ready"] = t_sfu
                t_sbf = s.emit(V, lambda e: e.tensor_copy(Sbf[:, :], Sf[:, :]),
                               after=[t_sfu, t_inter_last])
                prev["Sbf_ready"] = t_sbf
                t_og = s.emit(V, lambda e, sl=sl: e.tensor_tensor(
                    og[:, sl, :], pbig1[:, :], th[:, :], ALU.mult),
                    after=[t_intra_last, t_inter_last, t_gate] +
                          ([prev["og_rd2"][sl]] if prev["og_rd2"][sl] else []))
                prev["th_rd"] = t_og
                prev["pbig1_free"] = t_og
                f["t_og"] = t_og
                f["t_inter_last"] = t_inter_last
                f["t_vb"] = t_vb
                f["t_th"] = t_th
                return f

            def emit_back(c, f):
                xc = f["xc"]
                sl = c % 2
                ogs = og[:, sl, :]
                ln_toks, rs_ap, m_ap, t_lnsq = emit_ln_stats(
                    scol2, ogs, [f["t_og"]] +
                    ([prev["scol2_rd"]] if prev["scol2_rd"] else []))
                t_oln = s.emit(V, lambda e, rs_ap=rs_ap, m_ap=m_ap, ogs=ogs:
                               e.tensor_scalar(sb_o[:, :], ogs, m_ap, rs_ap,
                                               ALU.subtract, ALU.mult),
                               after=ln_toks +
                               ([prev["sb_o_rd"]] if prev["sb_o_rd"] else []) +
                               ([prev["t_store"]] if prev["t_store"] else []))
                prev["og_rd2"][sl] = t_oln
                t_tro = None
                for j in range(KT):
                    t_tro = s.emit(T, lambda e, j=j: e.transpose(
                        ptr[:, j, :], sb_o[:, j * 128:(j + 1) * 128], identbf[:, :]),
                        after=[t_oln, prev["ptr_free"]] if j == 0 else [])
                t_oT = s.emit(S, lambda e: e.copy(qT[:, :, :], ptr[:, :, :]),
                              after=[t_tro])
                prev["ptr_free"] = t_oT
                t_ao = None
                for nh in range(2):
                    for k in range(KT):
                        def fa(e, nh=nh, k=k):
                            return e.matmul(pbig0[:, nh * 512:(nh + 1) * 512],
                                            qT[:, k, :],
                                            Wo_bf[:, k, nh * 512:(nh + 1) * 512],
                                            start=(k == 0), stop=False)
                        t_ao = s.emit(T, fa,
                                      after=[t_oT, f["t_vb"]] + t_wog
                                      if (nh == 0 and k == 0) else [])
                    def fb(e, nh=nh):
                        return e.matmul(pbig0[:, nh * 512:(nh + 1) * 512],
                                        ones_row[:, :],
                                        bWo_row[0:1, nh * 512:(nh + 1) * 512],
                                        start=False, stop=True)
                    t_ao = s.emit(T, fb, after=[t_ones] if nh == 0 else [])
                prev["qT_rd"] = [t_ao]
                t_aoev = s.emit(V, lambda e: e.tensor_copy(
                    ao_sb[:, :], pbig0[:, :]), after=[t_ao])
                prev["pbig0_free"] = t_aoev
                ln_toks, rs_ap, m_ap, t_lnsq = emit_ln_stats(scol2, ao_sb[:, :],
                                                     [t_aoev])
                t_aoln = s.emit(V, lambda e, rs_ap=rs_ap, m_ap=m_ap:
                                e.tensor_scalar(sb_o[:, :], ao_sb[:, :],
                                                m_ap, rs_ap,
                                                ALU.subtract, ALU.mult),
                                after=ln_toks + [t_oT])
                t_x1 = s.emit(V, lambda e, xc=xc: e.tensor_tensor(
                    x1b[:, :], sb_o[:, :], xc, ALU.add),
                    after=[t_aoln] +
                    ([prev["x1_store"]] if prev["x1_store"] else []))
                t_x1d = s.emit(SY, lambda e, c=c: e.dma_start(
                    x1s[c * C:(c + 1) * C, :], x1b[:, :]), after=[t_x1],
                    dma="dx1s")
                prev["x1_store"] = t_x1d
                if c + 2 < n_chunks:
                    tok_x[c + 2] = s.emit(SY, lambda e, c=c: e.dma_start(
                        xb[:, c % 2, :], x[(c + 2) * C:(c + 3) * C, :]),
                        after=[t_x1] +
                        ([prev["x_sq"][c % 2]] if prev["x_sq"][c % 2] else []),
                        dma="dx0" if (c + 2) % 2 == 0 else "dx1")
                ln_toks, rs_ap, m_ap, t_lnsq = emit_ln_stats(scol2, x1b[:, :],
                                                     [t_x1])
                t_tb = s.emit(V, lambda e, rs_ap=rs_ap, m_ap=m_ap:
                              e.tensor_scalar(sb_o[:, :], x1b[:, :],
                                              m_ap, rs_ap,
                                              ALU.subtract, ALU.mult),
                              after=ln_toks + [t_x1])
                prev["scol2_rd"] = t_tb
                t_tbd = s.emit(SY, lambda e, c=c: e.dma_start(
                    ts_[c * C:(c + 1) * C, :], sb_o[:, :]), after=[t_tb],
                    dma="dts")
                prev["t_store"] = t_tbd
                t_trt = None
                for j in range(KT):
                    t_trt = s.emit(T, lambda e, j=j: e.transpose(
                        ptr[:, j, :], sb_o[:, j * 128:(j + 1) * 128], identbf[:, :]),
                        after=[t_tb, prev["ptr_free"]] if j == 0 else [])
                prev["sb_o_rd"] = t_trt
                t_tT = s.emit(S, lambda e, c=c: e.copy(
                    tT[:, :, c * C:(c + 1) * C], ptr[:, :, :]), after=[t_trt])
                prev["ptr_free"] = t_tT
                prev["outT_rd"] = t_ao
                prev["tT_slot"] = t_tT

            tok_ln1 = [None] * (NCH + 2)
            fstate = {}
            emit_pre(0)
            for i in range(n_chunks + 1):
                if i < n_chunks:
                    fstate[i] = emit_front(i)
                if i >= 1:
                    emit_back(i - 1, fstate.pop(i - 1))
                if i + 1 < n_chunks:
                    emit_pre(i + 1)

            a_final = []
            if prev["tT_slot"] is not None:
                a_final.append(prev["tT_slot"])
            for nm in ("dx1s", "dts"):
                if s.cnt.get(nm, 0) > 0:
                    a_final.append((nm, s.cnt[nm]))
            with nc.Block() as blockA:
                s.flush(blockA, final_tokens=a_final)

        if "B" not in phases:
            return nc
        with ExitStack() as bctx:
            be = bctx.enter_context
            rw1_bf = be(nc.sbuf_tensor("rw1_bf", [128, KT, E], BF16))
            rw2_bf = be(nc.sbuf_tensor("rw2_bf", [128, KT, NEXP], BF16))
            rb1_sb = be(nc.sbuf_tensor("rb1_sb", [128, KT], F32))
            rb2_sb = be(nc.sbuf_tensor("rb2_sb", [128, NEXP], F32))
            r1b = be(nc.sbuf_tensor("r1b", [128, KT, 512], BF16))
            l_sb = be(nc.sbuf_tensor("l_sb", [128, NCH, NEXP], F32))
            tpw = be(nc.sbuf_tensor("tpw", [128, 6 * NEXP], F32))
            wbf = be(nc.sbuf_tensor("wbf", [128, NEXP], BF16))
            pr0 = be(nc.psum_tensor("pr0", [128, 512], F32))
            pr1 = be(nc.psum_tensor("pr1", [128, 512], F32))
            pl = be(nc.psum_tensor("pl", [128, NEXP], F32))
            pwt = be(nc.psum_tensor("pwt", [NEXP, 128], BF16))

            s2 = Sched(nc, bctx, "b")
            s2.self_chain = self_chain
            T, V, S, GP, SY = "tensor", "vector", "scalar", "gpsimd", "sync"
            t_rw1 = s2.emit(GP, lambda e: e.dma_start(
                rw1_bf[:, :, :], rw1[:, :].rearrange("(k p) n -> p k n", p=128)),
                dma="dw")
            t_rw2 = s2.emit(GP, lambda e: e.dma_start(
                rw2_bf[:, :, :], rw2[:, :].rearrange("(k p) n -> p k n", p=128)),
                dma="dw")
            t_rw1 = t_rw2 = ("dw", s2.cnt["dw"])
            t_rb1 = s2.emit(SY, lambda e: e.dma_start(rb1_sb[:, :], rb1_fm[:, :]),
                            dma="dmisc")
            t_rb2 = s2.emit(SY, lambda e: e.dma_start(rb2_sb[:, :], rb2_b[:, :]),
                            dma="dmisc")
            t_rb1 = t_rb2 = ("dmisc", s2.cnt["dmisc"])
            prfree = [None, None]
            t_lev = [None] * NCH
            for t8 in range(NTOK // 512):
                t_r1 = [None] * KT
                for m in range(KT):
                    pr = (pr0, pr1)[m % 2]
                    t_mm = None
                    for k in range(KT):
                        def f(e, m=m, k=k, t8=t8, pr=pr):
                            return e.matmul(pr[:, :], rw1_bf[:, k, m * 128:(m + 1) * 128],
                                            tT[:, k, t8 * 512:(t8 + 1) * 512],
                                            start=(k == 0), stop=(k == KT - 1))
                        aft = []
                        if k == 0:
                            aft = [t_rw1]
                            if prfree[m % 2] is not None:
                                aft.append(prfree[m % 2])
                        t_mm = s2.emit(T, f, after=aft)
                    t_r1[m] = s2.emit(S, lambda e, m=m, pr=pr: e.activation(
                        r1b[:, m, :], pr[:, :], AF.Silu,
                        bias=rb1_sb[:, m:m + 1]), after=[t_mm, t_rb1])
                    prfree[m % 2] = t_r1[m]
                for sub in range(4):
                    ci = t8 * 4 + sub
                    t_lmm = None
                    for k in range(KT):
                        def f(e, k=k, sub=sub):
                            return e.matmul(pl[:, :], r1b[:, k, sub * 128:(sub + 1) * 128],
                                            rw2_bf[:, k, :],
                                            start=(k == 0), stop=(k == KT - 1))
                        t_lmm = s2.emit(T, f,
                                        after=[t_r1[KT - 1], t_rw2] + ([t_lev[ci - 1]] if ci > 0 else [])
                                        if k == 0 else [])
                    t_lev[ci] = s2.emit(V, lambda e, ci=ci: e.tensor_tensor(
                        l_sb[:, ci, :], pl[:, :], rb2_sb[:, :], ALU.add),
                        after=[t_lmm, t_rb2])
            cm1, cnm1, cm2, cZ, crz = 0, 1, 2, 3, 4
            OFF = NEXP
            t_wT_ev = None
            t_wsb = [None] * NCH
            for ci in range(NCH):
                lap = l_sb[:, ci, :]
                t_m1 = s2.emit(V, lambda e, lap=lap: e.reduce_max(
                    tpw[:, cm1:cm1 + 1], lap, axis=AX.X), after=[t_lev[ci]])
                t_nm1 = s2.emit(V, lambda e: e.tensor_scalar(
                    tpw[:, cnm1:cnm1 + 1], tpw[:, cm1:cm1 + 1], -1.0, None,
                    ALU.mult), after=[t_m1])
                t_msk1 = s2.emit(V, lambda e, lap=lap: e.tensor_scalar(
                    tpw[:, OFF:OFF + NEXP], lap, tpw[:, cm1:cm1 + 1], None,
                    ALU.is_ge), after=[t_m1])
                t_masked = s2.emit(V, lambda e, lap=lap: e.scalar_tensor_tensor(
                    tpw[:, 2 * OFF:2 * OFF + NEXP], tpw[:, OFF:OFF + NEXP], -BIG,
                    lap, ALU.mult, ALU.add), after=[t_msk1])
                t_m2 = s2.emit(V, lambda e: e.reduce_max(
                    tpw[:, cm2:cm2 + 1], tpw[:, 2 * OFF:2 * OFF + NEXP],
                    axis=AX.X), after=[t_masked])
                t_sel = s2.emit(V, lambda e, lap=lap: e.tensor_scalar(
                    tpw[:, 3 * OFF:3 * OFF + NEXP], lap, tpw[:, cm2:cm2 + 1],
                    None, ALU.is_ge), after=[t_m2])
                t_e = s2.emit(S, lambda e, lap=lap: e.activation(
                    tpw[:, 4 * OFF:4 * OFF + NEXP], lap, AF.Exp,
                    bias=tpw[:, cnm1:cnm1 + 1]), after=[t_nm1, t_lev[ci]])
                t_es = s2.emit(V, lambda e: e.tensor_tensor(
                    tpw[:, 5 * OFF:5 * OFF + NEXP], tpw[:, 4 * OFF:4 * OFF + NEXP],
                    tpw[:, 3 * OFF:3 * OFF + NEXP], ALU.mult),
                    after=[t_e, t_sel])
                t_Z = s2.emit(V, lambda e: e.reduce_sum(
                    tpw[:, cZ:cZ + 1], tpw[:, 5 * OFF:5 * OFF + NEXP],
                    axis=AX.X), after=[t_es])
                t_rz = s2.emit(V, lambda e: e.reciprocal(
                    tpw[:, crz:crz + 1], tpw[:, cZ:cZ + 1]), after=[t_Z])
                t_wsb[ci] = s2.emit(V, lambda e, ci=ci: e.tensor_scalar(
                    wsb[:, ci, :], tpw[:, 5 * OFF:5 * OFF + NEXP],
                    tpw[:, crz:crz + 1], None, ALU.mult), after=[t_rz])
                t_wbf = s2.emit(V, lambda e, ci=ci: e.tensor_copy(
                    wbf[:, :], wsb[:, ci, :]), after=[t_wsb[ci], t_wT_ev]
                    if t_wT_ev else [t_wsb[ci]])
                t_wT = s2.emit(T, lambda e: e.transpose(
                    pwt[:, :], wbf[:, :], identbf[:, :]),
                    after=[t_wbf] + ([t_wT_ev] if t_wT_ev else []))
                t_wT_ev = s2.emit(S, lambda e, ci=ci: e.copy(
                    wTsb[:, ci * C:(ci + 1) * C], pwt[:, :]), after=[t_wT])
            if debug:
                t_wdump = s2.emit(SY, lambda e: e.dma_start(
                    dbg_w[:, :, :].rearrange("c p e -> p c e"), wsb[:, :, :]),
                    after=t_wsb, dma="ddbg")
            b_final = [t_wT_ev] + t_wsb + (
                [("ddbg", s2.cnt.get("ddbg", 0))] if debug else [])
            with nc.Block() as blockB:
                s2.flush(blockB, final_tokens=b_final)

        if "C" not in phases:
            return nc
        with ExitStack() as cctx:
            ce = cctx.enter_context
            ew1_bf = ce(nc.sbuf_tensor("ew1_bf", [128, KT, DH], BF16))
            ew2_bf = ce(nc.sbuf_tensor("ew2_bf", [128, KT2, E], BF16))
            ehb = ce(nc.sbuf_tensor("ehb", [128, KT2, 512], BF16))
            yac = ce(nc.sbuf_tensor("yac", [128, CPG, E], BF16))
            eb1_sb = ce(nc.sbuf_tensor("eb1_sb", [128, NEXP, KT2], F32))
            eb2_bf = ce(nc.sbuf_tensor("eb2_bf", [NEXP, E], BF16))
            x1c = ce(nc.sbuf_tensor("x1c", [128, E], F32))
            tc = ce(nc.sbuf_tensor("tc", [128, E], BF16))
            yf = ce(nc.sbuf_tensor("yf", [128, E], F32))
            ylnb = ce(nc.sbuf_tensor("ylnb", [128, E], BF16))
            sc3 = ce(nc.sbuf_tensor("sc3", [128, 16], F32))
            peh0 = ce(nc.psum_tensor("peh0", [128, 512], F32))
            peh1 = ce(nc.psum_tensor("peh1", [128, 512], F32))
            peo0 = ce(nc.psum_tensor("peo0", [128, 1024], F32))
            peo1 = ce(nc.psum_tensor("peo1", [128, 1024], F32))

            s3 = Sched(nc, cctx, "c")
            s3.self_chain = self_chain
            T, V, S, GP, SY = "tensor", "vector", "scalar", "gpsimd", "sync"

            t_eb1 = s3.emit(SY, lambda e: e.dma_start(eb1_sb[:, :, :],
                                                      eb1_fm[:, :, :]), dma="dmisc")
            t_eb2 = s3.emit(GP, lambda e: e.dma_start(eb2_bf[:, :], eb2[:, :]),
                            dma="dmiscg")
            t_eb1 = ("dmisc", s3.cnt["dmisc"])
            t_eb2 = ("dmiscg", s3.cnt["dmiscg"])

            def load_ew1(eidx, after):
                return s3.emit(GP, lambda e, eidx=eidx: e.dma_start(
                    ew1_bf[:, :, :],
                    ew1[eidx, :, :].rearrange("(k p) h -> p k h", p=128)),
                    after=after, dma="dw1")

            def load_ew2(eidx, after):
                return s3.emit(GP, lambda e, eidx=eidx: e.dma_start(
                    ew2_bf[:, :, :],
                    ew2[eidx, :, :].rearrange("(k p) n -> p k n", p=128)),
                    after=after, dma="dw2")

            t_ew1 = load_ew1(0, [])
            t_dbg_eh = None
            t_ew2 = None
            ew1_lastuse = None
            ew2_lastuse = None
            t_out_final = []

            def emit_ln3(src_ap, after_tok):
                c_sx, c_sq, c_m, c_a, c_msq, c_b, c_s, c_rs = range(8)
                t_sx = s3.emit(V, lambda e: e.reduce_sum(
                    sc3[:, c_sx:c_sx + 1], src_ap, axis=AX.X), after=after_tok)
                t_sq = s3.emit(S, lambda e: e.activation(
                    tc[:, :], src_ap, AF.Square,
                    accum_out=sc3[:, c_sq:c_sq + 1]), after=after_tok)
                t_m = s3.emit(V, lambda e: e.tensor_scalar(
                    sc3[:, c_m:c_m + 1], sc3[:, c_sx:c_sx + 1], 1.0 / E, None,
                    ALU.mult), after=[t_sx])
                t_a = s3.emit(V, lambda e: e.tensor_scalar(
                    sc3[:, c_a:c_a + 1], sc3[:, c_sq:c_sq + 1], 1.0 / E, EPS,
                    ALU.mult, ALU.add), after=[t_sq])
                t_msq = s3.emit(V, lambda e: e.tensor_tensor(
                    sc3[:, c_msq:c_msq + 1], sc3[:, c_m:c_m + 1],
                    sc3[:, c_m:c_m + 1], ALU.mult), after=[t_m])
                t_b = s3.emit(V, lambda e: e.tensor_tensor(
                    sc3[:, c_b:c_b + 1], sc3[:, c_a:c_a + 1],
                    sc3[:, c_msq:c_msq + 1], ALU.subtract), after=[t_a, t_msq])
                t_s = s3.emit(S, lambda e: e.activation(
                    sc3[:, c_s:c_s + 1], sc3[:, c_b:c_b + 1], AF.Sqrt),
                    after=[t_b])
                t_rs = s3.emit(V, lambda e: e.reciprocal(
                    sc3[:, c_rs:c_rs + 1], sc3[:, c_s:c_s + 1]), after=[t_s])
                return ([t_rs], sc3[:, c_rs:c_rs + 1], sc3[:, c_m:c_m + 1])

            slot_guard = [None] * CPG
            peo_guard = [None, None]
            t_prev_out = None
            t_prev_od = None
            t_prev_dbg = None
            for g in range(NGRP):
                tok0 = g * TPG
                ch0 = g * CPG
                t_yinit = [None] * CPG
                pyfree = None
                for ci in range(CPG):
                    t_mm = None
                    for nh in range(2):
                        def f(e, ci=ci, nh=nh, ch0=ch0):
                            return e.matmul(
                                peo0[:, nh * 512:(nh + 1) * 512],
                                wTsb[:, (ch0 + ci) * C:(ch0 + ci + 1) * C],
                                eb2_bf[:, nh * 512:(nh + 1) * 512],
                                start=True, stop=True)
                        aft = []
                        if nh == 0:
                            aft = [t_eb2]
                            if pyfree is not None:
                                aft.append(pyfree)
                            elif peo_guard[0] is not None:
                                aft.append(peo_guard[0])
                        t_mm = s3.emit(T, f, after=aft)
                    aft_ev = [t_mm]
                    if slot_guard[ci] is not None:
                        aft_ev.append(slot_guard[ci])
                    t_yinit[ci] = s3.emit(S, lambda e, ci=ci: e.copy(
                        yac[:, ci, :], peo0[:, :]), after=aft_ev)
                    pyfree = t_yinit[ci]
                peo_free = [pyfree, pyfree]
                peh_free = [None, None]
                t_yacc_last = t_yinit

                elist = list(range(NEXP)) if g == 0 else list(range(NEXP - 1, -1, -1))
                for ei, eidx in enumerate(elist):
                    first = (g == 0 and ei == 0)
                    if not (g == 1 and ei == 0):
                        t_ew2 = load_ew2(eidx, [ew2_lastuse] if ew2_lastuse else [])
                    for t4 in range(TPG // 512):
                        tstart = tok0 + t4 * 512
                        t_ehev = [None] * KT2
                        for m in range(KT2):
                            ph = (peh0, peh1)[m % 2]
                            t_mm = None
                            for k in range(KT):
                                def f(e, m=m, k=k, ph=ph, tstart=tstart):
                                    return e.matmul(
                                        ph[:, :], ew1_bf[:, k, m * 128:(m + 1) * 128],
                                        tT[:, k, tstart:tstart + 512],
                                        start=(k == 0), stop=(k == KT - 1))
                                aft = []
                                if k == 0:
                                    aft = [t_ew1]
                                    if peh_free[m % 2] is not None:
                                        aft.append(peh_free[m % 2])
                                t_mm = s3.emit(T, f, after=aft)
                            t_ehev[m] = s3.emit(S, lambda e, m=m, ph=ph, eidx=eidx:
                                                e.activation(
                                ehb[:, m, :], ph[:, :], AF.Silu,
                                bias=eb1_sb[:, eidx, m:m + 1]),
                                after=[t_mm, t_eb1] +
                                ([t_dbg_eh] if t_dbg_eh else []))
                            peh_free[m % 2] = t_ehev[m]
                        ew1_lastuse = t_mm
                        if debug and g == 0 and ei == 0 and t4 == 0:
                            t_dbg_eh = s3.emit(SY, lambda e: e.dma_start(
                                dbg_eh[:, :, :], ehb[:, :, :]),
                                after=[t_ehev[KT2 - 1]], dma="ddbg")
                        if t4 == TPG // 512 - 1:
                            nxt = None
                            if ei + 1 < NEXP:
                                nxt = elist[ei + 1]
                            elif g == 0:
                                nxt = None
                            if nxt is not None:
                                t_ew1 = load_ew1(nxt, [ew1_lastuse])
                        for sub in range(4):
                            ci = t4 * 4 + sub
                            po = (peo0, peo1)[sub % 2]
                            t_mm = None
                            for nh in range(2):
                                for k in range(KT2):
                                    def f(e, sub=sub, nh=nh, k=k, po=po):
                                        return e.matmul(
                                            po[:, nh * 512:(nh + 1) * 512],
                                            ehb[:, k, sub * 128:(sub + 1) * 128],
                                            ew2_bf[:, k, nh * 512:(nh + 1) * 512],
                                            start=(k == 0), stop=(k == KT2 - 1))
                                    aft = []
                                    if nh == 0 and k == 0:
                                        aft = [t_ehev[KT2 - 1], t_ew2]
                                        if peo_free[sub % 2] is not None:
                                            aft.append(peo_free[sub % 2])
                                    t_mm = s3.emit(T, f, after=aft)
                            ew2_lastuse = t_mm
                            def facc(e, ci=ci, po=po, eidx=eidx, ch0=ch0):
                                return e.scalar_tensor_tensor(
                                    yac[:, ci, :], po[:, :],
                                    wsb[:, ch0 + ci, eidx:eidx + 1],
                                    yac[:, ci, :], ALU.mult, ALU.add)
                            t_yacc_last[ci] = s3.emit(V, facc,
                                                      after=[t_mm, t_yacc_last[ci]])
                            peo_free[sub % 2] = t_yacc_last[ci]
                    if debug and g == 0 and ei == 0 and t4 == TPG // 512 - 1:
                        t_dbgyac = s3.emit(SY, lambda e: e.dma_start(
                            dbg_yac[:, :, :], yac[:, :, :]),
                            after=[t_yacc_last[i] for i in range(CPG)],
                            dma="ddbg")
                        for i in range(CPG):
                            t_yacc_last[i] = t_dbgyac
                for ci in range(CPG):
                    cg = ch0 + ci
                    aft_in = ([t_prev_out] if t_prev_out else []) + (
                        [t_prev_od] if t_prev_od else [])
                    t_x1c = s3.emit(SY, lambda e, cg=cg: e.dma_start(
                        x1c[:, :], x1s[cg * C:(cg + 1) * C, :]),
                        after=aft_in, dma="dpcx")
                    t_tc = s3.emit(SY, lambda e, cg=cg: e.dma_start(
                        tc[:, :], ts_[cg * C:(cg + 1) * C, :]),
                        after=aft_in, dma="dpct")
                    t_y = s3.emit(V, lambda e, ci=ci: e.tensor_tensor(
                        yf[:, :], yac[:, ci, :], tc[:, :], ALU.add),
                        after=[t_tc, t_yacc_last[ci]] +
                              ([t_prev_dbg] if t_prev_dbg else []))
                    slot_guard[ci] = t_y
                    if debug:
                        t_prev_dbg = s3.emit(SY, lambda e, cg=cg: e.dma_start(
                            dbg_y[cg * C:(cg + 1) * C, :], yf[:, :]),
                            after=[t_y], dma="ddbg")
                    ln_toks, rs_ap, m_ap = emit_ln3(yf[:, :], [t_y])
                    t_yln = s3.emit(V, lambda e, rs_ap=rs_ap, m_ap=m_ap:
                                    e.tensor_scalar(ylnb[:, :], yf[:, :],
                                                    m_ap, rs_ap,
                                                    ALU.subtract, ALU.mult),
                                    after=ln_toks)
                    t_ob = s3.emit(V, lambda e: e.tensor_tensor(
                        x1c[:, :], ylnb[:, :], x1c[:, :], ALU.add),
                        after=[t_yln, t_x1c])
                    t_od = s3.emit(SY, lambda e, cg=cg: e.dma_start(
                        out[cg * C:(cg + 1) * C, :], x1c[:, :]),
                        after=[t_ob], dma="dout")
                    t_out_final.append(t_od)
                    t_prev_out = t_ob
                    t_prev_od = t_od
                peo_guard = list(peo_free)

            with nc.Block() as blockC:
                s3.flush(blockC, final_tokens=[("dout", s3.cnt["dout"])] + (
                    [("ddbg", s3.cnt.get("ddbg", 0))] if debug else []))

    return nc


_NC_CACHE = None


def _get_nc():
    global _NC_CACHE
    if _NC_CACHE is None:
        _NC_CACHE = build()
    return _NC_CACHE


def kernel(**inputs):
    cos_rep, sin_rep, triu, ident = _host_tables()
    f32 = lambda a: np.ascontiguousarray(np.asarray(a, dtype=np.float32))
    g = {k: f32(v) for k, v in inputs.items()}
    base = {
        "Wq": g["L0_attn_Wq"], "Wk": g["L0_attn_Wk"], "Wv": g["L0_attn_Wv"],
        "Wo": g["L0_attn_Wo"], "Wg1": g["L0_attn_Wg1"], "Wg2": g["L0_attn_Wg2"],
        "gamma_fm": f32(g["L0_attn_gamma"].reshape(KT, 128).T),
        "beta_fm": f32(g["L0_attn_beta"].reshape(KT, 128).T),
        "rw1": g["L0_ffn_rw1"],
        "rb1_fm": f32(g["L0_ffn_rb1"].reshape(KT, 128).T),
        "rw2": g["L0_ffn_rw2"],
        "rb2_b": f32(np.broadcast_to(g["L0_ffn_rb2"], (128, NEXP))),
        "ew1": g["L0_ffn_ew1"],
        "eb1_fm": f32(g["L0_ffn_eb1"].reshape(NEXP, KT2, 128).transpose(2, 0, 1)),
        "ew2": g["L0_ffn_ew2"],
        "eb2": g["L0_ffn_eb2"],
        "cosr": cos_rep, "sinr": sin_rep, "triu": triu, "ident": ident,
    }
    xfull = g["L0_attn_x"]
    in_maps = []
    for b in range(NB):
        m = dict(base)
        m["x"] = np.ascontiguousarray(xfull[b])
        in_maps.append(m)
    nc = _get_nc()
    res = run_bass_kernel_spmd(nc, in_maps, core_ids=list(range(NB)))
    outp = np.stack([np.asarray(res.results[b]["out"], dtype=np.float32)
                     for b in range(NB)], axis=0)
    return outp


# version 14
# speedup vs baseline: 1.3613x; 1.0391x over previous
import numpy as np
import ml_dtypes
from contextlib import ExitStack

import concourse.bass as bass
import concourse.mybir as mybir
from concourse.bass_utils import run_bass_kernel_spmd

F32 = mybir.dt.float32
BF16 = mybir.dt.bfloat16
I32 = mybir.dt.int32
AF = mybir.ActivationFunctionType
ALU = mybir.AluOpType
AX = mybir.AxisListType

NB, NTOK, E = 8, 4096, 1024
H, D = 16, 64
C = 128
NCH = NTOK // C
KT = E // 128
NEXP, DH = 8, 2048
KT2 = DH // 128
EPS = 1e-5
RBASE = 10000.0
NGRP = 2
CPG = NCH // NGRP
TPG = NTOK // NGRP
BIG = 1.0e30

ENGS = ("sync", "gpsimd", "vector", "scalar", "tensor")


class Sched:

    def __init__(self, nc, ctx, tag):
        self.nc = nc
        self.ctx = ctx
        self.tag = tag
        self.sem = {}
        self.cnt = {}
        for e in ENGS:
            self.sem[e] = ctx.enter_context(nc.semaphore(f"{tag}_{e}"))
            self.cnt[e] = 0
        self.recs = {e: [] for e in ENGS}
        self.pend_bar = {e: False for e in ENGS}
        self.bar_snapshot = None

    def _dma_sem(self, name):
        if name not in self.sem:
            self.sem[name] = self.ctx.enter_context(
                self.nc.semaphore(f"{self.tag}_{name}"))
            self.cnt[name] = 0
        return name

    SELF_CHAIN_HW = ("scalar", "vector", "gpsimd")
    SELF_CHAIN_SIM = ("vector", "scalar", "gpsimd")
    self_chain = False

    def emit(self, eng, fn, after=(), dma=None):
        chain = self.SELF_CHAIN_SIM if self.self_chain else self.SELF_CHAIN_HW
        waits = []
        if eng in chain and self.cnt[eng] > 0:
            waits.append((eng, self.cnt[eng]))
        if self.pend_bar[eng] and self.bar_snapshot:
            for s, v in self.bar_snapshot.items():
                if s != eng and v > 0:
                    waits.append((s, v))
            self.pend_bar[eng] = False
        for s, v in after:
            if s == eng:
                continue
            waits.append((s, v))
        if dma is not None:
            semname = self._dma_sem(dma)
            inc = 16
        else:
            semname = eng
            inc = 1
        self.cnt[semname] += inc
        tok = (semname, self.cnt[semname])
        self.recs[eng].append((waits, fn, semname, inc))
        return tok

    def barrier(self):
        self.bar_snapshot = {e: self.cnt[e] for e in ENGS}
        for e in ENGS:
            self.pend_bar[e] = True

    def flush(self, block, final_tokens=()):
        recs = self.recs
        sems = self.sem
        waited_vals = {}
        for ename in ENGS:
            for waits, fn, semname, inc in recs[ename]:
                for sname, v in waits:
                    waited_vals.setdefault(sname, set()).add(v)
        for sname, v in final_tokens:
            waited_vals.setdefault(sname, set()).add(v)
        remap = {}
        for ename in ENGS:
            vals = sorted(waited_vals.get(ename, ()))
            remap[ename] = ({v: i + 1 for i, v in enumerate(vals)}, set(vals))

        def map_tok(sname, v):
            if sname in remap:
                return remap[sname][0][v]
            return v

        def make_body(ename, extra_final):
            inc_set = remap[ename][1]

            def body(eng):
                waited = {}
                dense = 0
                for waits, fn, semname, inc in recs[ename]:
                    for sname, v in waits:
                        mv = map_tok(sname, v)
                        if waited.get(sname, 0) >= mv:
                            continue
                        eng.wait_ge(sems[sname], mv)
                        waited[sname] = mv
                    inst = fn(eng)
                    if semname == ename:
                        dense += 1
                        if dense in inc_set:
                            inst.then_inc(sems[semname], 1)
                    else:
                        inst.then_inc(sems[semname], inc)
                for sname, v in extra_final:
                    mv = map_tok(sname, v)
                    if waited.get(sname, 0) >= mv:
                        continue
                    eng.wait_ge(sems[sname], mv)
                    waited[sname] = mv
            return body

        block.sync(make_body("sync", list(final_tokens)))
        block.gpsimd(make_body("gpsimd", []))
        block.vector(make_body("vector", []))
        block.scalar(make_body("scalar", []))
        block.tensor(make_body("tensor", []))


def _host_tables():
    d2 = np.arange(D // 2, dtype=np.float64)
    inv = RBASE ** (-(d2 * 2.0) / D)
    pos = np.arange(NTOK, dtype=np.float64)
    ang = pos[:, None] * inv[None, :]
    cos = np.cos(ang).astype(np.float32)
    sin = np.sin(ang).astype(np.float32)
    cos_rep = np.ascontiguousarray(np.tile(cos, (1, H)))
    sin_rep = np.ascontiguousarray(np.tile(sin, (1, H)))
    triu = np.ascontiguousarray(np.tile(
        np.triu(np.ones((C, C), dtype=np.float32)), (1, 2))).astype(
        ml_dtypes.bfloat16)
    ident = np.eye(C, dtype=np.float32)
    return cos_rep, sin_rep, triu, ident


def build(debug=False, n_chunks=NCH, phases="ABC", stage=999,
          self_chain=False):
    nc = bass.Bass("TRN2", target_bir_lowering=False)
    dt = nc.dram_tensor
    x = dt("x", [NTOK, E], F32, kind="ExternalInput")
    Wq = dt("Wq", [E, E], F32, kind="ExternalInput")
    Wk = dt("Wk", [E, E], F32, kind="ExternalInput")
    Wv = dt("Wv", [E, E], F32, kind="ExternalInput")
    Wo = dt("Wo", [E, E], F32, kind="ExternalInput")
    Wg1 = dt("Wg1", [E, D], F32, kind="ExternalInput")
    Wg2 = dt("Wg2", [D, E], F32, kind="ExternalInput")
    gamma_fm = dt("gamma_fm", [128, KT], F32, kind="ExternalInput")
    beta_fm = dt("beta_fm", [128, KT], F32, kind="ExternalInput")
    rw1 = dt("rw1", [E, E], F32, kind="ExternalInput")
    rb1_fm = dt("rb1_fm", [128, KT], F32, kind="ExternalInput")
    rw2 = dt("rw2", [E, NEXP], F32, kind="ExternalInput")
    rb2_b = dt("rb2_b", [128, NEXP], F32, kind="ExternalInput")
    ew1 = dt("ew1", [NEXP, E, DH], F32, kind="ExternalInput")
    eb1_fm = dt("eb1_fm", [128, NEXP, KT2], F32, kind="ExternalInput")
    ew2 = dt("ew2", [NEXP, DH, E], F32, kind="ExternalInput")
    eb2 = dt("eb2", [NEXP, E], F32, kind="ExternalInput")
    cosr = dt("cosr", [NTOK, H * 32], F32, kind="ExternalInput")
    sinr = dt("sinr", [NTOK, H * 32], F32, kind="ExternalInput")
    triu_in = dt("triu", [C, 2 * C], BF16, kind="ExternalInput")
    ident_in = dt("ident", [C, C], F32, kind="ExternalInput")
    out = dt("out", [NTOK, E], F32, kind="ExternalOutput")
    skind = "ExternalOutput" if debug else "Internal"
    x1s = dt("x1s", [NTOK, E], F32, kind=skind)
    ts_ = dt("ts", [NTOK, E], BF16, kind=skind)
    dbg_w = dt("dbg_w", [NCH, 128, NEXP], F32, kind=skind) if debug else None
    dbg_y = dt("dbg_y", [NTOK, E], F32, kind=skind) if debug else None
    dbg_eh = dt("dbg_eh", [128, KT2, 512], BF16, kind=skind) if debug else None
    dbg_yac = dt("dbg_yac", [128, CPG, E], BF16, kind=skind) if debug else None

    with ExitStack() as octx:
        oe = octx.enter_context
        tT = oe(nc.sbuf_tensor("tT", [128, KT, NTOK], BF16))
        wsb = oe(nc.sbuf_tensor("wsb", [128, NCH, NEXP], F32))
        wTsb = oe(nc.sbuf_tensor("wTsb", [NEXP, NTOK], BF16))
        ones_row = oe(nc.sbuf_tensor("ones_row", [1, 128], BF16))
        bWo_row = oe(nc.sbuf_tensor("bWo_row", [1, E], BF16))
        identbf = oe(nc.sbuf_tensor("identbf", [C, C], BF16))
        trium = oe(nc.sbuf_tensor("trium", [C, 2 * C], BF16))

        with ExitStack() as actx:
            ae = actx.enter_context
            Wq_bf = ae(nc.sbuf_tensor("Wq_bf", [128, KT, E], BF16))
            Wk_bf = ae(nc.sbuf_tensor("Wk_bf", [128, KT, E], BF16))
            Wv_bf = ae(nc.sbuf_tensor("Wv_bf", [128, KT, E], BF16))
            Wo_bf = ae(nc.sbuf_tensor("Wo_bf", [128, KT, E], BF16))
            Wg1_bf = ae(nc.sbuf_tensor("Wg1_bf", [128, KT, D], BF16))
            Wg2_bf = ae(nc.sbuf_tensor("Wg2_bf", [D, E], BF16))
            gamma_sb = ae(nc.sbuf_tensor("gamma_sb", [128, KT], F32))
            beta_sb = ae(nc.sbuf_tensor("beta_sb", [128, KT], F32))
            beta_bf = ae(nc.sbuf_tensor("beta_bf", [128, KT], BF16))
            cosb = ae(nc.sbuf_tensor("cosb", [128, 512], F32))
            sinb = ae(nc.sbuf_tensor("sinb", [128, 512], F32))
            xb = ae(nc.sbuf_tensor("xb", [128, 2, E], F32))
            ln1b = ae(nc.sbuf_tensor("ln1b", [128, E], BF16))
            ln1T = ae(nc.sbuf_tensor("ln1T", [128, KT, C], BF16))
            qs = ae(nc.sbuf_tensor("qs", [128, E], F32))
            ks = ae(nc.sbuf_tensor("ks", [128, E], F32))
            ao_sb = ae(nc.sbuf_tensor("ao_sb", [128, E], BF16))
            rt2 = ae(nc.sbuf_tensor("rt2", [128, 512], F32))
            kt2 = ae(nc.sbuf_tensor("kt2", [128, 512], F32))
            qro = ae(nc.sbuf_tensor("qro", [128, E], BF16))
            kro = ae(nc.sbuf_tensor("kro", [128, E], BF16))
            vb = ae(nc.sbuf_tensor("vb", [128, E], BF16))
            qT = ae(nc.sbuf_tensor("qT", [128, KT, C], BF16))
            kTt = ae(nc.sbuf_tensor("kTt", [128, KT, C], BF16))
            g1b = ae(nc.sbuf_tensor("g1b", [D, C], BF16))
            th = ae(nc.sbuf_tensor("th", [128, E], BF16))
            eT = ae(nc.sbuf_tensor("eT", [128, 2, 2 * C], BF16))
            Sf = ae(nc.sbuf_tensor("Sf", [128, 512], F32))
            Sbf = ae(nc.sbuf_tensor("Sbf", [128, 512], BF16))
            sb_o = ae(nc.sbuf_tensor("sb_o", [128, E], BF16))
            x1b = ae(nc.sbuf_tensor("x1b", [128, E], F32))
            dump = ae(nc.sbuf_tensor("dump", [128, E], BF16))
            og = ae(nc.sbuf_tensor("og", [128, 2, E], BF16))
            scol = ae(nc.sbuf_tensor("scol", [128, 32], F32))
            scol2 = ae(nc.sbuf_tensor("scol2", [128, 32], F32))
            pbig0 = ae(nc.psum_tensor("pbig0", [128, 1024], F32))
            pbig1 = ae(nc.psum_tensor("pbig1", [128, 1024], F32))
            ptr = ae(nc.psum_tensor("ptr", [128, KT, C], BF16))
            pS = ae(nc.psum_tensor("pS", [128, 512], F32))
            pm6 = ae(nc.psum_tensor("pm6", [128, 512], F32))
            pm7 = ae(nc.psum_tensor("pm7", [128, 512], F32))

            s = Sched(nc, actx, "a")
            s.self_chain = self_chain
            T, V, S, GP, SY = "tensor", "vector", "scalar", "gpsimd", "sync"

            t_idb = s.emit(GP, lambda e: e.dma_start(
                identbf[:, :], ident_in[:, :]), dma="dident")
            t_wo = s.emit(GP, lambda e: e.dma_start(
                Wo_bf[:, :, :], Wo[:, :].rearrange("(k p) n -> p k n", p=128)), dma="dwo")
            t_wq = s.emit(GP, lambda e: e.dma_start(
                Wq_bf[:, :, :], Wq[:, :].rearrange("(k p) n -> p k n", p=128)), dma="dwq")
            t_wk = s.emit(GP, lambda e: e.dma_start(
                Wk_bf[:, :, :], Wk[:, :].rearrange("(k p) n -> p k n", p=128)), dma="dwk")
            t_wv = s.emit(GP, lambda e: e.dma_start(
                Wv_bf[:, :, :], Wv[:, :].rearrange("(k p) n -> p k n", p=128)), dma="dw")
            t_wg1 = s.emit(GP, lambda e: e.dma_start(
                Wg1_bf[:, :, :], Wg1[:, :].rearrange("(k p) n -> p k n", p=128)), dma="dw")
            t_wg2 = s.emit(GP, lambda e: e.dma_start(Wg2_bf[:, :], Wg2[:, :]), dma="dw")
            tok_wall = ("dw", s.cnt["dw"])
            t_wv = t_wg1 = t_wg2 = tok_wall
            t_gam = s.emit(SY, lambda e: e.dma_start(gamma_sb[:, :], gamma_fm[:, :]), dma="dmisc")
            t_bet = s.emit(SY, lambda e: e.dma_start(beta_sb[:, :], beta_fm[:, :]), dma="dmisc")
            t_tri = s.emit(SY, lambda e: e.dma_start(trium[:, :], triu_in[:, :]), dma="dmisc")
            tok_misc = ("dmisc", s.cnt["dmisc"])
            t_gam = t_bet = t_tri = tok_misc
            t_bbf = s.emit(V, lambda e: e.tensor_copy(beta_bf[:, :], beta_sb[:, :]),
                           after=[t_bet])
            t_ones = s.emit(V, lambda e: e.memset(ones_row[:, :], 1.0))
            t_sf0 = s.emit(V, lambda e: e.memset(Sf[:, :], 0.0))
            t_sbf0 = s.emit(V, lambda e: e.memset(Sbf[:, :], 0.0), after=[t_sf0])
            t_bwo_mm = None
            t_bwo_ev = None
            for half in range(2):
                for k in range(KT):
                    def f(e, half=half, k=k):
                        return e.matmul(pm7[0:1, 0:512],
                                        beta_bf[:, k:k + 1],
                                        Wo_bf[:, k, half * 512:(half + 1) * 512],
                                        start=(k == 0), stop=(k == KT - 1))
                    aft = []
                    if k == 0:
                        aft = [t_wo, t_bbf]
                        if t_bwo_ev is not None:
                            aft.append(t_bwo_ev)
                    t_bwo_mm = s.emit(T, f, after=aft)
                t_bwo_ev = s.emit(S, lambda e, half=half: e.copy(
                    bWo_row[0:1, half * 512:(half + 1) * 512], pm7[0:1, 0:512]),
                    after=[t_bwo_mm])
            t_wog = []
            for k in range(KT):
                t_wog.append(s.emit(V, lambda e, k=k: e.tensor_scalar(
                    Wo_bf[:, k, :], Wo_bf[:, k, :], gamma_sb[:, k:k + 1], None,
                    ALU.mult), after=[t_bwo_ev, t_gam, t_wo]))

            tok_x = [None] * (NCH + 2)
            tok_x[0] = s.emit(SY, lambda e: e.dma_start(xb[:, 0, :], x[0:C, :]), dma="dx0")
            if n_chunks > 1:
                tok_x[1] = s.emit(SY, lambda e: e.dma_start(
                    xb[:, 1, :], x[C:2 * C, :]), dma="dx1")
            t_cos = s.emit(SY, lambda e: e.dma_start(cosb[:, :], cosr[0:C, :]), dma="dtab")
            t_sin = s.emit(SY, lambda e: e.dma_start(sinb[:, :], sinr[0:C, :]), dma="dtab")
            t_cos = t_sin = ("dtab", s.cnt["dtab"])

            prev = {
                "x1_store": None,
                "t_store": None,
                "ptr_free": t_idb,
                "pbig0_free": t_bwo_ev,
                "pbig1_free": t_bwo_ev,
                "Sbf_ready": t_sbf0,
                "Sf_ready": t_sf0,
                "pS_free": t_sf0,
                "rope_done_v": t_ones,
                "rope_done_g": t_idb,
                "x_free": [t_bbf],
                "tT_slot": None,
            }

            def emit_ln_stats(colbuf, src_ap, after_tok, psum_src=False):
                c_sx, c_sq, c_m, c_a, c_msq, c_b, c_s, c_rs = range(8)
                t_sx = s.emit(V, lambda e: e.reduce_sum(
                    colbuf[:, c_sx:c_sx + 1], src_ap, axis=AX.X), after=after_tok)
                t_sq = s.emit(S, lambda e: e.activation(
                    dump[:, :], src_ap, AF.Square,
                    accum_out=colbuf[:, c_sq:c_sq + 1]),
                    after=after_tok + ([t_sx] if psum_src else []))
                t_m = s.emit(V, lambda e: e.tensor_scalar(
                    colbuf[:, c_m:c_m + 1], colbuf[:, c_sx:c_sx + 1], 1.0 / E,
                    None, ALU.mult), after=[t_sx])
                t_a = s.emit(V, lambda e: e.tensor_scalar(
                    colbuf[:, c_a:c_a + 1], colbuf[:, c_sq:c_sq + 1], 1.0 / E,
                    EPS, ALU.mult, ALU.add), after=[t_sq])
                t_msq = s.emit(V, lambda e: e.tensor_tensor(
                    colbuf[:, c_msq:c_msq + 1], colbuf[:, c_m:c_m + 1],
                    colbuf[:, c_m:c_m + 1], ALU.mult), after=[t_m])
                t_b = s.emit(V, lambda e: e.tensor_tensor(
                    colbuf[:, c_b:c_b + 1], colbuf[:, c_a:c_a + 1],
                    colbuf[:, c_msq:c_msq + 1], ALU.subtract),
                    after=[t_a, t_msq])
                t_s = s.emit(S, lambda e: e.activation(
                    colbuf[:, c_s:c_s + 1], colbuf[:, c_b:c_b + 1], AF.Sqrt),
                    after=[t_b])
                t_rs = s.emit(V, lambda e: e.reciprocal(
                    colbuf[:, c_rs:c_rs + 1], colbuf[:, c_s:c_s + 1]),
                    after=[t_s])
                return ([t_rs], colbuf[:, c_rs:c_rs + 1],
                        colbuf[:, c_m:c_m + 1])

            cos_v = cosb[:, :].rearrange("p (h d) -> p h d", h=H)
            sin_v = sinb[:, :].rearrange("p (h d) -> p h d", h=H)

            def emit_rope(eng, src, scr2, dst, after_toks):
                sv = src[:, :].rearrange("p (h d) -> p h d", h=H)
                dv = dst[:, :].rearrange("p (h d) -> p h d", h=H)
                s2 = scr2[:, :].rearrange("p (h d) -> p h d", h=H)
                x1v, x2v = sv[:, :, 0:32], sv[:, :, 32:64]
                d1, d2 = dv[:, :, 0:32], dv[:, :, 32:64]
                s.emit(eng, lambda e: e.tensor_tensor(
                    s2[:, :, :], x2v, sin_v, ALU.mult), after=after_toks)
                s.emit(eng, lambda e: e.tensor_tensor(
                    d1, x1v, cos_v, ALU.mult))
                s.emit(eng, lambda e: e.tensor_tensor(
                    d1, d1, s2[:, :, :], ALU.subtract))
                s.emit(eng, lambda e: e.tensor_tensor(
                    s2[:, :, :], x2v, cos_v, ALU.mult))
                s.emit(eng, lambda e: e.tensor_tensor(
                    d2, x1v, sin_v, ALU.mult))
                t6 = s.emit(eng, lambda e: e.tensor_tensor(
                    d2, d2, s2[:, :, :], ALU.add))
                return t6

            prev.update({
                "scol_rd": None, "scol2_rd": None, "ln1b_rd": None,
                "ln1T_rd": None, "qs_rd": None, "ks_rd": None,
                "qro_rd": None, "kro_rd": [], "vb_rd": [],
                "qT_rd": [], "kT_rd": [], "th_rd": None,
                "sb_o_rd": None, "outT_rd": None, "og_rd2": [None, None],
                "t_mask": [None, None],
            })

            def emit_front(c):
                f = {}
                sl = c % 2
                if c > 0:
                    s.emit(SY, lambda e, c=c: e.dma_start(
                        cosb[:, :], cosr[c * C:(c + 1) * C, :]),
                        after=[prev["rope_done_v"], prev["rope_done_g"]],
                        dma="dtab")
                    s.emit(SY, lambda e, c=c: e.dma_start(
                        sinb[:, :], sinr[c * C:(c + 1) * C, :]),
                        after=[prev["rope_done_v"], prev["rope_done_g"]],
                        dma="dtab")
                    f["tab"] = ("dtab", s.cnt["dtab"])
                else:
                    f["tab"] = t_cos
                xc = xb[:, sl, :]
                f["xc"] = xc
                f["c"] = c
                ln_toks, rs_ap, m_ap = emit_ln_stats(
                    scol, xc, [tok_x[c]] +
                    ([prev["scol_rd"]] if prev["scol_rd"] else []))
                t_ln1 = s.emit(V, lambda e, rs_ap=rs_ap, m_ap=m_ap, xc=xc:
                               e.tensor_scalar(ln1b[:, :], xc, m_ap, rs_ap,
                                               ALU.subtract, ALU.mult),
                               after=ln_toks +
                               ([prev["ln1b_rd"]] if prev["ln1b_rd"] else []))
                prev["scol_rd"] = t_ln1
                t_tr = None
                for j in range(KT):
                    t_tr = s.emit(T, lambda e, j=j: e.transpose(
                        ptr[:, j, :], ln1b[:, j * 128:(j + 1) * 128],
                        identbf[:, :]),
                        after=[t_ln1, prev["ptr_free"], t_idb] if j == 0 else [])
                prev["ln1b_rd"] = t_tr
                t_ln1T = s.emit(S, lambda e: e.copy(
                    ln1T[:, :, :], ptr[:, :, :]),
                    after=[t_tr] +
                    ([prev["ln1T_rd"]] if prev["ln1T_rd"] else []))
                prev["ptr_free"] = t_ln1T
                t_qmm = None
                for nh in range(2):
                    for k in range(KT):
                        def fq(e, nh=nh, k=k):
                            return e.matmul(pbig0[:, nh * 512:(nh + 1) * 512],
                                            ln1T[:, k, :],
                                            Wq_bf[:, k, nh * 512:(nh + 1) * 512],
                                            start=(k == 0), stop=(k == KT - 1))
                        t_qmm = s.emit(T, fq,
                                       after=[t_ln1T, prev["pbig0_free"], t_wq]
                                       if (nh == 0 and k == 0) else [])
                t_kmm = None
                for nh in range(2):
                    for k in range(KT):
                        def fk(e, nh=nh, k=k):
                            return e.matmul(pbig1[:, nh * 512:(nh + 1) * 512],
                                            ln1T[:, k, :],
                                            Wk_bf[:, k, nh * 512:(nh + 1) * 512],
                                            start=(k == 0), stop=(k == KT - 1))
                        t_kmm = s.emit(T, fk,
                                       after=[t_ln1T, prev["pbig1_free"], t_wk]
                                       if (nh == 0 and k == 0) else [])
                t_sq_ = s.emit(S, lambda e: e.activation(qs[:, :], pbig0[:, :],
                                                         AF.Silu),
                               after=[t_qmm] +
                               ([prev["qs_rd"]] if prev["qs_rd"] else []))
                t_sk_ = s.emit(S, lambda e: e.activation(ks[:, :], pbig1[:, :],
                                                         AF.Silu),
                               after=[t_kmm] +
                               ([prev["ks_rd"]] if prev["ks_rd"] else []))
                t_vmm = None
                for nh in range(2):
                    for k in range(KT):
                        def fv(e, nh=nh, k=k):
                            return e.matmul(pbig0[:, nh * 512:(nh + 1) * 512],
                                            ln1T[:, k, :],
                                            Wv_bf[:, k, nh * 512:(nh + 1) * 512],
                                            start=(k == 0), stop=(k == KT - 1))
                        t_vmm = s.emit(T, fv, after=[t_sq_, t_wv]
                                       if (nh == 0 and k == 0) else [])
                t_g1 = None
                for k in range(KT):
                    def fg(e, k=k):
                        return e.matmul(pm6[0:64, 0:128], Wg1_bf[:, k, :],
                                        ln1T[:, k, :],
                                        start=(k == 0), stop=(k == KT - 1))
                    aft = []
                    if k == 0:
                        aft = [t_ln1T, t_wg1]
                        if prev["t_mask"][0] is not None:
                            aft.append(prev["t_mask"][0])
                    t_g1 = s.emit(T, fg, after=aft)
                prev["ln1T_rd"] = t_g1
                t_g1e = s.emit(S, lambda e: e.copy(g1b[:, :], pm6[0:64, 0:128]),
                               after=[t_g1])
                t_g2 = None
                for nh in range(2):
                    def fg2(e, nh=nh):
                        return e.matmul(pbig1[:, nh * 512:(nh + 1) * 512],
                                        g1b[:, :],
                                        Wg2_bf[:, nh * 512:(nh + 1) * 512],
                                        start=True, stop=True)
                    t_g2 = s.emit(T, fg2, after=[t_sk_, t_g1e, t_wg2]
                                  if nh == 0 else [])
                t_th = s.emit(S, lambda e: e.activation(th[:, :], pbig1[:, :],
                                                        AF.Tanh, scale=0.5),
                              after=[t_g2] +
                              ([prev["th_rd"]] if prev["th_rd"] else []))
                t_vb = s.emit(V, lambda e: e.tensor_copy(vb[:, :], pbig0[:, :]),
                              after=[t_vmm] + prev["vb_rd"])
                f["t_vb"] = t_vb
                t_ropeq = emit_rope(GP, qs, rt2, qro,
                                    [f["tab"], t_sq_] +
                                    ([prev["qro_rd"]] if prev["qro_rd"] else []))
                t_ropek = emit_rope(GP, ks, kt2, kro,
                                    [f["tab"], t_sk_] + prev["kro_rd"])
                prev["rope_done_v"] = t_ropeq
                prev["rope_done_g"] = t_ropek
                prev["qs_rd"] = t_ropeq
                prev["ks_rd"] = t_ropek
                t_gate = s.emit(V, lambda e: e.tensor_scalar(
                    th[:, :], th[:, :], 0.5, 0.5, ALU.mult, ALU.add),
                    after=[t_th])
                t_trq = None
                for j in range(KT):
                    t_trq = s.emit(T, lambda e, j=j: e.transpose(
                        ptr[:, j, :], qro[:, j * 128:(j + 1) * 128], identbf[:, :]),
                        after=[t_ropeq, prev["ptr_free"]] if j == 0 else [])
                prev["qro_rd"] = t_trq
                t_qT = s.emit(S, lambda e: e.copy(qT[:, :, :], ptr[:, :, :]),
                              after=[t_trq] + prev["qT_rd"])
                prev["ptr_free"] = t_qT
                t_trk = None
                for j in range(KT):
                    t_trk = s.emit(T, lambda e, j=j: e.transpose(
                        ptr[:, j, :], kro[:, j * 128:(j + 1) * 128], identbf[:, :]),
                        after=[t_ropek, prev["ptr_free"]] if j == 0 else [])
                t_kT = s.emit(S, lambda e: e.copy(kTt[:, :, :], ptr[:, :, :]),
                              after=[t_trk] + prev["kT_rd"])
                prev["ptr_free"] = t_kT
                t_maskp = prev["t_mask"]
                t_intra_last = None
                t_inter_last = None
                t_state_last = None
                for h in range(H):
                    po = 64 * (h % 2)
                    tile = h // 2
                    slot = h % 2
                    pe_slot = (pm6, pm7)[slot]
                    t_en = s.emit(T, lambda e, po=po, tile=tile, pe_slot=pe_slot:
                                  e.matmul(pe_slot[:, 128:256],
                                           kTt[po:po + 64, tile, :],
                                           qT[po:po + 64, tile, :],
                                           start=True, stop=True),
                                  after=([t_qT, t_kT, t_th] if h == 0 else []) +
                                        ([t_maskp[slot]] if t_maskp[slot] else []))
                    t_maskp[slot] = s.emit(V, lambda e, slot=slot, pe_slot=pe_slot:
                                          e.tensor_tensor(
                                              eT[:, slot, 0:128],
                                              pe_slot[:, 128:256],
                                              trium[:, 0:128], ALU.mult),
                                          after=[t_en, t_tri])
                    t_intra_last = s.emit(T, lambda e, h=h, slot=slot: e.matmul(
                        pbig1[:, h * 64:(h + 1) * 64], eT[:, slot, 0:128],
                        vb[:, h * 64:(h + 1) * 64], start=True, stop=False),
                        after=[t_maskp[slot], t_vb] + ([t_th] if h == 0 else []))
                    t_inter_last = s.emit(T, lambda e, h=h, po=po, tile=tile:
                                          e.matmul(
                        pbig1[:, h * 64:(h + 1) * 64],
                        qT[po:po + 64, tile, :],
                        Sbf[po:po + 64, (h // 2) * 64:(h // 2) * 64 + 64],
                        start=False, stop=True),
                        after=[prev["Sbf_ready"]] if h == 0 else [])
                    t_state_last = s.emit(T, lambda e, h=h, po=po: e.matmul(
                        pS[po:po + 64, (h // 2) * 64:(h // 2) * 64 + 64],
                        kro[:, h * 64:(h + 1) * 64],
                        vb[:, h * 64:(h + 1) * 64], start=True, stop=True),
                        after=[t_ropek, t_vb, prev["pS_free"]] if h == 0 else [])
                prev["kro_rd"] = [t_trk, t_state_last]
                prev["vb_rd"] = [t_intra_last, t_state_last]
                prev["kT_rd"] = [t_inter_last]
                t_sfu = s.emit(V, lambda e: e.tensor_tensor(
                    Sf[:, :], Sf[:, :], pS[:, :], ALU.add),
                    after=[t_state_last, prev["Sf_ready"]])
                prev["pS_free"] = t_sfu
                prev["Sf_ready"] = t_sfu
                t_sbf = s.emit(V, lambda e: e.tensor_copy(Sbf[:, :], Sf[:, :]),
                               after=[t_sfu, t_inter_last])
                prev["Sbf_ready"] = t_sbf
                t_og = s.emit(V, lambda e, sl=sl: e.tensor_tensor(
                    og[:, sl, :], pbig1[:, :], th[:, :], ALU.mult),
                    after=[t_intra_last, t_inter_last, t_gate] +
                          ([prev["og_rd2"][sl]] if prev["og_rd2"][sl] else []))
                prev["th_rd"] = t_og
                prev["pbig1_free"] = t_og
                f["t_og"] = t_og
                f["t_inter_last"] = t_inter_last
                f["t_vb"] = t_vb
                f["t_th"] = t_th
                return f

            def emit_back(c, f):
                xc = f["xc"]
                sl = c % 2
                ogs = og[:, sl, :]
                ln_toks, rs_ap, m_ap = emit_ln_stats(
                    scol2, ogs, [f["t_og"]] +
                    ([prev["scol2_rd"]] if prev["scol2_rd"] else []))
                t_oln = s.emit(V, lambda e, rs_ap=rs_ap, m_ap=m_ap, ogs=ogs:
                               e.tensor_scalar(sb_o[:, :], ogs, m_ap, rs_ap,
                                               ALU.subtract, ALU.mult),
                               after=ln_toks +
                               ([prev["sb_o_rd"]] if prev["sb_o_rd"] else []) +
                               ([prev["t_store"]] if prev["t_store"] else []))
                prev["og_rd2"][sl] = t_oln
                t_tro = None
                for j in range(KT):
                    t_tro = s.emit(T, lambda e, j=j: e.transpose(
                        ptr[:, j, :], sb_o[:, j * 128:(j + 1) * 128], identbf[:, :]),
                        after=[t_oln, prev["ptr_free"]] if j == 0 else [])
                t_oT = s.emit(S, lambda e: e.copy(qT[:, :, :], ptr[:, :, :]),
                              after=[t_tro])
                prev["ptr_free"] = t_oT
                t_ao = None
                for nh in range(2):
                    for k in range(KT):
                        def fa(e, nh=nh, k=k):
                            return e.matmul(pbig0[:, nh * 512:(nh + 1) * 512],
                                            qT[:, k, :],
                                            Wo_bf[:, k, nh * 512:(nh + 1) * 512],
                                            start=(k == 0), stop=False)
                        t_ao = s.emit(T, fa,
                                      after=[t_oT, f["t_vb"]] + t_wog
                                      if (nh == 0 and k == 0) else [])
                    def fb(e, nh=nh):
                        return e.matmul(pbig0[:, nh * 512:(nh + 1) * 512],
                                        ones_row[:, :],
                                        bWo_row[0:1, nh * 512:(nh + 1) * 512],
                                        start=False, stop=True)
                    t_ao = s.emit(T, fb, after=[t_ones] if nh == 0 else [])
                prev["qT_rd"] = [t_ao]
                t_aoev = s.emit(V, lambda e: e.tensor_copy(
                    ao_sb[:, :], pbig0[:, :]), after=[t_ao])
                prev["pbig0_free"] = t_aoev
                ln_toks, rs_ap, m_ap = emit_ln_stats(scol2, ao_sb[:, :],
                                                     [t_aoev])
                t_aoln = s.emit(V, lambda e, rs_ap=rs_ap, m_ap=m_ap:
                                e.tensor_scalar(sb_o[:, :], ao_sb[:, :],
                                                m_ap, rs_ap,
                                                ALU.subtract, ALU.mult),
                                after=ln_toks + [t_oT])
                t_x1 = s.emit(V, lambda e, xc=xc: e.tensor_tensor(
                    x1b[:, :], sb_o[:, :], xc, ALU.add),
                    after=[t_aoln] +
                    ([prev["x1_store"]] if prev["x1_store"] else []))
                t_x1d = s.emit(SY, lambda e, c=c: e.dma_start(
                    x1s[c * C:(c + 1) * C, :], x1b[:, :]), after=[t_x1],
                    dma="dx1s")
                prev["x1_store"] = t_x1d
                if c + 2 < n_chunks:
                    tok_x[c + 2] = s.emit(SY, lambda e, c=c: e.dma_start(
                        xb[:, c % 2, :], x[(c + 2) * C:(c + 3) * C, :]),
                        after=[t_x1], dma="dx0" if (c + 2) % 2 == 0 else "dx1")
                ln_toks, rs_ap, m_ap = emit_ln_stats(scol2, x1b[:, :],
                                                     [t_x1])
                t_tb = s.emit(V, lambda e, rs_ap=rs_ap, m_ap=m_ap:
                              e.tensor_scalar(sb_o[:, :], x1b[:, :],
                                              m_ap, rs_ap,
                                              ALU.subtract, ALU.mult),
                              after=ln_toks + [t_x1])
                prev["scol2_rd"] = t_tb
                t_tbd = s.emit(SY, lambda e, c=c: e.dma_start(
                    ts_[c * C:(c + 1) * C, :], sb_o[:, :]), after=[t_tb],
                    dma="dts")
                prev["t_store"] = t_tbd
                t_trt = None
                for j in range(KT):
                    t_trt = s.emit(T, lambda e, j=j: e.transpose(
                        ptr[:, j, :], sb_o[:, j * 128:(j + 1) * 128], identbf[:, :]),
                        after=[t_tb, prev["ptr_free"]] if j == 0 else [])
                prev["sb_o_rd"] = t_trt
                t_tT = s.emit(S, lambda e, c=c: e.copy(
                    tT[:, :, c * C:(c + 1) * C], ptr[:, :, :]), after=[t_trt])
                prev["ptr_free"] = t_tT
                prev["outT_rd"] = t_ao
                prev["tT_slot"] = t_tT

            fstate = {}
            for i in range(n_chunks + 1):
                if i < n_chunks:
                    fstate[i] = emit_front(i)
                if i >= 1:
                    emit_back(i - 1, fstate.pop(i - 1))

            a_final = []
            if prev["tT_slot"] is not None:
                a_final.append(prev["tT_slot"])
            for nm in ("dx1s", "dts"):
                if s.cnt.get(nm, 0) > 0:
                    a_final.append((nm, s.cnt[nm]))
            with nc.Block() as blockA:
                s.flush(blockA, final_tokens=a_final)

        if "B" not in phases:
            return nc
        with ExitStack() as bctx:
            be = bctx.enter_context
            rw1_bf = be(nc.sbuf_tensor("rw1_bf", [128, KT, E], BF16))
            rw2_bf = be(nc.sbuf_tensor("rw2_bf", [128, KT, NEXP], BF16))
            rb1_sb = be(nc.sbuf_tensor("rb1_sb", [128, KT], F32))
            rb2_sb = be(nc.sbuf_tensor("rb2_sb", [128, NEXP], F32))
            r1b = be(nc.sbuf_tensor("r1b", [128, KT, 512], BF16))
            l_sb = be(nc.sbuf_tensor("l_sb", [128, NCH, NEXP], F32))
            tpw = be(nc.sbuf_tensor("tpw", [128, 6 * NEXP], F32))
            wbf = be(nc.sbuf_tensor("wbf", [128, NEXP], BF16))
            pr0 = be(nc.psum_tensor("pr0", [128, 512], F32))
            pr1 = be(nc.psum_tensor("pr1", [128, 512], F32))
            pl = be(nc.psum_tensor("pl", [128, NEXP], F32))
            pwt = be(nc.psum_tensor("pwt", [NEXP, 128], BF16))

            s2 = Sched(nc, bctx, "b")
            s2.self_chain = self_chain
            T, V, S, GP, SY = "tensor", "vector", "scalar", "gpsimd", "sync"
            t_rw1 = s2.emit(GP, lambda e: e.dma_start(
                rw1_bf[:, :, :], rw1[:, :].rearrange("(k p) n -> p k n", p=128)),
                dma="dw")
            t_rw2 = s2.emit(GP, lambda e: e.dma_start(
                rw2_bf[:, :, :], rw2[:, :].rearrange("(k p) n -> p k n", p=128)),
                dma="dw")
            t_rw1 = t_rw2 = ("dw", s2.cnt["dw"])
            t_rb1 = s2.emit(SY, lambda e: e.dma_start(rb1_sb[:, :], rb1_fm[:, :]),
                            dma="dmisc")
            t_rb2 = s2.emit(SY, lambda e: e.dma_start(rb2_sb[:, :], rb2_b[:, :]),
                            dma="dmisc")
            t_rb1 = t_rb2 = ("dmisc", s2.cnt["dmisc"])
            prfree = [None, None]
            t_lev = [None] * NCH
            for t8 in range(NTOK // 512):
                t_r1 = [None] * KT
                for m in range(KT):
                    pr = (pr0, pr1)[m % 2]
                    t_mm = None
                    for k in range(KT):
                        def f(e, m=m, k=k, t8=t8, pr=pr):
                            return e.matmul(pr[:, :], rw1_bf[:, k, m * 128:(m + 1) * 128],
                                            tT[:, k, t8 * 512:(t8 + 1) * 512],
                                            start=(k == 0), stop=(k == KT - 1))
                        aft = []
                        if k == 0:
                            aft = [t_rw1]
                            if prfree[m % 2] is not None:
                                aft.append(prfree[m % 2])
                        t_mm = s2.emit(T, f, after=aft)
                    t_r1[m] = s2.emit(S, lambda e, m=m, pr=pr: e.activation(
                        r1b[:, m, :], pr[:, :], AF.Silu,
                        bias=rb1_sb[:, m:m + 1]), after=[t_mm, t_rb1])
                    prfree[m % 2] = t_r1[m]
                for sub in range(4):
                    ci = t8 * 4 + sub
                    t_lmm = None
                    for k in range(KT):
                        def f(e, k=k, sub=sub):
                            return e.matmul(pl[:, :], r1b[:, k, sub * 128:(sub + 1) * 128],
                                            rw2_bf[:, k, :],
                                            start=(k == 0), stop=(k == KT - 1))
                        t_lmm = s2.emit(T, f,
                                        after=[t_r1[KT - 1], t_rw2] + ([t_lev[ci - 1]] if ci > 0 else [])
                                        if k == 0 else [])
                    t_lev[ci] = s2.emit(V, lambda e, ci=ci: e.tensor_tensor(
                        l_sb[:, ci, :], pl[:, :], rb2_sb[:, :], ALU.add),
                        after=[t_lmm, t_rb2])
            cm1, cnm1, cm2, cZ, crz = 0, 1, 2, 3, 4
            OFF = NEXP
            t_wT_ev = None
            t_wsb = [None] * NCH
            for ci in range(NCH):
                lap = l_sb[:, ci, :]
                t_m1 = s2.emit(V, lambda e, lap=lap: e.reduce_max(
                    tpw[:, cm1:cm1 + 1], lap, axis=AX.X), after=[t_lev[ci]])
                t_nm1 = s2.emit(V, lambda e: e.tensor_scalar(
                    tpw[:, cnm1:cnm1 + 1], tpw[:, cm1:cm1 + 1], -1.0, None,
                    ALU.mult), after=[t_m1])
                t_msk1 = s2.emit(V, lambda e, lap=lap: e.tensor_scalar(
                    tpw[:, OFF:OFF + NEXP], lap, tpw[:, cm1:cm1 + 1], None,
                    ALU.is_ge), after=[t_m1])
                t_masked = s2.emit(V, lambda e, lap=lap: e.scalar_tensor_tensor(
                    tpw[:, 2 * OFF:2 * OFF + NEXP], tpw[:, OFF:OFF + NEXP], -BIG,
                    lap, ALU.mult, ALU.add), after=[t_msk1])
                t_m2 = s2.emit(V, lambda e: e.reduce_max(
                    tpw[:, cm2:cm2 + 1], tpw[:, 2 * OFF:2 * OFF + NEXP],
                    axis=AX.X), after=[t_masked])
                t_sel = s2.emit(V, lambda e, lap=lap: e.tensor_scalar(
                    tpw[:, 3 * OFF:3 * OFF + NEXP], lap, tpw[:, cm2:cm2 + 1],
                    None, ALU.is_ge), after=[t_m2])
                t_e = s2.emit(S, lambda e, lap=lap: e.activation(
                    tpw[:, 4 * OFF:4 * OFF + NEXP], lap, AF.Exp,
                    bias=tpw[:, cnm1:cnm1 + 1]), after=[t_nm1, t_lev[ci]])
                t_es = s2.emit(V, lambda e: e.tensor_tensor(
                    tpw[:, 5 * OFF:5 * OFF + NEXP], tpw[:, 4 * OFF:4 * OFF + NEXP],
                    tpw[:, 3 * OFF:3 * OFF + NEXP], ALU.mult),
                    after=[t_e, t_sel])
                t_Z = s2.emit(V, lambda e: e.reduce_sum(
                    tpw[:, cZ:cZ + 1], tpw[:, 5 * OFF:5 * OFF + NEXP],
                    axis=AX.X), after=[t_es])
                t_rz = s2.emit(V, lambda e: e.reciprocal(
                    tpw[:, crz:crz + 1], tpw[:, cZ:cZ + 1]), after=[t_Z])
                t_wsb[ci] = s2.emit(V, lambda e, ci=ci: e.tensor_scalar(
                    wsb[:, ci, :], tpw[:, 5 * OFF:5 * OFF + NEXP],
                    tpw[:, crz:crz + 1], None, ALU.mult), after=[t_rz])
                t_wbf = s2.emit(V, lambda e, ci=ci: e.tensor_copy(
                    wbf[:, :], wsb[:, ci, :]), after=[t_wsb[ci], t_wT_ev]
                    if t_wT_ev else [t_wsb[ci]])
                t_wT = s2.emit(T, lambda e: e.transpose(
                    pwt[:, :], wbf[:, :], identbf[:, :]),
                    after=[t_wbf] + ([t_wT_ev] if t_wT_ev else []))
                t_wT_ev = s2.emit(S, lambda e, ci=ci: e.copy(
                    wTsb[:, ci * C:(ci + 1) * C], pwt[:, :]), after=[t_wT])
            if debug:
                t_wdump = s2.emit(SY, lambda e: e.dma_start(
                    dbg_w[:, :, :].rearrange("c p e -> p c e"), wsb[:, :, :]),
                    after=t_wsb, dma="ddbg")
            b_final = [t_wT_ev] + t_wsb + (
                [("ddbg", s2.cnt.get("ddbg", 0))] if debug else [])
            with nc.Block() as blockB:
                s2.flush(blockB, final_tokens=b_final)

        if "C" not in phases:
            return nc
        with ExitStack() as cctx:
            ce = cctx.enter_context
            ew1_bf = ce(nc.sbuf_tensor("ew1_bf", [128, KT, DH], BF16))
            ew2_bf = ce(nc.sbuf_tensor("ew2_bf", [128, KT2, E], BF16))
            ehb = ce(nc.sbuf_tensor("ehb", [128, KT2, 512], BF16))
            yac = ce(nc.sbuf_tensor("yac", [128, CPG, E], BF16))
            eb1_sb = ce(nc.sbuf_tensor("eb1_sb", [128, NEXP, KT2], F32))
            eb2_bf = ce(nc.sbuf_tensor("eb2_bf", [NEXP, E], BF16))
            x1c = ce(nc.sbuf_tensor("x1c", [128, 2, E], F32))
            tc = ce(nc.sbuf_tensor("tc", [128, E], BF16))
            yf = ce(nc.sbuf_tensor("yf", [128, E], F32))
            ylnb = ce(nc.sbuf_tensor("ylnb", [128, E], BF16))
            sc3 = ce(nc.sbuf_tensor("sc3", [128, 16], F32))
            peh0 = ce(nc.psum_tensor("peh0", [128, 512], F32))
            peh1 = ce(nc.psum_tensor("peh1", [128, 512], F32))
            peo0 = ce(nc.psum_tensor("peo0", [128, 1024], F32))
            peo1 = ce(nc.psum_tensor("peo1", [128, 1024], F32))

            s3 = Sched(nc, cctx, "c")
            s3.self_chain = self_chain
            T, V, S, GP, SY = "tensor", "vector", "scalar", "gpsimd", "sync"

            t_eb1 = s3.emit(SY, lambda e: e.dma_start(eb1_sb[:, :, :],
                                                      eb1_fm[:, :, :]), dma="dmisc")
            t_eb2 = s3.emit(GP, lambda e: e.dma_start(eb2_bf[:, :], eb2[:, :]),
                            dma="dmiscg")
            t_eb1 = ("dmisc", s3.cnt["dmisc"])
            t_eb2 = ("dmiscg", s3.cnt["dmiscg"])

            def load_ew1(eidx, after):
                return s3.emit(GP, lambda e, eidx=eidx: e.dma_start(
                    ew1_bf[:, :, :],
                    ew1[eidx, :, :].rearrange("(k p) h -> p k h", p=128)),
                    after=after, dma="dw1")

            def load_ew2(eidx, after):
                return s3.emit(GP, lambda e, eidx=eidx: e.dma_start(
                    ew2_bf[:, :, :],
                    ew2[eidx, :, :].rearrange("(k p) n -> p k n", p=128)),
                    after=after, dma="dw2")

            t_ew1 = load_ew1(0, [])
            t_dbg_eh = None
            t_ew2 = None
            ew1_lastuse = None
            ew2_lastuse = None
            t_out_final = []

            def emit_ln3(src_ap, after_tok, off=0, guards=()):
                c_sx, c_sq, c_m, c_a, c_msq, c_b, c_s, c_rs = (
                    off + i for i in range(8))
                guards = list(guards)
                t_sx = s3.emit(V, lambda e: e.reduce_sum(
                    sc3[:, c_sx:c_sx + 1], src_ap, axis=AX.X), after=after_tok)
                t_sq = s3.emit(S, lambda e: e.activation(
                    ylnb[:, :], src_ap, AF.Square,
                    accum_out=sc3[:, c_sq:c_sq + 1]), after=after_tok + guards)
                t_m = s3.emit(V, lambda e: e.tensor_scalar(
                    sc3[:, c_m:c_m + 1], sc3[:, c_sx:c_sx + 1], 1.0 / E, None,
                    ALU.mult), after=[t_sx])
                t_a = s3.emit(V, lambda e: e.tensor_scalar(
                    sc3[:, c_a:c_a + 1], sc3[:, c_sq:c_sq + 1], 1.0 / E, EPS,
                    ALU.mult, ALU.add), after=[t_sq])
                t_msq = s3.emit(V, lambda e: e.tensor_tensor(
                    sc3[:, c_msq:c_msq + 1], sc3[:, c_m:c_m + 1],
                    sc3[:, c_m:c_m + 1], ALU.mult), after=[t_m])
                t_b = s3.emit(V, lambda e: e.tensor_tensor(
                    sc3[:, c_b:c_b + 1], sc3[:, c_a:c_a + 1],
                    sc3[:, c_msq:c_msq + 1], ALU.subtract), after=[t_a, t_msq])
                t_s = s3.emit(S, lambda e: e.activation(
                    sc3[:, c_s:c_s + 1], sc3[:, c_b:c_b + 1], AF.Sqrt),
                    after=[t_b] + guards)
                t_rs = s3.emit(V, lambda e: e.reciprocal(
                    sc3[:, c_rs:c_rs + 1], sc3[:, c_s:c_s + 1]), after=[t_s])
                return ([t_rs], sc3[:, c_rs:c_rs + 1], sc3[:, c_m:c_m + 1],
                        t_sq, t_a)

            slot_guard = [None] * CPG
            peo_guard = [None, None]
            pcdma = {}
            pc = {"x1_rd": [None, None], "x1_od": [None, None],
                  "tc_rd": None, "yf_rd": None, "ob_last": None,
                  "a_g": [None, None]}

            def emit_pc_dma(cg):
                par = cg % 2
                aftx = [t for t in (pc["x1_rd"][par], pc["x1_od"][par]) if t]
                t_x1c = s3.emit(SY, lambda e, cg=cg, par=par: e.dma_start(
                    x1c[:, par, :], x1s[cg * C:(cg + 1) * C, :]),
                    after=aftx, dma=f"dpcx{par}")
                aftt = [pc["tc_rd"]] if pc["tc_rd"] else []
                t_tc = s3.emit(SY, lambda e, cg=cg: e.dma_start(
                    tc[:, :], ts_[cg * C:(cg + 1) * C, :]),
                    after=aftt, dma="dpct")
                pcdma[cg] = (t_x1c, t_tc)

            def emit_pc(g, ci, t_yacc_tok):
                cg = g * CPG + ci
                par = cg % 2
                t_x1c, t_tc = pcdma[cg]
                t_y = s3.emit(V, lambda e, ci=ci: e.tensor_tensor(
                    yf[:, :], yac[:, ci, :], tc[:, :], ALU.add),
                    after=[t_tc, t_yacc_tok] +
                    ([pc["yf_rd"]] if pc["yf_rd"] else []))
                slot_guard[ci] = t_y
                guards = [t for t in (pc["a_g"][par], pc["ob_last"]) if t]
                ln_toks, rs_ap, m_ap, t_sq, t_a = emit_ln3(
                    yf[:, :], [t_y], off=8 * par, guards=guards)
                t_yln = s3.emit(V, lambda e, rs_ap=rs_ap, m_ap=m_ap:
                                e.tensor_scalar(ylnb[:, :],
                                                yf[:, :], m_ap, rs_ap,
                                                ALU.subtract, ALU.mult),
                                after=ln_toks)
                t_ob = s3.emit(V, lambda e, par=par: e.tensor_tensor(
                    x1c[:, par, :], ylnb[:, :], x1c[:, par, :], ALU.add),
                    after=[t_yln, t_x1c])
                t_od = s3.emit(SY, lambda e, cg=cg, par=par: e.dma_start(
                    out[cg * C:(cg + 1) * C, :], x1c[:, par, :]),
                    after=[t_ob], dma=f"dout{par}")
                pc["x1_rd"][par] = t_ob
                pc["x1_od"][par] = t_od
                pc["tc_rd"] = t_y
                pc["yf_rd"] = t_sq
                pc["ob_last"] = t_ob
                pc["a_g"][par] = t_a

            for g in range(NGRP):
                tok0 = g * TPG
                ch0 = g * CPG
                t_yinit = [None] * CPG
                pyfree = None
                for ci in range(CPG):
                    t_mm = None
                    for nh in range(2):
                        def f(e, ci=ci, nh=nh, ch0=ch0):
                            return e.matmul(
                                peo0[:, nh * 512:(nh + 1) * 512],
                                wTsb[:, (ch0 + ci) * C:(ch0 + ci + 1) * C],
                                eb2_bf[:, nh * 512:(nh + 1) * 512],
                                start=True, stop=True)
                        aft = []
                        if nh == 0:
                            aft = [t_eb2]
                            if pyfree is not None:
                                aft.append(pyfree)
                            elif peo_guard[0] is not None:
                                aft.append(peo_guard[0])
                        t_mm = s3.emit(T, f, after=aft)
                    aft_ev = [t_mm]
                    if slot_guard[ci] is not None:
                        aft_ev.append(slot_guard[ci])
                    t_yinit[ci] = s3.emit(S, lambda e, ci=ci: e.copy(
                        yac[:, ci, :], peo0[:, :]), after=aft_ev)
                    pyfree = t_yinit[ci]
                peo_free = [pyfree, pyfree]
                peh_free = [None, None]
                t_yacc_last = t_yinit

                elist = list(range(NEXP)) if g == 0 else list(range(NEXP - 1, -1, -1))
                for ei, eidx in enumerate(elist):
                    first = (g == 0 and ei == 0)
                    is_last = (ei == NEXP - 1)
                    if not (g == 1 and ei == 0):
                        t_ew2 = load_ew2(eidx, [ew2_lastuse] if ew2_lastuse else [])
                    for t4 in range(TPG // 512):
                        tstart = tok0 + t4 * 512
                        t_ehev = [None] * KT2
                        for m in range(KT2):
                            ph = (peh0, peh1)[m % 2]
                            t_mm = None
                            for k in range(KT):
                                def f(e, m=m, k=k, ph=ph, tstart=tstart):
                                    return e.matmul(
                                        ph[:, :], ew1_bf[:, k, m * 128:(m + 1) * 128],
                                        tT[:, k, tstart:tstart + 512],
                                        start=(k == 0), stop=(k == KT - 1))
                                aft = []
                                if k == 0:
                                    aft = [t_ew1]
                                    if peh_free[m % 2] is not None:
                                        aft.append(peh_free[m % 2])
                                t_mm = s3.emit(T, f, after=aft)
                            t_ehev[m] = s3.emit(S, lambda e, m=m, ph=ph, eidx=eidx:
                                                e.activation(
                                ehb[:, m, :], ph[:, :], AF.Silu,
                                bias=eb1_sb[:, eidx, m:m + 1]),
                                after=[t_mm, t_eb1] +
                                ([t_dbg_eh] if t_dbg_eh else []))
                            peh_free[m % 2] = t_ehev[m]
                        ew1_lastuse = t_mm
                        if debug and g == 0 and ei == 0 and t4 == 0:
                            t_dbg_eh = s3.emit(SY, lambda e: e.dma_start(
                                dbg_eh[:, :, :], ehb[:, :, :]),
                                after=[t_ehev[KT2 - 1]], dma="ddbg")
                        if t4 == TPG // 512 - 1:
                            nxt = None
                            if ei + 1 < NEXP:
                                nxt = elist[ei + 1]
                            elif g == 0:
                                nxt = None
                            if nxt is not None:
                                t_ew1 = load_ew1(nxt, [ew1_lastuse])
                        for sub in range(4):
                            ci = t4 * 4 + sub
                            po = (peo0, peo1)[sub % 2]
                            t_mm = None
                            for nh in range(2):
                                for k in range(KT2):
                                    def f(e, sub=sub, nh=nh, k=k, po=po):
                                        return e.matmul(
                                            po[:, nh * 512:(nh + 1) * 512],
                                            ehb[:, k, sub * 128:(sub + 1) * 128],
                                            ew2_bf[:, k, nh * 512:(nh + 1) * 512],
                                            start=(k == 0), stop=(k == KT2 - 1))
                                    aft = []
                                    if nh == 0 and k == 0:
                                        aft = [t_ehev[KT2 - 1], t_ew2]
                                        if peo_free[sub % 2] is not None:
                                            aft.append(peo_free[sub % 2])
                                    t_mm = s3.emit(T, f, after=aft)
                            ew2_lastuse = t_mm
                            def facc(e, ci=ci, po=po, eidx=eidx, ch0=ch0):
                                return e.scalar_tensor_tensor(
                                    yac[:, ci, :], po[:, :],
                                    wsb[:, ch0 + ci, eidx:eidx + 1],
                                    yac[:, ci, :], ALU.mult, ALU.add)
                            t_yacc_last[ci] = s3.emit(V, facc,
                                                      after=[t_mm, t_yacc_last[ci]])
                            peo_free[sub % 2] = t_yacc_last[ci]
                            if is_last:
                                if ci >= 1:
                                    emit_pc(g, ci - 1, t_yacc_last[ci - 1])
                                emit_pc_dma(g * CPG + ci)
                    if debug and g == 0 and ei == 0 and t4 == TPG // 512 - 1:
                        t_dbgyac = s3.emit(SY, lambda e: e.dma_start(
                            dbg_yac[:, :, :], yac[:, :, :]),
                            after=[t_yacc_last[i] for i in range(CPG)],
                            dma="ddbg")
                        for i in range(CPG):
                            t_yacc_last[i] = t_dbgyac
                emit_pc(g, CPG - 1, t_yacc_last[CPG - 1])
                peo_guard = list(peo_free)

            with nc.Block() as blockC:
                s3.flush(blockC, final_tokens=[
                    ("dout0", s3.cnt["dout0"]), ("dout1", s3.cnt["dout1"])] + (
                    [("ddbg", s3.cnt.get("ddbg", 0))] if debug else []))

    return nc


_NC_CACHE = None


def _get_nc():
    global _NC_CACHE
    if _NC_CACHE is None:
        _NC_CACHE = build()
    return _NC_CACHE


def kernel(**inputs):
    cos_rep, sin_rep, triu, ident = _host_tables()
    f32 = lambda a: np.ascontiguousarray(np.asarray(a, dtype=np.float32))
    g = {k: f32(v) for k, v in inputs.items()}
    base = {
        "Wq": g["L0_attn_Wq"], "Wk": g["L0_attn_Wk"], "Wv": g["L0_attn_Wv"],
        "Wo": g["L0_attn_Wo"], "Wg1": g["L0_attn_Wg1"], "Wg2": g["L0_attn_Wg2"],
        "gamma_fm": f32(g["L0_attn_gamma"].reshape(KT, 128).T),
        "beta_fm": f32(g["L0_attn_beta"].reshape(KT, 128).T),
        "rw1": g["L0_ffn_rw1"],
        "rb1_fm": f32(g["L0_ffn_rb1"].reshape(KT, 128).T),
        "rw2": g["L0_ffn_rw2"],
        "rb2_b": f32(np.broadcast_to(g["L0_ffn_rb2"], (128, NEXP))),
        "ew1": g["L0_ffn_ew1"],
        "eb1_fm": f32(g["L0_ffn_eb1"].reshape(NEXP, KT2, 128).transpose(2, 0, 1)),
        "ew2": g["L0_ffn_ew2"],
        "eb2": g["L0_ffn_eb2"],
        "cosr": cos_rep, "sinr": sin_rep, "triu": triu, "ident": ident,
    }
    xfull = g["L0_attn_x"]
    in_maps = []
    for b in range(NB):
        m = dict(base)
        m["x"] = np.ascontiguousarray(xfull[b])
        in_maps.append(m)
    nc = _get_nc()
    res = run_bass_kernel_spmd(nc, in_maps, core_ids=list(range(NB)))
    outp = np.stack([np.asarray(res.results[b]["out"], dtype=np.float32)
                     for b in range(NB)], axis=0)
    return outp
